# version 03
# baseline (speedup 1.0000x reference)
import numpy as np
import ml_dtypes
import concourse.bass as bass
import concourse.mybir as mybir
from bass_rust import ReduceOp
from concourse.bass_utils import run_bass_kernel_spmd

F32 = mybir.dt.float32
BF16 = mybir.dt.bfloat16
AF = mybir.ActivationFunctionType
ALU = mybir.AluOpType
AX = mybir.AxisListType

T, D, H, DK, DV, C = 4096, 1024, 4, 256, 256, 32
NCH, NDT, NBLK, NTT = 128, 8, 32, 8
MAGIC = float(np.float32(1.5 * 2.0 ** 23))
FORGET_MIN, MAX_FLOOR = 0.55, 0.075
NI = 24
KB = 1024


class Cnt:
    def __init__(self, es, nc, name):
        self.sem = es.enter_context(nc.semaphore(name))
        self.n = 0

    def inc(self, tok, k=1):
        tok.then_inc(self.sem, k)
        self.n += k
        return self.n


class Rec:
    def __init__(self):
        self.calls = []

    def __getattr__(self, name):
        if name.startswith("_"):
            raise AttributeError(name)
        calls = self.calls

        def call(*a, **k):
            rec = {"m": name, "a": a, "k": k, "incs": []}
            calls.append(rec)

            class Tok:
                def then_inc(self, sem, n=1):
                    rec["incs"].append((sem, n))
                    return self
            return Tok()
        return call

    def replay(self, eng):
        for idx, rec in enumerate(self.calls):
            try:
                instr = getattr(eng, rec["m"])(*rec["a"], **rec["k"])
            except Exception:
                import sys as _s
                print("REPLAY FAIL", rec["m"], "idx", idx,
                      [getattr(x, "tensor", None) and x.tensor.name or str(x)[:40]
                       for x in rec["a"] if hasattr(x, "tensor") or True][:4],
                      file=_s.stderr)
                raise
            for sem, n in rec["incs"]:
                instr.then_inc(sem, n)


class B:
    pass


def mkbuilder(dump):
    import contextlib
    b = B()
    b.dump = dump
    b.nc = nc = bass.Bass("TRN2", target_bir_lowering=False)
    b.es = es = contextlib.ExitStack()
    b.t, b.v, b.a, b.g, b.s = Rec(), Rec(), Rec(), Rec(), Rec()

    b.dx = nc.dram_tensor("x", [T, D], F32, kind="ExternalInput").ap()
    b.dy = nc.dram_tensor("y", [T, D], F32, kind="ExternalOutput").ap()
    b.W = {}
    for nm, shape in [
        ("wq", [D, D]), ("wk", [D, D]), ("wv", [D, D]), ("wb", [D, H]),
        ("forget", [1, H]), ("conv_q", [D, 4]), ("conv_k", [D, 4]),
        ("conv_v", [D, 4]), ("fir_s", [D, 3]), ("fir_l", [D, 63]),
        ("log_temp", [1, H]), ("floor", [H, 4]),
        ("w1", [1088, 2048]), ("b1", [1, 2048]), ("w2", [2048, 16]),
        ("b2", [1, 16]), ("norm_w", [1, DV]), ("wo", [D, D]),
        ("wg_f", [3072, D]), ("wu_f", [3072, D]), ("wd_f", [D, 3072]),
    ]:
        b.W[nm] = nc.dram_tensor(nm, shape, F32, kind="ExternalInput").ap()

    def dram2(name, rows, cols, dt):
        return nc.dram_tensor(name, [rows, cols], dt).ap()
    b.h_xn = dram2("h_xn", NDT * 128, T, BF16)
    b.h_v = dram2("h_v", NDT * 128, T, BF16)
    b.h_df = dram2("h_df", NDT * 128, T, BF16)
    b.h_sh = dram2("h_sh", NDT * 128, T, BF16)
    b.h_lo = dram2("h_lo", NDT * 128, T, BF16)
    b.h_s = dram2("h_s", NDT * 128, T, F32)
    b.h_up = dram2("h_up", NI * 128, T, F32)
    b.h_ga = dram2("h_ga", NI * 128, T, F32)
    b.h_t = dram2("h_t", NI * 128, T, F32)
    b.h_ot = dram2("h_ot", NI * 128, T, F32)
    b.ar_in = [nc.dram_tensor(f"ar_in{i}", [1, 2], F32) for i in range(4)]
    b.ar_out = [nc.dram_tensor(f"ar_out{i}", [1, 2], F32) for i in range(4)]
    b.d_dbg = (nc.dram_tensor("dbg", [128, T], F32, kind="ExternalOutput").ap()
               if dump is not None else None)

    for nm in ["pe", "ve", "ac", "gp", "qs", "qa", "qg", "ccs"]:
        setattr(b, nm, Cnt(es, nc, nm))

    b.sb = {}
    BLOB = 176 * KB
    blob0, _ = nc.bump_sbuf(BLOB)
    b.AQ, b.AK, b.R2 = blob0, blob0 + 64 * KB, blob0 + 128 * KB

    def At(name, shape, dt, off):
        b.sb[name] = nc.alloc_sbuf_tensor_at("sb_" + name, shape, dt, offset=off)
        return b.sb[name]

    def S(name, shape, dt=BF16):
        b.sb[name] = nc.alloc_sbuf_tensor("sb_" + name, shape, dt)
        return b.sb[name]
    b.At, b.S = At, S

    S("ones", [128, 128], BF16); S("onesf", [128, 128], F32)
    S("ident", [128, 128], BF16); S("identf", [128, 128], F32)
    S("sel4", [128, 32], BF16)
    S("selh", [4, 4 * 128], BF16)
    S("selhf", [4, 4 * 128], F32)
    S("negselh", [4, 4 * 128], BF16)
    S("selh16", [16, 16 * 128], BF16)
    S("hsel", [128, 32], BF16)
    S("sumsel", [16, 16], BF16)
    S("sumself", [16, 16], F32)
    S("mask_bdsl", [128, 128], BF16)
    S("mask_triu", [128, 128], BF16)
    S("convw", [128, 128], F32)
    S("firs", [128, 32], F32)
    S("firl", [128, 512], F32)
    S("normw", [128, 2], F32)
    S("b1t", [128, 16], F32)
    S("misc", [4, 64], F32)
    S("lam", [128, 8], F32)
    S("sc", [1, 64], F32)
    S("scb", [128, 24], F32)
    S("lns", [128, 16], F32)
    S("amax", [128, 48], F32)
    S("wb_w", [128, 32], BF16)
    S("w2w", [128, 256], BF16)
    S("rows1", [128, T], BF16)
    S("gstats", [64, T], BF16)
    S("p16", [16, 16], F32)

    At("q", [128, NDT * T], BF16, b.AQ)
    At("k", [128, NDT * T], BF16, b.AK)
    R2 = b.R2
    At("x_t", [128, D], F32, R2)
    At("xn_t", [128, D], BF16, R2 + 4 * KB)
    At("sq_t", [128, D], BF16, R2 + 6 * KB)
    At("xfm", [128, NDT * 512], BF16, R2 + 8 * KB)
    At("xfm2", [128, NDT * 512], BF16, R2 + 16 * KB)
    At("xs", [128, NDT * 512], BF16, R2)
    At("pre", [128, NDT * 520], BF16, R2 + 8 * KB)
    At("wbuf", [128, NDT * D], BF16, R2 + 17 * KB)
    At("sqrow", [128, T], BF16, R2 + 33 * KB)
    At("accA", [128, 512], BF16, R2 + 41 * KB)
    At("accB", [128, 512], BF16, R2 + 42 * KB)
    At("ev1", [128, 512], F32, R2 + 43 * KB)
    At("ev2", [128, 512], F32, R2 + 45 * KB)
    At("S_f", [128, 2 * 1024], F32, R2)
    At("S_b", [128, 2 * 1024], BF16, R2 + 8 * KB)
    At("ktm", [128, 1024], BF16, R2 + 12 * KB)
    At("vtm", [128, 1024], BF16, R2 + 14 * KB)
    At("kbn", [128, 1024], BF16, R2 + 16 * KB)
    At("vblk", [128, 1024], BF16, R2 + 18 * KB)
    At("o_tm", [128, 1024], BF16, R2 + 20 * KB)
    At("z_sb", [128, 1024], BF16, R2 + 22 * KB)
    At("u_sb", [128, 1024], BF16, R2 + 24 * KB)
    At("ttf", [128, 128], BF16, R2 + 26 * KB)
    At("attf", [128, 128], BF16, R2 + 27 * KB)
    At("btm", [128, 8], F32, R2 + 28 * KB)
    At("dfm", [128, NDT * 512], BF16, R2 + 30 * KB)
    At("tmA", [128, 128], BF16, R2 + 44 * KB)
    At("tmB", [128, 128], BF16, R2 + 44 * KB + 256)
    At("ttA", [128, 128], BF16, R2 + 44 * KB + 512)
    At("ttB", [128, 128], BF16, R2 + 44 * KB + 768)
    At("vrow", [128, T], BF16, R2)
    At("srow", [128, T], BF16, R2 + 8 * KB)
    At("lrow", [128, T], BF16, R2 + 16 * KB)
    At("scr", [128, 128], F32, R2 + 24 * KB)
    At("sumP", [128, 512], F32, b.AQ)
    At("sqP", [128, 512], F32, b.AQ + 2 * KB)
    At("maxP", [128, 512], F32, b.AQ + 4 * KB)
    At("statm", [128, 1024], F32, b.AQ + 6 * KB)
    At("sqcb", [128, 1024], BF16, b.AQ + 54 * KB)
    At("gxs", [128, NDT * 512], BF16, R2)
    At("hmid", [128, 16 * 512], BF16, R2 + 8 * KB)
    At("w1b", [128, 9 * 2048], BF16, b.AK)
    At("p_fm", [16, T], BF16, R2 + 24 * KB)
    At("e16", [16, 512], F32, R2 + 32 * KB)
    At("zb16", [16, 512], F32, R2 + 34 * KB)
    At("vtile", [128, NDT * 512], BF16, R2 + 32 * KB)
    At("dtile", [128, NDT * 512], BF16, R2 + 40 * KB)
    At("stile", [128, NDT * 512], BF16, b.AQ)
    At("ltile", [128, NDT * 512], BF16, b.AQ + 8 * KB)
    At("onrm", [128, NDT * 512], BF16, b.AQ + 16 * KB)
    At("wobuf", [128, NDT * D], BF16, b.AQ + 24 * KB)
    At("ocomb", [128, 512], F32, b.AQ + 40 * KB)
    At("ocm2", [128, 512], F32, b.AQ + 42 * KB)
    At("ocm3", [128, 512], F32, b.AQ + 44 * KB)
    At("ev5", [128, 512], F32, b.AQ + 46 * KB)
    At("ev6", [128, 512], F32, b.AQ + 48 * KB)
    At("obuf", [128, 1024], F32, b.AQ + 50 * KB)
    At("ev7", [128, 512], F32, R2 + 8 * KB)
    At("ev8", [128, 512], F32, R2 + 10 * KB)
    At("wut", [128, NDT * 3072], BF16, b.AQ)
    At("wgt", [128, NDT * 3072], BF16, b.AK)
    At("wdt", [128, NI * 1024], BF16, b.AK)
    At("xq_t", [128, NDT * 512], BF16, b.AQ + 48 * KB)
    At("dqs", [128, 1024], F32, b.AQ + 56 * KB)
    At("dqm", [128, 64], F32, b.AQ + 60 * KB)
    At("dqb", [128, 1024], BF16, b.AQ + 62 * KB)
    At("ffa", [128, T], F32, R2)
    At("ffb", [128, T], F32, R2 + 16 * KB)
    At("ffc", [128, T], F32, R2 + 32 * KB)
    At("o2t", [128, 512], BF16, b.AQ)
    At("sgt", [128, 512], F32, b.AQ + 1 * KB)
    At("gt", [128, 512], F32, b.AQ + 3 * KB)
    At("xpt", [128, NDT * 512], F32, b.AQ + 8 * KB)
    At("outf", [128, NDT * 512], F32, b.AQ + 24 * KB)

    b.pp = [nc.alloc_psum_tensor(f"pp{i}", [128, 512], F32) for i in range(6)]
    b.pb = [nc.alloc_psum_tensor(f"pb{i}", [128, 1024], BF16) for i in range(2)]

    def w(rec, *deps):
        for d in deps:
            if d.n > 0:
                rec.wait_ge(d.sem, d.n)
    b.w = w

    def selfw(rec, cnt):
        rec.wait_ge(cnt.sem, cnt.n)
    b.selfw = selfw

    def barrier():
        for rec in (b.t, b.v, b.a, b.g, b.s):
            w(rec, b.pe, b.ve, b.ac, b.gp, b.qs, b.qa, b.qg, b.ccs)
    b.barrier = barrier

    def hview(hbm):
        return hbm.rearrange("(dt p) t -> p dt t", p=128)
    b.hview = hview
    return b


def p0(b):
    t, v, a, g, s = b.t, b.v, b.a, b.g, b.s
    sb, pp, W, w = b.sb, b.pp, b.W, b.w
    pe, ve, ac, gp, qs, qg = b.pe, b.ve, b.ac, b.gp, b.qs, b.qg
    g.memset(sb["ones"][:, :], 1.0)
    g.memset(sb["onesf"][:, :], 1.0)
    g.memset(sb["amax"][:, :], 0.0)
    g.memset(sb["pre"][:, :], 0.0)
    gp.inc(g.memset(sb["lns"][:, :], 0.0))
    w(g, gp)
    g.affine_select(sb["identf"][:, :], sb["onesf"][:, :], pattern=[[-1, 128]],
                    compare_op=ALU.is_equal, fill=0.0, base=0,
                    channel_multiplier=1)
    bd4 = sb["mask_bdsl"][:, :].rearrange("p (hf cf) -> p hf cf", cf=32)
    on4 = sb["ones"][:, :].rearrange("p (hf cf) -> p hf cf", cf=32)
    gp.inc(g.affine_select(bd4, on4, pattern=[[-32, 4], [-1, 32]],
                           compare_op=ALU.is_ge, fill=0.0, base=-1,
                           channel_multiplier=1))
    w(g, gp)
    gp.inc(g.affine_select(bd4, bd4, pattern=[[-32, 4], [0, 32]],
                           compare_op=ALU.is_ge, fill=0.0, base=0,
                           channel_multiplier=1))
    w(g, gp)
    gp.inc(g.affine_select(bd4, bd4, pattern=[[32, 4], [0, 32]],
                           compare_op=ALU.is_ge, fill=0.0, base=31,
                           channel_multiplier=-1))
    w(g, gp)
    for r0 in range(0, 128, 32):
        mt4 = sb["mask_triu"][r0:r0 + 32, :].rearrange("p (hf cf) -> p hf cf", cf=32)
        on32 = sb["ones"][0:32, :].rearrange("p (hf cf) -> p hf cf", cf=32)
        gp.inc(g.affine_select(mt4, on32, pattern=[[0, 4], [1, 32]],
                               compare_op=ALU.is_ge, fill=0.0, base=0,
                               channel_multiplier=-1))
        w(g, gp)
    gp.inc(g.memset(sb["sel4"][:, :], 0.0))
    gp.inc(g.memset(sb["selh"][:, :], 0.0))
    gp.inc(g.memset(sb["selh16"][:, :], 0.0))
    gp.inc(g.memset(sb["hsel"][:, :], 0.0))
    gp.inc(g.memset(sb["sumsel"][:, :], 0.0))
    w(v, gp)
    ve.inc(v.tensor_copy(sb["ident"][:, :], sb["identf"][:, :]))
    w(g, ve)
    for h in range(H):
        qg.inc(g.dma_start(out=sb["sel4"][32 * h:32 * h + 32, :],
                           in_=sb["ident"][0:32, 0:32]), 16)
        qg.inc(g.dma_start(out=sb["selh"][h:h + 1, 128 * h:128 * h + 128],
                           in_=sb["ones"][0:1, 0:128]), 16)
        qg.inc(g.dma_start(out=sb["hsel"][0:128, h * 4 + h: h * 4 + h + 1],
                           in_=sb["ones"][0:128, 0:1]), 16)
    for j in range(16):
        qg.inc(g.dma_start(out=sb["selh16"][j:j + 1, 128 * j:128 * j + 128],
                           in_=sb["ones"][0:1, 0:128]), 16)
        h = j // 4
        qg.inc(g.dma_start(out=sb["sumsel"][4 * h:4 * h + 4, j:j + 1],
                           in_=sb["ones"][0:4, 0:1]), 16)

    w(v, qg)
    b.selfw(v, ve)
    ve.inc(v.tensor_copy(sb["sumself"][:, :], sb["sumsel"][:, :]))
    b.selfw(v, ve)
    ve.inc(v.tensor_copy(sb["selhf"][:, :], sb["selh"][:, :]))
    b.selfw(v, ve)
    ve.inc(v.tensor_scalar(sb["negselh"][:, :], sb["selh"][:, :], -1.0, None, ALU.mult))
    for j, nm in enumerate(["conv_q", "conv_k", "conv_v"]):
        for dd in range(NDT):
            qs.inc(s.dma_start(
                out=sb["convw"][0:128, dd * 16 + j * 4: dd * 16 + j * 4 + 4],
                in_=W[nm][dd * 128:(dd + 1) * 128, :]), 16)
    for dd in range(NDT):
        qs.inc(s.dma_start(out=sb["firs"][0:128, dd * 4: dd * 4 + 3],
                           in_=W["fir_s"][dd * 128:(dd + 1) * 128, :]), 16)
        qs.inc(s.dma_start(out=sb["firl"][0:128, dd * 64: dd * 64 + 63],
                           in_=W["fir_l"][dd * 128:(dd + 1) * 128, :]), 16)
    qs.inc(s.dma_start(out=sb["normw"][0:128, 0:1],
                       in_=W["norm_w"][0:1, 0:128].rearrange("a b -> b a")), 16)
    qs.inc(s.dma_start(out=sb["normw"][0:128, 1:2],
                       in_=W["norm_w"][0:1, 128:256].rearrange("a b -> b a")), 16)
    for mt in range(16):
        qs.inc(s.dma_start(
            out=sb["b1t"][0:128, mt:mt + 1],
            in_=W["b1"][0:1, mt * 128:(mt + 1) * 128].rearrange("a b -> b a")), 16)
    qs.inc(s.dma_start(out=sb["misc"][0:1, 0:4], in_=W["forget"]), 16)
    qs.inc(s.dma_start(out=sb["misc"][0:1, 4:8], in_=W["log_temp"]), 16)
    qs.inc(s.dma_start(out=sb["misc"][0:4, 8:12], in_=W["floor"]), 16)
    qs.inc(s.dma_start(out=sb["misc"][0:1, 12:28], in_=W["b2"]), 16)
    qg.inc(g.dma_start(out=sb["wb_w"][:, :].rearrange("p (kt h) -> p kt h", h=H),
                       in_=W["wb"].rearrange("(kt p) h -> p kt h", p=128)), 16)
    qg.inc(g.dma_start(out=sb["w2w"][:, :].rearrange("p (kt h) -> p kt h", h=16),
                       in_=W["w2"].rearrange("(kt p) h -> p kt h", p=128)), 16)
    w(a, qs)
    ac.inc(a.activation(sb["misc"][0:1, 32:36], sb["misc"][0:1, 0:4], AF.Sigmoid))
    w(v, ac)
    b.selfw(v, ve)
    ve.inc(v.tensor_scalar(sb["misc"][0:1, 32:36], sb["misc"][0:1, 32:36],
                           1.0 - FORGET_MIN, FORGET_MIN, ALU.mult, ALU.add))
    w(a, ve)
    b.selfw(a, ac)
    ac.inc(a.activation(sb["misc"][0:1, 36:40], sb["misc"][0:1, 4:8], AF.Exp))
    w(v, ac)
    b.selfw(v, ve)
    ve.inc(v.reciprocal(sb["misc"][0:1, 40:44], sb["misc"][0:1, 36:40]))
    w(a, ve)
    b.selfw(a, ac)
    ac.inc(a.activation(sb["misc"][0:4, 44:48], sb["misc"][0:4, 8:12], AF.Sigmoid))
    w(v, ac)
    b.selfw(v, ve)
    ve.inc(v.tensor_scalar(sb["misc"][0:4, 44:48], sb["misc"][0:4, 44:48],
                           MAX_FLOOR, None, ALU.mult))
    w(t, ve, gp)
    pe.inc(t.matmul(pp[2][:, 0:4], sb["onesf"][0:1, 0:128],
                    sb["misc"][0:1, 32:36], start=True, stop=True))
    w(v, pe)
    b.selfw(v, ve)
    ve.inc(v.tensor_copy(sb["lam"][:, 0:4], pp[2][:, 0:4]))
    w(t, ve)
    pe.inc(t.transpose(pp[3][0:16, 0:1], sb["misc"][0:1, 12:28], sb["identf"][0:1, 0:1]))
    w(v, pe)
    b.selfw(v, ve)
    ve.inc(v.tensor_copy(sb["p16"][0:16, 0:1], pp[3][0:16, 0:1]))
    w(g, ve, ac)
    b.selfw(g, gp)
    for h in range(4):
        for j in range(4):
            qg.inc(g.dma_start(out=sb["p16"][4 * h + j:4 * h + j + 1, 1:2],
                               in_=sb["misc"][0:1, 40 + h:41 + h]), 16)
            qg.inc(g.dma_start(out=sb["p16"][4 * h + j:4 * h + j + 1, 2:3],
                               in_=sb["misc"][h:h + 1, 44 + j:45 + j]), 16)


def p1(b):
    t, v, a, g, s = b.t, b.v, b.a, b.g, b.s
    sb, pp, w = b.sb, b.pp, b.w
    pe, ve, ac, gp, qs, qg, ccs = b.pe, b.ve, b.ac, b.gp, b.qs, b.qg, b.ccs
    for blk in range(NBLK):
        w(s, ac, ve)
        qs.inc(s.dma_start(out=sb["x_t"][:, :], in_=b.dx[blk * 128:(blk + 1) * 128, :]), 16)
        w(a, qs)
        b.selfw(a, ac)
        w(a, ve)
        ac.inc(a.activation(sb["sq_t"][:, :], sb["x_t"][:, :], AF.Square,
                            accum_out=sb["lns"][:, 0:1]))
        w(v, qs)
        b.selfw(v, ve)
        ve.inc(v.reduce_sum(sb["lns"][:, 1:2], sb["x_t"][:, :], AX.X))
        w(v, ac)
        b.selfw(v, ve)
        ve.inc(v.tensor_scalar(sb["lns"][:, 2:3], sb["lns"][:, 1:2], 1.0 / D, None, ALU.mult))
        ve.inc(v.tensor_scalar(sb["lns"][:, 3:4], sb["lns"][:, 0:1], 1.0 / D, None, ALU.mult))
        b.selfw(v, ve)
        ve.inc(v.tensor_mul(sb["lns"][:, 4:5], sb["lns"][:, 2:3], sb["lns"][:, 2:3]))
        b.selfw(v, ve)
        ve.inc(v.tensor_sub(sb["lns"][:, 5:6], sb["lns"][:, 3:4], sb["lns"][:, 4:5]))
        b.selfw(v, ve)
        ve.inc(v.tensor_scalar(sb["lns"][:, 6:7], sb["lns"][:, 5:6], 1e-5, None, ALU.add))
        b.selfw(v, ve)
        ve.inc(v.reciprocal(sb["lns"][:, 7:8], sb["lns"][:, 6:7]))
        w(a, ve)
        b.selfw(a, ac)
        ac.inc(a.activation(sb["lns"][:, 8:9], sb["lns"][:, 7:8], AF.Sqrt))
        w(v, ve, pe, ac)
        b.selfw(v, ve)
        ve.inc(v.tensor_scalar(sb["lns"][:, 10:11], sb["lns"][:, 2:3], -1.0,
                               None, ALU.mult))
        b.selfw(v, ve)
        ve.inc(v.tensor_scalar(sb["xn_t"][:, :], sb["x_t"][:, :],
                               sb["lns"][:, 10:11], sb["lns"][:, 8:9],
                               ALU.add, ALU.mult))
        b.selfw(v, ve)
        ve.inc(v.tensor_reduce(sb["amax"][:, blk:blk + 1], sb["xn_t"][:, :],
                               AX.X, ALU.max, apply_absolute_value=True))
        xf = sb["xfm"] if (blk // 4) % 2 == 0 else sb["xfm2"]
        for dd in range(NDT):
            bank = b.pb[dd % 2]
            w(t, ac, ve)
            pe.inc(t.transpose(bank[:, 0:128], sb["xn_t"][:, dd * 128:(dd + 1) * 128],
                               sb["ident"][:, :]))
            if dd % 2 == 0:
                w(a, pe, qs)
                b.selfw(a, ac)
                ac.inc(a.copy(
                    xf[:, dd * 512 + (blk % 4) * 128: dd * 512 + (blk % 4) * 128 + 128],
                    bank[:, 0:128]))
            else:
                w(v, pe, qs)
                b.selfw(v, ve)
                ve.inc(v.tensor_copy(
                    xf[:, dd * 512 + (blk % 4) * 128: dd * 512 + (blk % 4) * 128 + 128],
                    bank[:, 0:128]))
        if blk % 4 == 3:
            tt = blk // 4
            for dd in range(NDT):
                w(s, ve, ac)
                qs.inc(s.dma_start(
                    out=b.h_xn[dd * 128:(dd + 1) * 128, tt * 512:(tt + 1) * 512],
                    in_=xf[:, dd * 512:(dd + 1) * 512]), 16)
    b.selfw(v, ve)
    ve.inc(v.tensor_reduce(sb["amax"][:, 32:33], sb["amax"][:, 0:32], AX.X, ALU.max))
    w(t, ve)
    pe.inc(t.transpose(pp[2][0:1, 0:128], sb["amax"][:, 32:33], sb["identf"][:, :]))
    w(v, pe)
    b.selfw(v, ve)
    ve.inc(v.tensor_reduce(sb["sc"][0:1, 1:2], pp[2][0:1, 0:128], AX.X, ALU.max))
    b.selfw(v, ve)
    ve.inc(v.tensor_copy(sb["sc"][0:1, 2:3], sb["sc"][0:1, 1:2]))
    w(g, ve)
    qg.inc(g.dma_start(out=b.ar_in[0].ap()[0:1, 0:2], in_=sb["sc"][0:1, 1:3]), 16)
    w(g, qg)
    ccs.inc(g.collective_compute("AllReduce", ALU.max,
                                 replica_groups=[list(range(8))],
                                 ins=[b.ar_in[0].ap()], outs=[b.ar_out[0].ap()]))
    w(g, ccs)
    qg.inc(g.dma_start(out=sb["sc"][0:1, 4:6], in_=b.ar_out[0].ap()[0:1, 0:2]), 16)
    w(v, qg)
    b.selfw(v, ve)
    ve.inc(v.tensor_scalar(sb["sc"][0:1, 6:7], sb["sc"][0:1, 4:5],
                           1.0 / 32767.0, 1e-12, ALU.mult, ALU.max))
    b.selfw(v, ve)
    ve.inc(v.reciprocal(sb["sc"][0:1, 7:8], sb["sc"][0:1, 6:7]))
    w(t, ve)
    pe.inc(t.matmul(pp[3][:, 0:2], sb["onesf"][0:1, 0:128], sb["sc"][0:1, 6:8],
                    start=True, stop=True))
    w(v, pe)
    b.selfw(v, ve)
    ve.inc(v.tensor_copy(sb["scb"][:, 0:2], pp[3][:, 0:2]))


def p3(b):
    t, v, a, g, s = b.t, b.v, b.a, b.g, b.s
    sb, pp, W, w = b.sb, b.pp, b.W, b.w
    pe, ve, ac, gp, qs, qa, qg = b.pe, b.ve, b.ac, b.gp, b.qs, b.qa, b.qg

    def sweep(wname, convj, dest):
        b.barrier()
        qg.inc(g.dma_start(out=sb["wbuf"][:, :].rearrange("p (kt c) -> p kt c", c=D),
                           in_=W[wname].rearrange("(kt p) c -> p kt c", p=128)), 16)
        w(g, ve, ac, pe, qs, qa)
        b.selfw(g, gp)
        gp.inc(g.memset(sb["pre"][:, :], 0.0))
        w(v, gp)
        w(a, gp)
        w(t, gp)
        bankseq = {}
        need = {}
        for tt in range(NTT):
            w(s, pe)
            qs.inc(s.dma_start(out=sb["xs"][:, :],
                               in_=b.hview(b.h_xn)[:, :, tt * 512:(tt + 1) * 512]), 16)
            for mt in range(NDT):
                bA = (mt % 2) * 2
                w(t, qs, qg)
                if bA in need:
                    t.wait_ge(ve.sem, need[bA][0])
                    t.wait_ge(ac.sem, need[bA][1])
                for kt in range(NDT):
                    bk = bA + kt % 2
                    if bk in bankseq:
                        t.wait_ge(pe.sem, bankseq[bk])
                    tok = t.matmul(
                        pp[bk][:, 0:512],
                        sb["wbuf"][:, kt * D + mt * 128: kt * D + mt * 128 + 128],
                        sb["xs"][:, :].rearrange("p (dt c) -> p dt c", dt=NDT)[:, kt, :],
                        start=kt < 2, stop=kt >= 6)
                    bankseq[bk] = b.pe.inc(tok)
                base = mt * 520
                bA, bB = (mt % 2) * 2, (mt % 2) * 2 + 1
                w(v, ac)
                b.selfw(v, ve)
                ve.inc(v.tensor_copy(sb["pre"][:, base + 0: base + 3],
                                     sb["pre"][:, base + 512: base + 515]))
                w(a, pe, ve)
                b.selfw(a, ac)
                ac.inc(a.copy(sb["ev1"][:, :], pp[bA][:, 0:512]))
                w(v, ac, pe)
                b.selfw(v, ve)
                ve.inc(v.tensor_add(sb["pre"][:, base + 3: base + 515],
                                    pp[bB][:, 0:512], sb["ev1"][:, :]))
                need[bA] = (ve.n, ac.n)
                need[bB] = (ve.n, ac.n)
                cw = sb["convw"]
                w(v, qa)
                b.selfw(v, ve)
                ve.inc(v.tensor_scalar(
                    sb["accA"][:, :], sb["pre"][:, base + 0: base + 512],
                    cw[:, mt * 16 + convj * 4 + 0: mt * 16 + convj * 4 + 1],
                    None, ALU.mult))
                for tap in range(1, 4):
                    src, dst = ("accA", "accB") if tap % 2 == 1 else ("accB", "accA")
                    b.selfw(v, ve)
                    ve.inc(v.scalar_tensor_tensor(
                        sb[dst][:, :], sb["pre"][:, base + tap: base + tap + 512],
                        cw[:, mt * 16 + convj * 4 + tap: mt * 16 + convj * 4 + tap + 1],
                        sb[src][:, :], ALU.mult, ALU.add))
                w(a, ve)
                b.selfw(a, ac)
                if dest == "v":
                    w(a, qa)
                    ac.inc(a.activation(sb["accA"][:, :], sb["accB"][:, :], AF.Silu))
                    w(a, ac)
                    qa.inc(a.dma_start(
                        out=b.h_v[mt * 128:(mt + 1) * 128, tt * 512:(tt + 1) * 512],
                        in_=sb["accA"][:, :]), 16)
                else:
                    ac.inc(a.activation(
                        sb[dest][:, mt * T + tt * 512: mt * T + (tt + 1) * 512],
                        sb["accB"][:, :], AF.Silu))

    sweep("wv", 2, "v")
    sweep("wk", 1, "k")
    sweep("wq", 0, "q")

    b.barrier()
    for tt in range(NTT):
        w(s, pe, ac)
        qs.inc(s.dma_start(out=sb["xs"][:, :],
                           in_=b.hview(b.h_xn)[:, :, tt * 512:(tt + 1) * 512]), 16)
        last = None
        for kt in range(NDT):
            w(t, qs, ve, ac)
            if last is not None:
                t.wait_ge(pe.sem, last)
            tok = t.matmul(pp[4][0:4, 0:512], sb["wb_w"][:, kt * 4:(kt + 1) * 4],
                           sb["xs"][:, :].rearrange("p (dt c) -> p dt c", dt=NDT)[:, kt, :],
                           start=kt == 0, stop=kt == 7)
            last = pe.inc(tok)
        w(a, pe)
        b.selfw(a, ac)
        ac.inc(a.activation(sb["rows1"][0:4, tt * 512:(tt + 1) * 512],
                            pp[4][0:4, 0:512], AF.Sigmoid))


    def l2pass(src, rowbase):
        b.barrier()
        for tt in range(NTT):
            last = None
            for dd in range(NDT):
                h = dd // 2
                w(a, pe, ve)
                b.selfw(a, ac)
                ac.inc(a.activation(sb["sqrow"][:, 0:512],
                                    sb[src][:, dd * T + tt * 512: dd * T + (tt + 1) * 512],
                                    AF.Square))
                w(t, ac)
                if last is not None:
                    t.wait_ge(pe.sem, last)
                last = pe.inc(t.matmul(pp[4][0:4, 0:512], sb["hsel"][:, h * 4: h * 4 + 4],
                                       sb["sqrow"][:, 0:512],
                                       start=dd == 0, stop=dd == 7))
            w(v, pe)
            b.selfw(v, ve)
            ve.inc(v.tensor_scalar(sb["ev1"][0:4, 0:512], pp[4][0:4, 0:512],
                                   1e-6, None, ALU.add))
            b.selfw(v, ve)
            ve.inc(v.reciprocal(sb["ev1"][0:4, 0:512], sb["ev1"][0:4, 0:512]))
            w(a, ve)
            b.selfw(a, ac)
            ac.inc(a.activation(sb["ev1"][0:4, 0:512], sb["ev1"][0:4, 0:512], AF.Sqrt))
            w(t, ac)
            tp = (0, rowbase) if rowbase == 96 else None
            pe.inc(t.matmul(pp[5][rowbase:rowbase + 4, 0:512],
                            sb["identf"][0:4, 0:4],
                            sb["ev1"][0:4, 0:512], start=True, stop=True,
                            tile_position=tp))
            w(v, pe)
            b.selfw(v, ve)
            ve.inc(v.tensor_copy(sb["rows1"][rowbase:rowbase + 4, tt * 512:(tt + 1) * 512],
                                 pp[5][rowbase:rowbase + 4, 0:512]))
            for h in range(4):
                bk = pp[2 + h % 2]
                w(t, ve)
                pe.inc(t.matmul(bk[:, 0:512], sb["selhf"][0:4, h * 128:(h + 1) * 128],
                                sb["ev1"][0:4, 0:512], start=True, stop=True))
                for dd in (2 * h, 2 * h + 1):
                    w(v, pe)
                    b.selfw(v, ve)
                    ve.inc(v.tensor_mul(
                        sb[src][:, dd * T + tt * 512: dd * T + (tt + 1) * 512],
                        sb[src][:, dd * T + tt * 512: dd * T + (tt + 1) * 512],
                        bk[:, 0:512]))

    l2pass("q", 64)
    l2pass("k", 96)


def p4(b, nblk=NBLK):
    t, v, a, g, s = b.t, b.v, b.a, b.g, b.s
    sb, pp, w = b.sb, b.pp, b.w
    pe, ve, ac, gp, qs, qg = b.pe, b.ve, b.ac, b.gp, b.qs, b.qg
    b.barrier()
    g.memset(sb["S_f"][:, :], 0.0)
    gp.inc(g.memset(sb["S_b"][:, :], 0.0))
    w(t, gp)

    for b4 in range(nblk):
        c0b = b4 * 128
        w(s, pe, ve, ac)
        qs.inc(s.dma_start(out=sb["vblk"][:, :],
                           in_=b.hview(b.h_v)[:, :, c0b:c0b + 128]), 16)
        w(t, qs, ve, ac)
        pe.inc(t.transpose(b.pb[0][0:128, 0:4], sb["rows1"][0:4, c0b:c0b + 128],
                           sb["ident"][0:4, 0:4]))
        w(v, pe)
        b.selfw(v, ve)
        ve.inc(v.tensor_copy(sb["btm"][:, 0:4], b.pb[0][0:128, 0:4]))
        for dd in range(NDT):
            h = dd // 2
            bank = b.pb[dd % 2]
            w(t, ve, ac)
            pe.inc(t.transpose(bank[:, 0:128],
                               sb["k"][:, dd * T + c0b: dd * T + c0b + 128],
                               sb["ident"][:, :]))
            w(a, pe)
            b.selfw(a, ac)
            ac.inc(a.copy(sb["ktm"][:, dd * 128:(dd + 1) * 128], bank[:, 0:128]))
            w(t, ac, qs)
            pe.inc(t.transpose(bank[:, 0:128], sb["vblk"][:, dd * 128:(dd + 1) * 128],
                               sb["ident"][:, :]))
            w(v, pe)
            b.selfw(v, ve)
            ve.inc(v.tensor_scalar(sb["vtm"][:, dd * 128:(dd + 1) * 128],
                                   bank[:, 0:128], sb["btm"][:, h:h + 1],
                                   None, ALU.mult))
        for h in range(4):
            bank = pp[4 + h % 2]
            w(t, ve)
            pe.inc(t.matmul(bank[:, 0:128], sb["negselh"][0:4, h * 128:(h + 1) * 128],
                            sb["rows1"][0:4, c0b:c0b + 128], start=True, stop=True))
            for dd in (2 * h, 2 * h + 1):
                w(v, pe)
                b.selfw(v, ve)
                ve.inc(v.tensor_mul(sb["kbn"][:, dd * 128:(dd + 1) * 128],
                                    sb["k"][:, dd * T + c0b: dd * T + c0b + 128],
                                    bank[:, 0:128]))
        for ci in range(4):
            c0 = ci * 32
            tpA = (0, c0) if c0 == 96 else None
            tpB = (96, 96) if c0 == 96 else None
            tpC = (96, 0) if c0 == 96 else None
            tok0 = c0b + ci * 32
            lastg = None
            for h in range(4):
                for dkt in range(2):
                    dd = 2 * h + dkt
                    w(t, ve, ac)
                    if lastg is not None and dkt == 1:
                        t.wait_ge(pe.sem, lastg)
                    tok = t.matmul(
                        pp[2][32 * h:32 * h + 32, 32 * h:32 * h + 32],
                        sb["kbn"][:, dd * 128 + ci * 32: dd * 128 + ci * 32 + 32],
                        sb["k"][:, dd * T + tok0: dd * T + tok0 + 32],
                        start=dkt == 0, stop=dkt == 1,
                        tile_position=(0, 32 * h) if h == 3 else None)
                    lastg = pe.inc(tok)
            w(v, pe)
            b.selfw(v, ve)
            ve.inc(v.scalar_tensor_tensor(sb["tmA"][:, :], pp[2][:, 0:128], 1.0,
                                          sb["mask_bdsl"][:, :], ALU.mult, ALU.mult))
            b.selfw(v, ve)
            ve.inc(v.transpose(sb["tmB"][:, :], sb["tmA"][:, :]))
            b.selfw(v, ve)
            ve.inc(v.tensor_add(sb["ttA"][:, :], sb["tmB"][:, :], sb["ident"][:, :]))
            ttc, ttn = "ttA", "ttB"
            for j in range(4):
                w(t, ve, ac)
                pe.inc(t.matmul(pp[3][:, 0:128], sb["tmB"][:, :], sb["tmA"][:, :],
                                start=True, stop=True))
                w(a, pe)
                b.selfw(a, ac)
                ac.inc(a.copy(sb["tmA"][:, :], pp[3][:, 0:128]))
                w(v, ac)
                b.selfw(v, ve)
                ve.inc(v.transpose(sb["tmB"][:, :], sb["tmA"][:, :]))
                w(t, ac, ve)
                t.wait_ge(pe.sem, pe.n)
                pe.inc(t.matmul(pp[3][:, 128:256], sb["tmA"][:, :], sb[ttc][:, :],
                                start=True, stop=True))
                w(v, pe)
                b.selfw(v, ve)
                ve.inc(v.scalar_tensor_tensor(sb[ttn][:, :], pp[3][:, 128:256], 1.0,
                                              sb[ttc][:, :], ALU.mult, ALU.add))
                ttc, ttn = ttn, ttc
            w(t, ve)
            pe.inc(t.matmul(pp[5][c0:c0 + 32, 0:128], sb["sel4"][:, :], sb[ttc][:, :],
                            start=True, stop=True, tile_position=tpA))
            w(a, pe)
            b.selfw(a, ac)
            ac.inc(a.copy(sb["ttf"][c0:c0 + 32, :], pp[5][c0:c0 + 32, 0:128]))
            lasta = None
            for h in range(4):
                for dkt in range(2):
                    dd = 2 * h + dkt
                    w(t, ac, ve)
                    if lasta is not None:
                        t.wait_ge(pe.sem, lasta)
                    tok = t.matmul(
                        pp[4][c0:c0 + 32, 32 * h:32 * h + 32],
                        sb["k"][:, dd * T + tok0: dd * T + tok0 + 32],
                        sb["q"][:, dd * T + tok0: dd * T + tok0 + 32],
                        start=dkt == 0, stop=dkt == 1, tile_position=tpA)
                    lasta = pe.inc(tok)
            w(v, pe)
            b.selfw(v, ve)
            ve.inc(v.scalar_tensor_tensor(sb["attf"][c0:c0 + 32, :],
                                          pp[4][c0:c0 + 32, 0:128], 1.0,
                                          sb["mask_triu"][c0:c0 + 32, :],
                                          ALU.mult, ALU.mult))
            zl = {}
            for h in range(4):
                for dkt in range(2):
                    dd = 2 * h + dkt
                    bank = pp[0] if h < 2 else pp[1]
                    fo = (h % 2) * 256
                    w(t, ve, ac)
                    key = (h < 2)
                    if key in zl:
                        t.wait_ge(pe.sem, zl[key])
                    tok = t.matmul(bank[c0:c0 + 32, fo:fo + 256],
                                   sb["kbn"][:, dd * 128 + ci * 32: dd * 128 + ci * 32 + 32],
                                   sb["S_b"][:, dkt * 1024 + h * 256: dkt * 1024 + (h + 1) * 256],
                                   start=dkt == 0, stop=dkt == 1, tile_position=tpA)
                    zl[key] = pe.inc(tok)
            w(v, pe)
            b.selfw(v, ve)
            ve.inc(v.scalar_tensor_tensor(sb["z_sb"][c0:c0 + 32, 0:512],
                                          pp[0][c0:c0 + 32, 0:512], 1.0,
                                          sb["vtm"][c0:c0 + 32, 0:512],
                                          ALU.mult, ALU.add))
            b.selfw(v, ve)
            ve.inc(v.scalar_tensor_tensor(sb["z_sb"][c0:c0 + 32, 512:1024],
                                          pp[1][c0:c0 + 32, 0:512], 1.0,
                                          sb["vtm"][c0:c0 + 32, 512:1024],
                                          ALU.mult, ALU.add))
            for h in range(4):
                bank = pp[4] if h < 2 else pp[5]
                fo = (h % 2) * 256
                w(t, ve)
                t.wait_ge(pe.sem, pe.n)
                pe.inc(t.matmul(bank[c0:c0 + 32, fo:fo + 256],
                                sb["ttf"][c0:c0 + 32, 32 * h:32 * h + 32],
                                sb["z_sb"][c0:c0 + 32, h * 256:(h + 1) * 256],
                                start=True, stop=True, tile_position=tpB))
            w(v, pe)
            b.selfw(v, ve)
            ve.inc(v.tensor_copy(sb["u_sb"][c0:c0 + 32, 0:512], pp[4][c0:c0 + 32, 0:512]))
            b.selfw(v, ve)
            ve.inc(v.tensor_copy(sb["u_sb"][c0:c0 + 32, 512:1024], pp[5][c0:c0 + 32, 0:512]))
            ol = {}
            for h in range(4):
                bank = pp[0] if h < 2 else pp[1]
                fo = (h % 2) * 256
                for dkt in range(2):
                    dd = 2 * h + dkt
                    w(t, ve)
                    key = (h < 2)
                    if key in ol:
                        t.wait_ge(pe.sem, ol[key])
                    ol[key] = pe.inc(t.matmul(
                        bank[c0:c0 + 32, fo:fo + 256],
                        sb["q"][:, dd * T + tok0: dd * T + tok0 + 32],
                        sb["S_b"][:, dkt * 1024 + h * 256: dkt * 1024 + (h + 1) * 256],
                        start=dkt == 0, stop=dkt == 1, tile_position=tpA))
                bank2 = pp[4] if h < 2 else pp[5]
                w(t, ve)
                t.wait_ge(pe.sem, pe.n)
                ol[("a", h < 2)] = pe.inc(t.matmul(
                    bank2[c0:c0 + 32, fo:fo + 256],
                    sb["attf"][c0:c0 + 32, 32 * h:32 * h + 32],
                    sb["u_sb"][c0:c0 + 32, h * 256:(h + 1) * 256],
                    start=True, stop=True, tile_position=tpB))
            w(v, pe)
            b.selfw(v, ve)
            ve.inc(v.tensor_copy(sb["o_tm"][c0:c0 + 32, 0:512], pp[4][c0:c0 + 32, 0:512]))
            b.selfw(v, ve)
            ve.inc(v.tensor_copy(sb["o_tm"][c0:c0 + 32, 512:1024], pp[5][c0:c0 + 32, 0:512]))
            b.selfw(v, ve)
            ve.inc(v.scalar_tensor_tensor(sb["o_tm"][c0:c0 + 32, 0:512],
                                          pp[0][c0:c0 + 32, 0:512], 1.0,
                                          sb["o_tm"][c0:c0 + 32, 0:512],
                                          ALU.mult, ALU.add))
            b.selfw(v, ve)
            ve.inc(v.scalar_tensor_tensor(sb["o_tm"][c0:c0 + 32, 512:1024],
                                          pp[1][c0:c0 + 32, 0:512], 1.0,
                                          sb["o_tm"][c0:c0 + 32, 512:1024],
                                          ALU.mult, ALU.add))
            for wave in range(2):
                for i2 in range(4):
                    h = wave * 2 + i2 // 2
                    dkt = i2 % 2
                    bank = pp[2 + i2]
                    w(t, ve, ac)
                    t.wait_ge(pe.sem, pe.n)
                    pe.inc(t.matmul(
                        bank[0:128, 0:256],
                        sb["ktm"][c0:c0 + 32, (2 * h + dkt) * 128:(2 * h + dkt + 1) * 128],
                        sb["u_sb"][c0:c0 + 32, h * 256:(h + 1) * 256],
                        start=True, stop=True, tile_position=tpC))
                for i2 in range(4):
                    h = wave * 2 + i2 // 2
                    dkt = i2 % 2
                    bank = pp[2 + i2]
                    sl = slice(dkt * 1024 + h * 256, dkt * 1024 + (h + 1) * 256)
                    w(v, pe)
                    b.selfw(v, ve)
                    ve.inc(v.scalar_tensor_tensor(sb["S_f"][:, sl], sb["S_f"][:, sl],
                                                  sb["lam"][:, h:h + 1],
                                                  bank[0:128, 0:256],
                                                  ALU.mult, ALU.add))
                    w(a, ve)
                    b.selfw(a, ac)
                    ac.inc(a.copy(sb["S_b"][:, sl], sb["S_f"][:, sl]))
        for dd in range(NDT):
            bank = b.pb[dd % 2]
            w(t, ve, ac)
            pe.inc(t.transpose(bank[:, 0:128], sb["o_tm"][:, dd * 128:(dd + 1) * 128],
                               sb["ident"][:, :]))
            if dd % 2 == 0:
                w(a, pe, qs)
                b.selfw(a, ac)
                ac.inc(a.copy(
                    sb["dfm"][:, dd * 512 + (b4 % 4) * 128: dd * 512 + (b4 % 4) * 128 + 128],
                    bank[:, 0:128]))
            else:
                w(v, pe, qs)
                b.selfw(v, ve)
                ve.inc(v.tensor_copy(
                    sb["dfm"][:, dd * 512 + (b4 % 4) * 128: dd * 512 + (b4 % 4) * 128 + 128],
                    bank[:, 0:128]))
        if b4 % 4 == 3:
            tt = b4 // 4
            for dd in range(NDT):
                w(s, ve, ac)
                qs.inc(s.dma_start(
                    out=b.h_df[dd * 128:(dd + 1) * 128, tt * 512:(tt + 1) * 512],
                    in_=sb["dfm"][:, dd * 512:(dd + 1) * 512]), 16)


def p5(b, upto="all"):
    t, v, a, g, s = b.t, b.v, b.a, b.g, b.s
    sb, pp, W, w = b.sb, b.pp, b.W, b.w
    pe, ve, ac, gp, qs, qa, qg = b.pe, b.ve, b.ac, b.gp, b.qs, b.qa, b.qg

    b.barrier()
    for dd in range(NDT):
        w(s, ve, ac, qa)
        qs.inc(s.dma_start(out=sb["vrow"][:, :], in_=b.h_v[dd * 128:(dd + 1) * 128, :]), 16)
        w(v, qs, qa)
        b.selfw(v, ve)
        ve.inc(v.tensor_scalar(sb["srow"][:, 0:T], sb["vrow"][:, 0:T],
                               sb["firs"][:, dd * 4 + 2: dd * 4 + 3], None, ALU.mult))
        for tap in range(2):
            sh = 2 - tap
            b.selfw(v, ve)
            ve.inc(v.scalar_tensor_tensor(sb["srow"][:, sh:T], sb["vrow"][:, 0:T - sh],
                                          sb["firs"][:, dd * 4 + tap: dd * 4 + tap + 1],
                                          sb["srow"][:, sh:T], ALU.mult, ALU.add))
        w(a, ve)
        qa.inc(a.dma_start(out=b.h_sh[dd * 128:(dd + 1) * 128, :], in_=sb["srow"][:, :]), 16)
        w(v, qa)
        b.selfw(v, ve)
        ve.inc(v.tensor_scalar(sb["lrow"][:, 0:T], sb["vrow"][:, 0:T],
                               sb["firl"][:, dd * 64 + 62: dd * 64 + 63], None, ALU.mult))
        for tap in range(62):
            sh = 62 - tap
            b.selfw(v, ve)
            ve.inc(v.scalar_tensor_tensor(sb["lrow"][:, sh:T], sb["vrow"][:, 0:T - sh],
                                          sb["firl"][:, dd * 64 + tap: dd * 64 + tap + 1],
                                          sb["lrow"][:, sh:T], ALU.mult, ALU.add))
        w(a, ve)
        b.selfw(a, ac)
        qa.inc(a.dma_start(out=b.h_lo[dd * 128:(dd + 1) * 128, :], in_=sb["lrow"][:, :]), 16)

    if upto == "fir":
        return

    def do_stats(pair, srcs, phase="all"):
        b.barrier()
        for sub, src_h in enumerate(srcs):
            for dd in range(NDT):
                slot = "vrow" if dd % 2 == 0 else "srow"
                w(s, ve, ac, pe)
                qs.inc(s.dma_start(out=sb[slot][:, :],
                                   in_=src_h[dd * 128:(dd + 1) * 128, :]), 16)
                for ti2 in range(32):
                    bank = b.pb[ti2 % 2]
                    c = sub * 256 + ti2 * 8 + dd
                    w(t, qs, ve, ac)
                    pe.inc(t.transpose(bank[:, 0:128],
                                       sb[slot][:, ti2 * 128:(ti2 + 1) * 128],
                                       sb["ident"][:, :]))
                    if phase != "accV":
                        w(a, pe)
                        b.selfw(a, ac)
                        ac.inc(a.activation(sb["scr"][:, :], bank[:, 0:128], AF.Copy,
                                            accum_out=sb["sumP"][:, c:c + 1]))
                        b.selfw(a, ac)
                        ac.inc(a.activation(sb["scr"][:, :], bank[:, 0:128], AF.Square,
                                            accum_out=sb["sqP"][:, c:c + 1]))
                    if phase != "accA":
                        w(v, ac, pe)
                        b.selfw(v, ve)
                        ve.inc(v.tensor_reduce(sb["maxP"][:, c:c + 1], bank[:, 0:128],
                                               AX.X, ALU.max))
        if phase == "acc":
            return
        sP = lambda nm, sub, t2: sb[nm][:, sub * 256 + t2 * 8:
                                        sub * 256 + (t2 + 1) * 8].rearrange(
            "p (h two) -> p h two", two=2)
        pbase = pair * 32
        for ti2 in range(32):
            for sub in range(2):
                c0 = ti2 * 32 + sub * 16
                stm = sb["statm"][:, c0:c0 + 16].rearrange(
                    "p (h st) -> p h st", st=4)
                se, so = sP("sumP", sub, ti2)[:, :, 0], sP("sumP", sub, ti2)[:, :, 1]
                qe, qo = sP("sqP", sub, ti2)[:, :, 0], sP("sqP", sub, ti2)[:, :, 1]
                me, mo = sP("maxP", sub, ti2)[:, :, 0], sP("maxP", sub, ti2)[:, :, 1]
                w(v, ac, pe)
                b.selfw(v, ve)
                ve.inc(v.tensor_add(stm[:, :, 0], se, so))
                b.selfw(v, ve)
                ve.inc(v.tensor_scalar(stm[:, :, 0], stm[:, :, 0], 1.0 / 256.0,
                                       None, ALU.mult))
                b.selfw(v, ve)
                ve.inc(v.tensor_add(stm[:, :, 3], qe, qo))
                b.selfw(v, ve)
                ve.inc(v.tensor_mul(stm[:, :, 1], stm[:, :, 0], stm[:, :, 0]))
                b.selfw(v, ve)
                ve.inc(v.scalar_tensor_tensor(stm[:, :, 1], stm[:, :, 3], 1.0 / 256.0,
                                              stm[:, :, 1], ALU.mult, ALU.subtract))
                b.selfw(v, ve)
                ve.inc(v.tensor_max(stm[:, :, 2], me, mo))
                w(a, ve)
                b.selfw(a, ac)
                ac.inc(a.activation(stm[:, :, 3], stm[:, :, 3], AF.Sqrt))
            bank = pp[ti2 % 2]
            w(t, ve, ac)
            t.wait_ge(pe.sem, pe.n)
            pe.inc(t.matmul(bank[pbase:pbase + 32, 0:128],
                            sb["statm"][:, ti2 * 32:(ti2 + 1) * 32],
                            sb["identf"][:, :], start=True, stop=True))
            w(v, pe)
            b.selfw(v, ve)
            ve.inc(v.tensor_copy(sb["gstats"][pbase:pbase + 32,
                                              ti2 * 128:(ti2 + 1) * 128],
                                 bank[pbase:pbase + 32, 0:128]))

    if upto in ("acc0", "accA", "accV"):
        do_stats(0, (b.h_sh, b.h_lo),
                 phase="acc" if upto == "acc0" else upto)
        return
    do_stats(0, (b.h_sh, b.h_lo))
    if upto == "stats0":
        return
    do_stats(1, (b.h_df, b.h_v))
    if upto == "stats":
        return

    b.barrier()
    qg.inc(g.dma_start(out=sb["w1b"][:, 0:8 * 2048].rearrange("p (kt c) -> p kt c", c=2048),
                       in_=W["w1"][0:1024, :].rearrange("(kt p) c -> p kt c", p=128)), 16)
    qg.inc(g.dma_start(out=sb["w1b"][0:64, 8 * 2048:9 * 2048],
                       in_=W["w1"][1024:1088, :]), 16)
    for tt in range(NTT):
        w(s, pe, ve, ac)
        qs.inc(s.dma_start(out=sb["gxs"][:, :],
                           in_=b.hview(b.h_xn)[:, :, tt * 512:(tt + 1) * 512]), 16)
        gneed = {}
        lastb = {}
        for mt in range(16):
            bA = (mt % 2) * 2
            w(t, qs, qg)
            if bA in gneed:
                t.wait_ge(ve.sem, gneed[bA][0])
                t.wait_ge(ac.sem, gneed[bA][1])
            for kt in range(9):
                bk = bA + kt % 2
                if bk in lastb:
                    t.wait_ge(pe.sem, lastb[bk])
                if kt < 8:
                    rhs = sb["gxs"][:, :].rearrange("p (dt c) -> p dt c", dt=NDT)[:, kt, :]
                    lhs = sb["w1b"][:, kt * 2048 + mt * 128: kt * 2048 + mt * 128 + 128]
                else:
                    rhs = sb["gstats"][0:64, tt * 512:(tt + 1) * 512]
                    lhs = sb["w1b"][0:64, 8 * 2048 + mt * 128: 8 * 2048 + mt * 128 + 128]
                lastb[bk] = pe.inc(t.matmul(pp[bk][:, 0:512], lhs, rhs,
                                            start=kt < 2, stop=kt >= 7))
            w(a, pe, ve)
            b.selfw(a, ac)
            ac.inc(a.copy(sb["ev5"][:, :], pp[bA][:, 0:512]))
            w(v, ac, pe)
            b.selfw(v, ve)
            ve.inc(v.tensor_add(sb["ev6"][:, :], pp[bA + 1][:, 0:512], sb["ev5"][:, :]))
            w(a, ve)
            b.selfw(a, ac)
            ac.inc(a.activation(sb["hmid"][:, mt * 512:(mt + 1) * 512], sb["ev6"][:, :],
                                AF.Gelu, bias=sb["b1t"][:, mt:mt + 1]))
            gneed[bA] = (ve.n, ac.n)
            gneed[bA + 1] = gneed[bA]
        lastw = None
        w(t, ac, ve)
        for kt in range(16):
            if lastw is not None:
                t.wait_ge(pe.sem, lastw)
            lastw = pe.inc(t.matmul(pp[2][0:16, 0:512], sb["w2w"][:, kt * 16:(kt + 1) * 16],
                                    sb["hmid"][:, kt * 512:(kt + 1) * 512],
                                    start=kt == 0, stop=kt == 15))
        w(v, pe)
        b.selfw(v, ve)
        ve.inc(v.tensor_scalar(sb["zb16"][0:16, 0:512], pp[2][0:16, 0:512],
                               sb["p16"][0:16, 0:1], sb["p16"][0:16, 1:2],
                               ALU.add, ALU.mult))
        w(a, ve)
        b.selfw(a, ac)
        ac.inc(a.activation(sb["e16"][0:16, 0:512], sb["zb16"][0:16, 0:512], AF.Exp))
        w(t, ac)
        t.wait_ge(pe.sem, pe.n)
        pe.inc(t.matmul(pp[3][0:16, 0:512], sb["sumself"][0:16, 0:16],
                        sb["e16"][0:16, 0:512], start=True, stop=True))
        w(v, pe)
        b.selfw(v, ve)
        ve.inc(v.reciprocal(sb["zb16"][0:16, 0:512], pp[3][0:16, 0:512]))
        b.selfw(v, ve)
        ve.inc(v.tensor_mul(sb["zb16"][0:16, 0:512], sb["zb16"][0:16, 0:512],
                            sb["e16"][0:16, 0:512]))
        b.selfw(v, ve)
        ve.inc(v.tensor_scalar(sb["e16"][0:16, 0:512], sb["zb16"][0:16, 0:512],
                               sb["p16"][0:16, 2:3], None, ALU.max))
        w(t, ve)
        t.wait_ge(pe.sem, pe.n)
        pe.inc(t.matmul(pp[3][0:16, 0:512], sb["sumself"][0:16, 0:16],
                        sb["e16"][0:16, 0:512], start=True, stop=True))
        w(v, pe)
        b.selfw(v, ve)
        ve.inc(v.reciprocal(sb["zb16"][0:16, 0:512], pp[3][0:16, 0:512]))
        b.selfw(v, ve)
        ve.inc(v.tensor_mul(sb["zb16"][0:16, 0:512], sb["zb16"][0:16, 0:512],
                            sb["e16"][0:16, 0:512]))
        b.selfw(v, ve)
        ve.inc(v.tensor_copy(sb["p_fm"][0:16, tt * 512:(tt + 1) * 512],
                             sb["zb16"][0:16, 0:512]))
    if upto == "gate":
        return

    b.barrier()
    qg.inc(g.dma_start(out=sb["wobuf"][:, :].rearrange("p (kt c) -> p kt c", c=D),
                       in_=W["wo"].rearrange("(kt p) c -> p kt c", p=128)), 16)
    for tt in range(NTT):
        sl = slice(tt * 512, (tt + 1) * 512)
        w(s, pe, ve, ac)
        qs.inc(s.dma_start(out=sb["vtile"][:, :], in_=b.hview(b.h_v)[:, :, sl]), 16)
        qs.inc(s.dma_start(out=sb["dtile"][:, :], in_=b.hview(b.h_df)[:, :, sl]), 16)
        qs.inc(s.dma_start(out=sb["stile"][:, :], in_=b.hview(b.h_sh)[:, :, sl]), 16)
        qs.inc(s.dma_start(out=sb["ltile"][:, :], in_=b.hview(b.h_lo)[:, :, sl]), 16)
        lastpb = {}
        for dd in range(NDT):
            h = dd // 2
            dsl = slice(dd * 512, (dd + 1) * 512)
            w(t, qs, ve)
            for j in range(4):
                if j in lastpb:
                    t.wait_ge(pe.sem, lastpb[j])
                lastpb[j] = pe.inc(t.matmul(
                    pp[j][:, 0:512],
                    sb["selh16"][0:16, (4 * h + j) * 128:(4 * h + j + 1) * 128],
                    sb["p_fm"][0:16, sl], start=True, stop=True))
            w(v, pe)
            b.selfw(v, ve)
            ve.inc(v.tensor_mul(sb["ocomb"][:, 0:512], sb["stile"][:, dsl], pp[0][:, 0:512]))
            b.selfw(v, ve)
            ve.inc(v.tensor_mul(sb["ocm2"][:, 0:512], sb["ltile"][:, dsl], pp[1][:, 0:512]))
            b.selfw(v, ve)
            ve.inc(v.tensor_add(sb["ocomb"][:, 0:512], sb["ocomb"][:, 0:512], sb["ocm2"][:, 0:512]))
            b.selfw(v, ve)
            ve.inc(v.tensor_mul(sb["ocm2"][:, 0:512], sb["dtile"][:, dsl], pp[2][:, 0:512]))
            b.selfw(v, ve)
            ve.inc(v.tensor_add(sb["ocomb"][:, 0:512], sb["ocomb"][:, 0:512], sb["ocm2"][:, 0:512]))
            b.selfw(v, ve)
            ve.inc(v.tensor_mul(sb["ocm2"][:, 0:512], sb["vtile"][:, dsl], pp[3][:, 0:512]))
            b.selfw(v, ve)
            ve.inc(v.tensor_add(sb["ocomb"][:, 0:512], sb["ocomb"][:, 0:512], sb["ocm2"][:, 0:512]))
            if dd % 2 == 1:
                w(a, ve)
                b.selfw(a, ac)
                ac.inc(a.activation(sb["sqcb"][:, 0:512], sb["ocm3"][:, 0:512],
                                    AF.Square))
                b.selfw(a, ac)
                ac.inc(a.activation(sb["sqcb"][:, 512:1024], sb["ocomb"][:, 0:512],
                                    AF.Square))
                w(t, ac)
                t.wait_ge(pe.sem, pe.n)
                rms1 = pe.inc(t.matmul(pp[5][0:1, 0:512], sb["ones"][:, 0:1],
                                       sb["sqcb"][:, 0:512], start=True, stop=False))
                t.wait_ge(pe.sem, rms1)
                pe.inc(t.matmul(pp[5][0:1, 0:512], sb["ones"][:, 0:1],
                                sb["sqcb"][:, 512:1024], start=False, stop=True))
                w(v, pe)
                b.selfw(v, ve)
                ve.inc(v.tensor_copy(sb["ev6"][0:1, 0:512], pp[5][0:1, 0:512]))
                b.selfw(v, ve)
                ve.inc(v.tensor_scalar(sb["ev6"][0:1, 0:512], sb["ev6"][0:1, 0:512],
                                       1.0 / 256.0, 1e-5, ALU.mult, ALU.add))
                b.selfw(v, ve)
                ve.inc(v.reciprocal(sb["ev6"][0:1, 0:512], sb["ev6"][0:1, 0:512]))
                w(a, ve)
                b.selfw(a, ac)
                ac.inc(a.activation(sb["ev6"][0:1, 0:512], sb["ev6"][0:1, 0:512], AF.Sqrt))
                w(t, ac, ve)
                t.wait_ge(pe.sem, pe.n)
                pe.inc(t.matmul(pp[4][:, 0:512], sb["onesf"][0:1, 0:128],
                                sb["ev6"][0:1, 0:512], start=True, stop=True))
                for d2 in (dd - 1, dd):
                    nwcol = 0 if d2 % 2 == 0 else 1
                    w(v, pe)
                    b.selfw(v, ve)
                    ve.inc(v.scalar_tensor_tensor(
                        sb["onrm"][:, d2 * 512:(d2 + 1) * 512],
                        sb["ocomb"][:, 0:512] if d2 == dd else sb["ocm3"][:, 0:512],
                        sb["normw"][:, nwcol:nwcol + 1], pp[4][:, 0:512],
                        ALU.mult, ALU.mult))
            if dd % 2 == 0:
                b.selfw(v, ve)
                ve.inc(v.tensor_copy(sb["ocm3"][:, 0:512], sb["ocomb"][:, 0:512]))
        for mt in range(NDT):
            lastb = {}
            w(t, ve, ac)
            for kt in range(NDT):
                bk = kt % 2
                if bk in lastb:
                    t.wait_ge(pe.sem, lastb[bk])
                lastb[bk] = pe.inc(t.matmul(
                    pp[bk][:, 0:512],
                    sb["wobuf"][:, kt * D + mt * 128: kt * D + mt * 128 + 128],
                    sb["onrm"][:, kt * 512:(kt + 1) * 512],
                    start=kt < 2, stop=kt >= 6))
            w(a, pe, ve)
            b.selfw(a, ac)
            ac.inc(a.copy(sb["ev5"][:, :], pp[0][:, 0:512]))
            w(v, ac, pe, qs)
            b.selfw(v, ve)
            ve.inc(v.tensor_add(sb["ev6"][:, :], pp[1][:, 0:512], sb["ev5"][:, :]))
            w(s, ve)
            qs.inc(s.dma_start(out=b.h_s[mt * 128:(mt + 1) * 128, sl],
                               in_=sb["ev6"][:, :]), 16)


def p6(b):
    t, v, a, g, s = b.t, b.v, b.a, b.g, b.s
    sb, pp, W, w = b.sb, b.pp, b.W, b.w
    pe, ve, ac, gp, qs, qa, qg, ccs = (b.pe, b.ve, b.ac, b.gp, b.qs, b.qa,
                                       b.qg, b.ccs)

    def dequant_t(wsrc, dstname):
        b.barrier()
        for it in range(NI):
            w(s, ve, ac, pe)
            qs.inc(s.dma_start(out=sb["dqs"][:, :], in_=wsrc[it * 128:(it + 1) * 128, :]), 16)
            w(v, qs)
            b.selfw(v, ve)
            ve.inc(v.tensor_reduce(sb["dqm"][:, 0:32],
                                   sb["dqs"][:, :].rearrange("p (bk e) -> p bk e", e=32),
                                   AX.X, ALU.max, apply_absolute_value=True))
            b.selfw(v, ve)
            ve.inc(v.tensor_scalar(sb["dqm"][:, 0:32], sb["dqm"][:, 0:32],
                                   1.0 / 7.0, 1e-12, ALU.mult, ALU.max))
            b.selfw(v, ve)
            ve.inc(v.reciprocal(sb["dqm"][:, 32:64], sb["dqm"][:, 0:32]))
            for bk in range(32):
                b.selfw(v, ve)
                ve.inc(v.tensor_scalar(sb["dqs"][:, bk * 32:(bk + 1) * 32],
                                       sb["dqs"][:, bk * 32:(bk + 1) * 32],
                                       sb["dqm"][:, 32 + bk:33 + bk], MAGIC,
                                       ALU.mult, ALU.add))
                b.selfw(v, ve)
                ve.inc(v.tensor_scalar(sb["dqs"][:, bk * 32:(bk + 1) * 32],
                                       sb["dqs"][:, bk * 32:(bk + 1) * 32],
                                       MAGIC, sb["dqm"][:, bk:bk + 1],
                                       ALU.subtract, ALU.mult))
            w(a, ve)
            b.selfw(a, ac)
            ac.inc(a.copy(sb["dqb"][:, :], sb["dqs"][:, :]))
            for kt in range(NDT):
                bank = b.pb[kt % 2]
                w(t, ac, ve)
                t.wait_ge(pe.sem, pe.n)
                pe.inc(t.transpose(bank[:, 0:128], sb["dqb"][:, kt * 128:(kt + 1) * 128],
                                   sb["ident"][:, :]))
                if kt % 2 == 0:
                    w(a, pe)
                    b.selfw(a, ac)
                    ac.inc(a.copy(
                        sb[dstname][:, kt * 3072 + it * 128: kt * 3072 + it * 128 + 128],
                        bank[:, 0:128]))
                else:
                    w(v, pe)
                    b.selfw(v, ve)
                    ve.inc(v.tensor_copy(
                        sb[dstname][:, kt * 3072 + it * 128: kt * 3072 + it * 128 + 128],
                        bank[:, 0:128]))

    dequant_t(W["wu_f"], "wut")
    dequant_t(W["wg_f"], "wgt")

    b.barrier()
    amax_col = [33, 34]
    w(v, qs)
    b.selfw(v, ve)
    ve.inc(v.memset(sb["amax"][:, 33:35], 0.0))
    for tt in range(NTT):
        w(s, pe, ve, ac)
        qs.inc(s.dma_start(out=sb["gxs"][:, :],
                           in_=b.hview(b.h_xn)[:, :, tt * 512:(tt + 1) * 512]), 16)
        for dd in range(NDT):
            w(v, qs)
            b.selfw(v, ve)
            ve.inc(v.tensor_scalar(sb["ev7"][:, :],
                                   sb["gxs"][:, :].rearrange("p (dt c) -> p dt c", dt=NDT)[:, dd, :],
                                   sb["scb"][:, 1:2], MAGIC, ALU.mult, ALU.add))
            b.selfw(v, ve)
            ve.inc(v.tensor_scalar(sb["xq_t"][:, dd * 512:(dd + 1) * 512],
                                   sb["ev7"][:, :], MAGIC, sb["scb"][:, 0:1],
                                   ALU.subtract, ALU.mult))
        w(t, ve)
        fneed = {}
        lastb = {}
        for which, wname, hdst, acol in (("up", "wut", b.h_up, 33), ("ga", "wgt", b.h_ga, 34)):
            for it in range(NI):
                bA = (it % 2) * 2
                w(t, qs)
                if bA in fneed:
                    t.wait_ge(ve.sem, fneed[bA][0])
                    t.wait_ge(ac.sem, fneed[bA][1])
                for kt in range(NDT):
                    bk = bA + kt % 2
                    if bk in lastb:
                        t.wait_ge(pe.sem, lastb[bk])
                    lastb[bk] = pe.inc(t.matmul(
                        pp[bk][:, 0:512],
                        sb[wname][:, kt * 3072 + it * 128: kt * 3072 + it * 128 + 128],
                        sb["xq_t"][:, kt * 512:(kt + 1) * 512],
                        start=kt < 2, stop=kt >= 6))
                w(a, pe, ve)
                b.selfw(a, ac)
                ac.inc(a.copy(sb["ev7"][:, :], pp[bA][:, 0:512]))
                w(v, ac, pe, qs)
                b.selfw(v, ve)
                ve.inc(v.tensor_add(sb["ev8"][:, :], pp[bA + 1][:, 0:512], sb["ev7"][:, :]))
                w(s, ve)
                qs.inc(s.dma_start(out=hdst[it * 128:(it + 1) * 128, tt * 512:(tt + 1) * 512],
                                   in_=sb["ev8"][:, :]), 16)
                b.selfw(v, ve)
                ve.inc(v.tensor_reduce(sb["amax"][:, 40:41], sb["ev8"][:, :], AX.X,
                                       ALU.max, apply_absolute_value=True))
                b.selfw(v, ve)
                ve.inc(v.tensor_max(sb["amax"][:, acol:acol + 1],
                                    sb["amax"][:, acol:acol + 1], sb["amax"][:, 40:41]))
                fneed[bA] = (ve.n, ac.n)
                fneed[bA + 1] = fneed[bA]

    def fold_and_ar(cols, aridx, scslots):
        for i, cn in enumerate(cols):
            w(t, ve, ac)
            t.wait_ge(pe.sem, pe.n)
            pe.inc(t.transpose(pp[2][0:1, 0:128], sb["amax"][:, cn:cn + 1],
                               sb["identf"][:, :]))
            w(v, pe)
            b.selfw(v, ve)
            ve.inc(v.tensor_reduce(sb["sc"][0:1, 10 + i:11 + i], pp[2][0:1, 0:128],
                                   AX.X, ALU.max))
        w(g, ve)
        b.selfw(g, gp)
        qg.inc(g.dma_start(out=b.ar_in[aridx].ap()[0:1, 0:2],
                           in_=sb["sc"][0:1, 10:12]), 16)
        w(g, qg)
        ccs.inc(g.collective_compute("AllReduce", ALU.max,
                                     replica_groups=[list(range(8))],
                                     ins=[b.ar_in[aridx].ap()],
                                     outs=[b.ar_out[aridx].ap()]))
        w(g, ccs)
        qg.inc(g.dma_start(out=sb["sc"][0:1, 12:14], in_=b.ar_out[aridx].ap()[0:1, 0:2]), 16)
        w(v, qg)
        for i, (scol, icol) in enumerate(scslots):
            b.selfw(v, ve)
            ve.inc(v.tensor_scalar(sb["sc"][0:1, 14 + i:15 + i], sb["sc"][0:1, 12 + i:13 + i],
                                   1.0 / 32767.0, 1e-12, ALU.mult, ALU.max))
            b.selfw(v, ve)
            ve.inc(v.reciprocal(sb["sc"][0:1, 16 + i:17 + i], sb["sc"][0:1, 14 + i:15 + i]))
        w(t, ve)
        t.wait_ge(pe.sem, pe.n)
        pe.inc(t.matmul(pp[3][:, 0:4], sb["onesf"][0:1, 0:128], sb["sc"][0:1, 14:18],
                        start=True, stop=True))
        w(v, pe)
        for i, (scol, icol) in enumerate(scslots):
            b.selfw(v, ve)
            ve.inc(v.tensor_copy(sb["scb"][:, scol:scol + 1], pp[3][:, i:i + 1]))
            b.selfw(v, ve)
            ve.inc(v.tensor_copy(sb["scb"][:, icol:icol + 1], pp[3][:, 2 + i:3 + i]))

    fold_and_ar([33, 34], 1, [(2, 3), (4, 5)])
    b.selfw(v, ve)
    ve.inc(v.tensor_scalar(sb["scb"][:, 10:11], sb["scb"][:, 4:5], -MAGIC, None, ALU.mult))
    b.selfw(v, ve)
    ve.inc(v.tensor_scalar(sb["scb"][:, 12:13], sb["scb"][:, 2:3], -MAGIC, None, ALU.mult))

    b.barrier()
    b.selfw(v, ve)
    ve.inc(v.memset(sb["amax"][:, 35:36], 0.0))
    for it in range(NI):
        w(s, ve, ac)
        qs.inc(s.dma_start(out=sb["ffa"][:, :], in_=b.h_ga[it * 128:(it + 1) * 128, :]), 16)
        w(a, qs)
        b.selfw(a, ac)
        ac.inc(a.activation(sb["ffb"][:, :], sb["ffa"][:, :], AF.Copy,
                            scale=sb["scb"][:, 5:6], bias=float(MAGIC)))
        w(v, ac)
        b.selfw(v, ve)
        ve.inc(v.tensor_scalar(sb["ffb"][:, :], sb["ffb"][:, :], MAGIC,
                               sb["scb"][:, 4:5], ALU.subtract, ALU.mult))
        w(a, ve)
        b.selfw(a, ac)
        ac.inc(a.activation(sb["ffc"][:, :], sb["ffb"][:, :], AF.Sigmoid))
        w(v, ac)
        b.selfw(v, ve)
        ve.inc(v.tensor_scalar(sb["ffc"][:, :], sb["ffc"][:, :], 65536.0, MAGIC,
                               ALU.mult, ALU.add))
        b.selfw(v, ve)
        ve.inc(v.tensor_scalar(sb["ffc"][:, :], sb["ffc"][:, :], MAGIC, 65535.0,
                               ALU.subtract, ALU.min))
        b.selfw(v, ve)
        ve.inc(v.scalar_tensor_tensor(sb["ffa"][:, :], sb["ffc"][:, :], float(2.0 ** -16),
                                      sb["ffb"][:, :], ALU.mult, ALU.mult))
        w(s, ve)
        qs.inc(s.dma_start(out=b.h_t[it * 128:(it + 1) * 128, :], in_=sb["ffa"][:, :]), 16)
        b.selfw(v, ve)
        ve.inc(v.tensor_reduce(sb["amax"][:, 40:41], sb["ffa"][:, :], AX.X,
                               ALU.max, apply_absolute_value=True))
        b.selfw(v, ve)
        ve.inc(v.tensor_max(sb["amax"][:, 35:36], sb["amax"][:, 35:36],
                            sb["amax"][:, 40:41]))
    fold_and_ar([35], 2, [(6, 7)])
    b.selfw(v, ve)
    ve.inc(v.tensor_scalar(sb["scb"][:, 11:12], sb["scb"][:, 6:7], -MAGIC, None, ALU.mult))

    b.barrier()
    b.selfw(v, ve)
    ve.inc(v.memset(sb["amax"][:, 36:37], 0.0))
    for it in range(NI):
        w(s, ve, ac)
        qs.inc(s.dma_start(out=sb["ffa"][:, :], in_=b.h_t[it * 128:(it + 1) * 128, :]), 16)
        qs.inc(s.dma_start(out=sb["ffb"][:, :], in_=b.h_up[it * 128:(it + 1) * 128, :]), 16)
        w(v, qs)
        b.selfw(v, ve)
        ve.inc(v.tensor_scalar(sb["ffa"][:, :], sb["ffa"][:, :], sb["scb"][:, 7:8],
                               MAGIC, ALU.mult, ALU.add))
        b.selfw(v, ve)
        ve.inc(v.tensor_scalar(sb["ffa"][:, :], sb["ffa"][:, :], MAGIC, sb["scb"][:, 6:7],
                               ALU.subtract, ALU.mult))
        b.selfw(v, ve)
        ve.inc(v.tensor_scalar(sb["ffb"][:, :], sb["ffb"][:, :], sb["scb"][:, 3:4],
                               MAGIC, ALU.mult, ALU.add))
        b.selfw(v, ve)
        ve.inc(v.tensor_scalar(sb["ffb"][:, :], sb["ffb"][:, :], MAGIC, sb["scb"][:, 2:3],
                               ALU.subtract, ALU.mult))
        b.selfw(v, ve)
        ve.inc(v.tensor_mul(sb["ffc"][:, :], sb["ffa"][:, :], sb["ffb"][:, :]))
        w(s, ve)
        qs.inc(s.dma_start(out=b.h_ot[it * 128:(it + 1) * 128, :], in_=sb["ffc"][:, :]), 16)
        b.selfw(v, ve)
        ve.inc(v.tensor_reduce(sb["amax"][:, 40:41], sb["ffc"][:, :], AX.X,
                               ALU.max, apply_absolute_value=True))
        b.selfw(v, ve)
        ve.inc(v.tensor_max(sb["amax"][:, 36:37], sb["amax"][:, 36:37],
                            sb["amax"][:, 40:41]))
    fold_and_ar([36], 3, [(8, 9)])
    b.selfw(v, ve)
    ve.inc(v.tensor_scalar(sb["scb"][:, 13:14], sb["scb"][:, 8:9], -MAGIC, None, ALU.mult))

    b.barrier()
    for pt in range(NDT):
        w(s, ve, ac, pe)
        qs.inc(s.dma_start(out=sb["ffa"][:, 0:3072], in_=W["wd_f"][pt * 128:(pt + 1) * 128, :]), 16)
        w(v, qs)
        b.selfw(v, ve)
        ve.inc(v.tensor_reduce(sb["dqm"][:, 0:96] if False else sb["ffb"][:, 0:96],
                               sb["ffa"][:, 0:3072].rearrange("p (bk e) -> p bk e", e=32),
                               AX.X, ALU.max, apply_absolute_value=True))
        b.selfw(v, ve)
        ve.inc(v.tensor_scalar(sb["ffb"][:, 0:96], sb["ffb"][:, 0:96],
                               1.0 / 7.0, 1e-12, ALU.mult, ALU.max))
        b.selfw(v, ve)
        ve.inc(v.reciprocal(sb["ffb"][:, 96:192], sb["ffb"][:, 0:96]))
        for bk in range(96):
            b.selfw(v, ve)
            ve.inc(v.tensor_scalar(sb["ffa"][:, bk * 32:(bk + 1) * 32],
                                   sb["ffa"][:, bk * 32:(bk + 1) * 32],
                                   sb["ffb"][:, 96 + bk:97 + bk], MAGIC,
                                   ALU.mult, ALU.add))
            b.selfw(v, ve)
            ve.inc(v.tensor_scalar(sb["ffa"][:, bk * 32:(bk + 1) * 32],
                                   sb["ffa"][:, bk * 32:(bk + 1) * 32],
                                   MAGIC, sb["ffb"][:, bk:bk + 1],
                                   ALU.subtract, ALU.mult))
        w(a, ve)
        b.selfw(a, ac)
        ac.inc(a.copy(sb["ffc"][:, 0:3072], sb["ffa"][:, 0:3072]))
        for c3 in range(3):
            w(v, ac)
            b.selfw(v, ve)
            ve.inc(v.tensor_copy(sb["dqb"][:, :], sb["ffc"][:, c3 * 1024:(c3 + 1) * 1024]))
            for kt in range(8):
                it = c3 * 8 + kt
                bank = b.pb[kt % 2]
                w(t, ve, ac)
                t.wait_ge(pe.sem, pe.n)
                pe.inc(t.transpose(bank[:, 0:128], sb["dqb"][:, kt * 128:(kt + 1) * 128],
                                   sb["ident"][:, :]))
                w(v, pe)
                b.selfw(v, ve)
                ve.inc(v.tensor_copy(
                    sb["wdt"][:, it * 1024 + pt * 128: it * 1024 + pt * 128 + 128],
                    bank[:, 0:128]))
    for tt in range(NTT):
        sl = slice(tt * 512, (tt + 1) * 512)
        for blk in range(4):
            w(s, ve, ac)
            qs.inc(s.dma_start(out=sb["x_t"][:, :],
                               in_=b.dx[tt * 512 + blk * 128: tt * 512 + (blk + 1) * 128, :]), 16)
            for dd in range(NDT):
                bank = pp[2 + dd % 2]
                w(t, qs, ve, ac)
                t.wait_ge(pe.sem, pe.n)
                pe.inc(t.transpose(bank[:, 0:128], sb["x_t"][:, dd * 128:(dd + 1) * 128],
                                   sb["identf"][:, :]))
                w(v, pe)
                b.selfw(v, ve)
                ve.inc(v.tensor_copy(
                    sb["xpt"][:, dd * 512 + blk * 128: dd * 512 + blk * 128 + 128],
                    bank[:, 0:128]))
        lastd = {}
        for half in range(2):
            for it in range(NI):
                w(s, ve, ac)
                qs.inc(s.dma_start(out=sb["sgt"][:, :], in_=b.h_ot[it * 128:(it + 1) * 128, sl]), 16)
                w(a, qs, pe)
                b.selfw(a, ac)
                ac.inc(a.activation(sb["sgt"][:, :], sb["sgt"][:, :], AF.Copy,
                                    scale=sb["scb"][:, 9:10], bias=float(MAGIC)))
                w(v, ac, pe)
                b.selfw(v, ve)
                ve.inc(v.tensor_scalar(sb["o2t"][:, :], sb["sgt"][:, :], MAGIC,
                                       sb["scb"][:, 8:9], ALU.subtract, ALU.mult))
                w(t, ve)
                if it == 0:
                    w(t, ac)
                for mi in range(4):
                    mt = half * 4 + mi
                    bank = pp[mi]
                    if mi in lastd:
                        t.wait_ge(pe.sem, lastd[mi])
                    lastd[mi] = pe.inc(t.matmul(
                        bank[:, 0:512],
                        sb["wdt"][:, it * 1024 + mt * 128: it * 1024 + mt * 128 + 128],
                        sb["o2t"][:, :],
                        start=it == 0, stop=it == NI - 1))
            for mi in range(4):
                mt = half * 4 + mi
                w(s, ve, ac)
                qs.inc(s.dma_start(out=sb["sgt"][:, :], in_=b.h_s[mt * 128:(mt + 1) * 128, sl]), 16)
                w(a, pe, qs)
                b.selfw(a, ac)
                ac.inc(a.activation(sb["gt"][:, :], pp[mi][:, 0:512], AF.Silu))
                w(v, ac, qs)
                b.selfw(v, ve)
                ve.inc(v.tensor_mul(sb["gt"][:, :], sb["gt"][:, :], sb["sgt"][:, :]))
                b.selfw(v, ve)
                ve.inc(v.tensor_add(sb["outf"][:, mt * 512:(mt + 1) * 512],
                                    sb["gt"][:, :], sb["xpt"][:, mt * 512:(mt + 1) * 512]))
        for blk in range(4):
            for dd in range(NDT):
                bank = pp[2 + dd % 2]
                w(t, ve, ac)
                t.wait_ge(pe.sem, pe.n)
                pe.inc(t.transpose(bank[:, 0:128],
                                   sb["outf"][:, dd * 512 + blk * 128: dd * 512 + blk * 128 + 128],
                                   sb["identf"][:, :]))
                if dd % 2 == 0:
                    w(a, pe, qs)
                    b.selfw(a, ac)
                    ac.inc(a.copy(sb["obuf"][:, dd * 128:(dd + 1) * 128], bank[:, 0:128]))
                else:
                    w(v, pe, qs)
                    b.selfw(v, ve)
                    ve.inc(v.tensor_copy(sb["obuf"][:, dd * 128:(dd + 1) * 128], bank[:, 0:128]))
            w(s, ve, ac)
            qs.inc(s.dma_start(
                out=b.dy[tt * 512 + blk * 128: tt * 512 + (blk + 1) * 128, :],
                in_=sb["obuf"][:, 0:1024]), 16)



def fixup_bias_note():
    pass


def finish(b):
    w = b.w
    b.barrier()
    b.g.wait_ge(b.qs.sem, b.qs.n)
    b.g.wait_ge(b.qa.sem, b.qa.n)
    b.g.wait_ge(b.qg.sem, b.qg.n)


def build(dump=None):
    b = mkbuilder(dump)
    p0(b)
    p1(b)
    p3(b)
    if dump == "xn":
        dbg_dump(b, b.h_xn, 0)
    elif dump == "q":
        dbg_dump_sb(b, "q", 0)
    elif dump == "v":
        dbg_dump(b, b.h_v, 0)
    elif dump == "S1":
        p4(b, nblk=1)
        sb = b.sb
        b.barrier()
        b.selfw(b.v, b.ve)
        b.ve.inc(b.v.memset(sb["ffb"][:, :], 0.0))
        b.selfw(b.v, b.ve)
        b.ve.inc(b.v.tensor_copy(sb["ffb"][:, 0:2048], sb["S_f"][:, :]))
        b.w(b.s, b.ve)
        b.qs.inc(b.s.dma_start(out=b.d_dbg, in_=sb["ffb"][:, :]), 16)
    else:
        p4(b)
        if dump == "delta":
            dbg_dump(b, b.h_df, 0)
        elif dump in ("fir", "acc0", "accA", "accV", "stats0", "stats", "gate"):
            p5(b, upto=dump)
            sb = b.sb
            if dump == "fir":
                dbg_dump(b, b.h_sh, 0)
            else:
                buf = "ffa" if dump == "gate" else "ffb"
                b.barrier()
                b.selfw(b.v, b.ve)
                b.ve.inc(b.v.memset(sb[buf][:, :], 0.0))
                b.selfw(b.v, b.ve)
                if dump in ("acc0", "accA", "accV"):
                    src = "maxP" if dump == "accV" else "sumP"
                    b.ve.inc(b.v.tensor_copy(sb[buf][0:64, 0:512], sb[src][0:64, :]))
                elif dump in ("stats", "stats0"):
                    b.ve.inc(b.v.tensor_copy(sb[buf][0:64, :], sb["gstats"][:, :]))
                else:
                    b.ve.inc(b.v.tensor_copy(sb[buf][0:16, :], sb["p_fm"][0:16, :]))
                b.w(b.s, b.ve)
                b.qs.inc(b.s.dma_start(out=b.d_dbg, in_=sb[buf][:, :]), 16)
        else:
            p5(b)
            if dump == "s":
                dbg_dump_f32(b, b.h_s, 0)
            else:
                p6(b)
    finish(b)
    nc, t, v, a, g, s = b.nc, b.t, b.v, b.a, b.g, b.s
    with nc.Block() as block:
        @block.sync
        def _(eng):
            s.replay(eng)

        @block.tensor
        def _(eng):
            t.replay(eng)

        @block.vector
        def _(eng):
            v.replay(eng)

        @block.scalar
        def _(eng):
            a.replay(eng)

        @block.gpsimd
        def _(eng):
            g.replay(eng)
    return b


def dbg_dump(b, hsrc, row):
    sb, w = b.sb, b.w
    b.barrier()
    b.qs.inc(b.s.dma_start(out=sb["vrow"][:, :], in_=hsrc[row * 128:(row + 1) * 128, :]), 16)
    w(b.v, b.qs)
    b.selfw(b.v, b.ve)
    b.ve.inc(b.v.tensor_copy(sb["ffb"][:, :], sb["vrow"][:, :]))
    w(b.s, b.ve)
    b.qs.inc(b.s.dma_start(out=b.d_dbg, in_=sb["ffb"][:, :]), 16)


def dbg_dump_sb(b, name, dd):
    sb, w = b.sb, b.w
    b.barrier()
    b.selfw(b.v, b.ve)
    b.ve.inc(b.v.tensor_copy(sb["ffb"][:, :], sb[name][:, dd * T:(dd + 1) * T]))
    w(b.s, b.ve)
    b.qs.inc(b.s.dma_start(out=b.d_dbg, in_=sb["ffb"][:, :]), 16)


def dbg_dump_f32(b, hsrc, row):
    sb, w = b.sb, b.w
    b.barrier()
    b.qs.inc(b.s.dma_start(out=sb["ffa"][:, :], in_=hsrc[row * 128:(row + 1) * 128, :]), 16)
    w(b.s, b.qs)
    b.qs.inc(b.s.dma_start(out=b.d_dbg, in_=sb["ffa"][:, :]), 16)


def inputs_to_maps(inputs):
    ins = {k: np.ascontiguousarray(np.asarray(vv, np.float32))
           for k, vv in inputs.items()}
    nmap = dict(
        wq="L0_attn_Wq", wk="L0_attn_Wk", wv="L0_attn_Wv", wb="L0_attn_Wb",
        conv_q="L0_attn_conv_q", conv_k="L0_attn_conv_k", conv_v="L0_attn_conv_v",
        w1="L0_attn_W1", w2="L0_attn_W2", wo="L0_attn_Wo",
        wg_f="L0_ffn_w_gate", wu_f="L0_ffn_w_up", wd_f="L0_ffn_w_down")
    base = {k: ins[v2] for k, v2 in nmap.items()}
    base["forget"] = ins["L0_attn_forget_param"].reshape(1, H)
    base["log_temp"] = ins["L0_attn_log_temp"].reshape(1, H)
    base["floor"] = np.ascontiguousarray(ins["L0_attn_floor_param"].reshape(H, 4))
    base["b1"] = ins["L0_attn_b1"].reshape(1, 2048)
    base["b2"] = ins["L0_attn_b2"].reshape(1, 16)
    base["norm_w"] = ins["L0_attn_norm_w"].reshape(1, DV)
    base["fir_s"] = np.ascontiguousarray(ins["L0_attn_fir_short_w"].reshape(D, 3))
    base["fir_l"] = np.ascontiguousarray(ins["L0_attn_fir_long_w"].reshape(D, 63))
    x = ins["L0_attn_hidden_states"]
    return [dict(base, x=np.ascontiguousarray(x[i])) for i in range(8)]


def kernel(**inputs):
    b = build()
    maps = inputs_to_maps(inputs)
    res = run_bass_kernel_spmd(b.nc, maps, core_ids=list(range(8)))
    return np.stack([r["y"] for r in res.results], axis=0)


# version 04
# speedup vs baseline: 1.0080x; 1.0080x over previous
import numpy as np
import ml_dtypes
import concourse.bass as bass
import concourse.mybir as mybir
from bass_rust import ReduceOp
from concourse.bass_utils import run_bass_kernel_spmd

F32 = mybir.dt.float32
BF16 = mybir.dt.bfloat16
AF = mybir.ActivationFunctionType
ALU = mybir.AluOpType
AX = mybir.AxisListType

T, D, H, DK, DV, C = 4096, 1024, 4, 256, 256, 32
NCH, NDT, NBLK, NTT = 128, 8, 32, 8
MAGIC = float(np.float32(1.5 * 2.0 ** 23))
FORGET_MIN, MAX_FLOOR = 0.55, 0.075
NI = 24
KB = 1024


class Cnt:
    def __init__(self, es, nc, name):
        self.sem = es.enter_context(nc.semaphore(name))
        self.n = 0

    def inc(self, tok, k=1):
        tok.then_inc(self.sem, k)
        self.n += k
        return self.n


class Rec:
    def __init__(self):
        self.calls = []

    def __getattr__(self, name):
        if name.startswith("_"):
            raise AttributeError(name)
        calls = self.calls

        def call(*a, **k):
            rec = {"m": name, "a": a, "k": k, "incs": []}
            calls.append(rec)

            class Tok:
                def then_inc(self, sem, n=1):
                    rec["incs"].append((sem, n))
                    return self
            return Tok()
        return call

    def replay(self, eng):
        for idx, rec in enumerate(self.calls):
            try:
                instr = getattr(eng, rec["m"])(*rec["a"], **rec["k"])
            except Exception:
                import sys as _s
                print("REPLAY FAIL", rec["m"], "idx", idx,
                      [getattr(x, "tensor", None) and x.tensor.name or str(x)[:40]
                       for x in rec["a"] if hasattr(x, "tensor") or True][:4],
                      file=_s.stderr)
                raise
            for sem, n in rec["incs"]:
                instr.then_inc(sem, n)


class B:
    pass


def mkbuilder(dump):
    import contextlib
    b = B()
    b.dump = dump
    b.nc = nc = bass.Bass("TRN2", target_bir_lowering=False)
    b.es = es = contextlib.ExitStack()
    b.t, b.v, b.a, b.g, b.s = Rec(), Rec(), Rec(), Rec(), Rec()

    b.dx = nc.dram_tensor("x", [T, D], F32, kind="ExternalInput").ap()
    b.dy = nc.dram_tensor("y", [T, D], F32, kind="ExternalOutput").ap()
    b.W = {}
    for nm, shape in [
        ("wq", [D, D]), ("wk", [D, D]), ("wv", [D, D]), ("wb", [D, H]),
        ("forget", [1, H]), ("conv_q", [D, 4]), ("conv_k", [D, 4]),
        ("conv_v", [D, 4]), ("fir_s", [D, 3]), ("fir_l", [D, 63]),
        ("log_temp", [1, H]), ("floor", [H, 4]),
        ("w1", [1088, 2048]), ("b1", [1, 2048]), ("w2", [2048, 16]),
        ("b2", [1, 16]), ("norm_w", [1, DV]), ("wo", [D, D]),
        ("wg_f", [3072, D]), ("wu_f", [3072, D]), ("wd_f", [D, 3072]),
    ]:
        b.W[nm] = nc.dram_tensor(nm, shape, F32, kind="ExternalInput").ap()

    def dram2(name, rows, cols, dt):
        return nc.dram_tensor(name, [rows, cols], dt).ap()
    b.h_xn = dram2("h_xn", NDT * 128, T, BF16)
    b.h_v = dram2("h_v", NDT * 128, T, BF16)
    b.h_df = dram2("h_df", NDT * 128, T, BF16)
    b.h_sh = dram2("h_sh", NDT * 128, T, BF16)
    b.h_lo = dram2("h_lo", NDT * 128, T, BF16)
    b.h_s = dram2("h_s", NDT * 128, T, F32)
    b.h_up = dram2("h_up", NI * 128, T, F32)
    b.h_ga = dram2("h_ga", NI * 128, T, F32)
    b.h_t = dram2("h_t", NI * 128, T, F32)
    b.h_ot = dram2("h_ot", NI * 128, T, F32)
    b.ar_in = [nc.dram_tensor(f"ar_in{i}", [1, 2], F32) for i in range(4)]
    b.ar_out = [nc.dram_tensor(f"ar_out{i}", [1, 2], F32) for i in range(4)]
    b.d_dbg = (nc.dram_tensor("dbg", [128, T], F32, kind="ExternalOutput").ap()
               if dump is not None else None)

    for nm in ["pe", "ve", "ac", "gp", "qs", "qa", "qg", "ccs"]:
        setattr(b, nm, Cnt(es, nc, nm))

    b.sb = {}
    BLOB = 176 * KB
    blob0, _ = nc.bump_sbuf(BLOB)
    b.AQ, b.AK, b.R2 = blob0, blob0 + 64 * KB, blob0 + 128 * KB

    def At(name, shape, dt, off):
        b.sb[name] = nc.alloc_sbuf_tensor_at("sb_" + name, shape, dt, offset=off)
        return b.sb[name]

    def S(name, shape, dt=BF16):
        b.sb[name] = nc.alloc_sbuf_tensor("sb_" + name, shape, dt)
        return b.sb[name]
    b.At, b.S = At, S

    S("ones", [128, 128], BF16); S("onesf", [128, 128], F32)
    S("ident", [128, 128], BF16); S("identf", [128, 128], F32)
    S("sel4", [128, 32], BF16)
    S("selh", [4, 4 * 128], BF16)
    S("selhf", [4, 4 * 128], F32)
    S("negselh", [4, 4 * 128], BF16)
    S("selh16", [16, 16 * 128], BF16)
    S("hsel", [128, 32], BF16)
    S("sumsel", [16, 16], BF16)
    S("sumself", [16, 16], F32)
    S("mask_bdsl", [128, 128], BF16)
    S("mask_triu", [128, 128], BF16)
    S("convw", [128, 128], F32)
    S("firs", [128, 32], F32)
    S("firl", [128, 512], F32)
    S("normw", [128, 2], F32)
    S("b1t", [128, 16], F32)
    S("misc", [4, 64], F32)
    S("lam", [128, 8], F32)
    S("sc", [1, 64], F32)
    S("scb", [128, 24], F32)
    S("lns", [128, 16], F32)
    S("amax", [128, 48], F32)
    S("wb_w", [128, 32], BF16)
    S("w2w", [128, 256], BF16)
    S("rows1", [128, T], BF16)
    S("gstats", [64, T], BF16)
    S("p16", [16, 16], F32)

    At("q", [128, NDT * T], BF16, b.AQ)
    At("k", [128, NDT * T], BF16, b.AK)
    R2 = b.R2
    At("x_t", [128, D], F32, R2)
    At("xn_t", [128, D], BF16, R2 + 4 * KB)
    At("sq_t", [128, D], BF16, R2 + 6 * KB)
    At("xfm", [128, NDT * 512], BF16, R2 + 8 * KB)
    At("xfm2", [128, NDT * 512], BF16, R2 + 16 * KB)
    At("xs", [128, NDT * 512], BF16, R2)
    At("pre", [128, NDT * 520], BF16, R2 + 8 * KB)
    At("wbuf", [128, NDT * D], BF16, R2 + 17 * KB)
    At("sqrow", [128, T], BF16, R2 + 33 * KB)
    At("accA", [128, 512], BF16, R2 + 41 * KB)
    At("accB", [128, 512], BF16, R2 + 42 * KB)
    At("ev1", [128, 512], F32, R2 + 43 * KB)
    At("ev2", [128, 512], F32, R2 + 45 * KB)
    At("S_f", [128, 2 * 1024], F32, R2)
    At("S_b", [128, 2 * 1024], BF16, R2 + 8 * KB)
    At("ktm", [128, 1024], BF16, R2 + 12 * KB)
    At("vtm", [128, 1024], BF16, R2 + 14 * KB)
    At("kbn", [128, 1024], BF16, R2 + 16 * KB)
    At("vblk", [128, 1024], BF16, R2 + 18 * KB)
    At("o_tm", [128, 1024], BF16, R2 + 20 * KB)
    At("z_sb", [128, 1024], BF16, R2 + 22 * KB)
    At("u_sb", [128, 1024], BF16, R2 + 24 * KB)
    At("ttf", [128, 128], BF16, R2 + 26 * KB)
    At("attf", [128, 128], BF16, R2 + 27 * KB)
    At("btm", [128, 8], F32, R2 + 28 * KB)
    At("dfm", [128, NDT * 512], BF16, R2 + 30 * KB)
    At("tmA", [128, 128], BF16, R2 + 44 * KB)
    At("tmB", [128, 128], BF16, R2 + 44 * KB + 256)
    At("ttA", [128, 128], BF16, R2 + 44 * KB + 512)
    At("ttB", [128, 128], BF16, R2 + 44 * KB + 768)
    At("vrow", [128, T], BF16, R2)
    At("srow", [128, T], BF16, R2 + 8 * KB)
    At("lrow", [128, T], BF16, R2 + 16 * KB)
    At("scr", [128, 128], F32, R2 + 24 * KB)
    At("sumP", [128, 512], F32, b.AQ)
    At("sqP", [128, 512], F32, b.AQ + 2 * KB)
    At("maxP", [128, 512], F32, b.AQ + 4 * KB)
    At("statm", [128, 1024], F32, b.AQ + 6 * KB)
    At("sqcb", [128, 1024], BF16, b.AQ + 54 * KB)
    At("gxs", [128, NDT * 512], BF16, R2)
    At("hmid", [128, 16 * 512], BF16, R2 + 8 * KB)
    At("w1b", [128, 9 * 2048], BF16, b.AK)
    At("p_fm", [16, T], BF16, R2 + 24 * KB)
    At("e16", [16, 512], F32, R2 + 32 * KB)
    At("zb16", [16, 512], F32, R2 + 34 * KB)
    At("vtile", [128, NDT * 512], BF16, R2 + 32 * KB)
    At("dtile", [128, NDT * 512], BF16, R2 + 40 * KB)
    At("stile", [128, NDT * 512], BF16, b.AQ)
    At("ltile", [128, NDT * 512], BF16, b.AQ + 8 * KB)
    At("onrm", [128, NDT * 512], BF16, b.AQ + 16 * KB)
    At("wobuf", [128, NDT * D], BF16, b.AQ + 24 * KB)
    At("ocomb", [128, 512], F32, b.AQ + 40 * KB)
    At("ocm2", [128, 512], F32, b.AQ + 42 * KB)
    At("ocm3", [128, 512], F32, b.AQ + 44 * KB)
    At("ev5", [128, 512], F32, b.AQ + 46 * KB)
    At("ev6", [128, 512], F32, b.AQ + 48 * KB)
    At("obuf", [128, 1024], F32, b.AQ + 50 * KB)
    At("ev7", [128, 512], F32, R2 + 8 * KB)
    At("ev8", [128, 512], F32, R2 + 10 * KB)
    At("wut", [128, NDT * 3072], BF16, b.AQ)
    At("wgt", [128, NDT * 3072], BF16, b.AK)
    At("wdt", [128, NI * 1024], BF16, b.AK)
    At("xq_t", [128, NDT * 512], BF16, b.AQ + 48 * KB)
    At("dqs", [128, 1024], F32, b.AQ + 56 * KB)
    At("dqm", [128, 64], F32, b.AQ + 60 * KB)
    At("dqb", [128, 1024], BF16, b.AQ + 62 * KB)
    At("ffa", [128, T], F32, R2)
    At("ffb", [128, T], F32, R2 + 16 * KB)
    At("ffc", [128, T], F32, R2 + 32 * KB)
    At("o2t", [128, 512], BF16, b.AQ)
    At("sgt", [128, 512], F32, b.AQ + 1 * KB)
    At("gt", [128, 512], F32, b.AQ + 3 * KB)
    At("xpt", [128, NDT * 512], F32, b.AQ + 8 * KB)
    At("outf", [128, NDT * 512], F32, b.AQ + 24 * KB)

    b.pp = [nc.alloc_psum_tensor(f"pp{i}", [128, 512], F32) for i in range(6)]
    b.pb = [nc.alloc_psum_tensor(f"pb{i}", [128, 1024], BF16) for i in range(2)]

    def w(rec, *deps):
        for d in deps:
            if d.n > 0:
                rec.wait_ge(d.sem, d.n)
    b.w = w

    def selfw(rec, cnt):
        rec.wait_ge(cnt.sem, cnt.n)
    b.selfw = selfw

    def barrier():
        for rec in (b.t, b.v, b.a, b.g, b.s):
            w(rec, b.pe, b.ve, b.ac, b.gp, b.qs, b.qa, b.qg, b.ccs)
    b.barrier = barrier

    def hview(hbm):
        return hbm.rearrange("(dt p) t -> p dt t", p=128)
    b.hview = hview
    return b


def p0(b):
    t, v, a, g, s = b.t, b.v, b.a, b.g, b.s
    sb, pp, W, w = b.sb, b.pp, b.W, b.w
    pe, ve, ac, gp, qs, qg = b.pe, b.ve, b.ac, b.gp, b.qs, b.qg
    g.memset(sb["ones"][:, :], 1.0)
    g.memset(sb["onesf"][:, :], 1.0)
    g.memset(sb["amax"][:, :], 0.0)
    g.memset(sb["pre"][:, :], 0.0)
    gp.inc(g.memset(sb["lns"][:, :], 0.0))
    w(g, gp)
    g.affine_select(sb["identf"][:, :], sb["onesf"][:, :], pattern=[[-1, 128]],
                    compare_op=ALU.is_equal, fill=0.0, base=0,
                    channel_multiplier=1)
    bd4 = sb["mask_bdsl"][:, :].rearrange("p (hf cf) -> p hf cf", cf=32)
    on4 = sb["ones"][:, :].rearrange("p (hf cf) -> p hf cf", cf=32)
    gp.inc(g.affine_select(bd4, on4, pattern=[[-32, 4], [-1, 32]],
                           compare_op=ALU.is_ge, fill=0.0, base=-1,
                           channel_multiplier=1))
    w(g, gp)
    gp.inc(g.affine_select(bd4, bd4, pattern=[[-32, 4], [0, 32]],
                           compare_op=ALU.is_ge, fill=0.0, base=0,
                           channel_multiplier=1))
    w(g, gp)
    gp.inc(g.affine_select(bd4, bd4, pattern=[[32, 4], [0, 32]],
                           compare_op=ALU.is_ge, fill=0.0, base=31,
                           channel_multiplier=-1))
    w(g, gp)
    for r0 in range(0, 128, 32):
        mt4 = sb["mask_triu"][r0:r0 + 32, :].rearrange("p (hf cf) -> p hf cf", cf=32)
        on32 = sb["ones"][0:32, :].rearrange("p (hf cf) -> p hf cf", cf=32)
        gp.inc(g.affine_select(mt4, on32, pattern=[[0, 4], [1, 32]],
                               compare_op=ALU.is_ge, fill=0.0, base=0,
                               channel_multiplier=-1))
        w(g, gp)
    gp.inc(g.memset(sb["sel4"][:, :], 0.0))
    gp.inc(g.memset(sb["selh"][:, :], 0.0))
    gp.inc(g.memset(sb["selh16"][:, :], 0.0))
    gp.inc(g.memset(sb["hsel"][:, :], 0.0))
    gp.inc(g.memset(sb["sumsel"][:, :], 0.0))
    w(v, gp)
    ve.inc(v.tensor_copy(sb["ident"][:, :], sb["identf"][:, :]))
    w(g, ve)
    for h in range(H):
        qg.inc(g.dma_start(out=sb["sel4"][32 * h:32 * h + 32, :],
                           in_=sb["ident"][0:32, 0:32]), 16)
        qg.inc(g.dma_start(out=sb["selh"][h:h + 1, 128 * h:128 * h + 128],
                           in_=sb["ones"][0:1, 0:128]), 16)
        qg.inc(g.dma_start(out=sb["hsel"][0:128, h * 4 + h: h * 4 + h + 1],
                           in_=sb["ones"][0:128, 0:1]), 16)
    for j in range(16):
        qg.inc(g.dma_start(out=sb["selh16"][j:j + 1, 128 * j:128 * j + 128],
                           in_=sb["ones"][0:1, 0:128]), 16)
        h = j // 4
        qg.inc(g.dma_start(out=sb["sumsel"][4 * h:4 * h + 4, j:j + 1],
                           in_=sb["ones"][0:4, 0:1]), 16)

    w(v, qg)
    b.selfw(v, ve)
    ve.inc(v.tensor_copy(sb["sumself"][:, :], sb["sumsel"][:, :]))
    b.selfw(v, ve)
    ve.inc(v.tensor_copy(sb["selhf"][:, :], sb["selh"][:, :]))
    b.selfw(v, ve)
    ve.inc(v.tensor_scalar(sb["negselh"][:, :], sb["selh"][:, :], -1.0, None, ALU.mult))
    for j, nm in enumerate(["conv_q", "conv_k", "conv_v"]):
        for dd in range(NDT):
            qs.inc(s.dma_start(
                out=sb["convw"][0:128, dd * 16 + j * 4: dd * 16 + j * 4 + 4],
                in_=W[nm][dd * 128:(dd + 1) * 128, :]), 16)
    for dd in range(NDT):
        qs.inc(s.dma_start(out=sb["firs"][0:128, dd * 4: dd * 4 + 3],
                           in_=W["fir_s"][dd * 128:(dd + 1) * 128, :]), 16)
        qs.inc(s.dma_start(out=sb["firl"][0:128, dd * 64: dd * 64 + 63],
                           in_=W["fir_l"][dd * 128:(dd + 1) * 128, :]), 16)
    qs.inc(s.dma_start(out=sb["normw"][0:128, 0:1],
                       in_=W["norm_w"][0:1, 0:128].rearrange("a b -> b a")), 16)
    qs.inc(s.dma_start(out=sb["normw"][0:128, 1:2],
                       in_=W["norm_w"][0:1, 128:256].rearrange("a b -> b a")), 16)
    for mt in range(16):
        qs.inc(s.dma_start(
            out=sb["b1t"][0:128, mt:mt + 1],
            in_=W["b1"][0:1, mt * 128:(mt + 1) * 128].rearrange("a b -> b a")), 16)
    qs.inc(s.dma_start(out=sb["misc"][0:1, 0:4], in_=W["forget"]), 16)
    qs.inc(s.dma_start(out=sb["misc"][0:1, 4:8], in_=W["log_temp"]), 16)
    qs.inc(s.dma_start(out=sb["misc"][0:4, 8:12], in_=W["floor"]), 16)
    qs.inc(s.dma_start(out=sb["misc"][0:1, 12:28], in_=W["b2"]), 16)
    qg.inc(g.dma_start(out=sb["wb_w"][:, :].rearrange("p (kt h) -> p kt h", h=H),
                       in_=W["wb"].rearrange("(kt p) h -> p kt h", p=128)), 16)
    qg.inc(g.dma_start(out=sb["w2w"][:, :].rearrange("p (kt h) -> p kt h", h=16),
                       in_=W["w2"].rearrange("(kt p) h -> p kt h", p=128)), 16)
    w(a, qs)
    ac.inc(a.activation(sb["misc"][0:1, 32:36], sb["misc"][0:1, 0:4], AF.Sigmoid))
    w(v, ac)
    b.selfw(v, ve)
    ve.inc(v.tensor_scalar(sb["misc"][0:1, 32:36], sb["misc"][0:1, 32:36],
                           1.0 - FORGET_MIN, FORGET_MIN, ALU.mult, ALU.add))
    w(a, ve)
    b.selfw(a, ac)
    ac.inc(a.activation(sb["misc"][0:1, 36:40], sb["misc"][0:1, 4:8], AF.Exp))
    w(v, ac)
    b.selfw(v, ve)
    ve.inc(v.reciprocal(sb["misc"][0:1, 40:44], sb["misc"][0:1, 36:40]))
    w(a, ve)
    b.selfw(a, ac)
    ac.inc(a.activation(sb["misc"][0:4, 44:48], sb["misc"][0:4, 8:12], AF.Sigmoid))
    w(v, ac)
    b.selfw(v, ve)
    ve.inc(v.tensor_scalar(sb["misc"][0:4, 44:48], sb["misc"][0:4, 44:48],
                           MAX_FLOOR, None, ALU.mult))
    w(t, ve, gp)
    pe.inc(t.matmul(pp[2][:, 0:4], sb["onesf"][0:1, 0:128],
                    sb["misc"][0:1, 32:36], start=True, stop=True))
    w(v, pe)
    b.selfw(v, ve)
    ve.inc(v.tensor_copy(sb["lam"][:, 0:4], pp[2][:, 0:4]))
    w(t, ve)
    pe.inc(t.transpose(pp[3][0:16, 0:1], sb["misc"][0:1, 12:28], sb["identf"][0:1, 0:1]))
    w(v, pe)
    b.selfw(v, ve)
    ve.inc(v.tensor_copy(sb["p16"][0:16, 0:1], pp[3][0:16, 0:1]))
    w(g, ve, ac)
    b.selfw(g, gp)
    for h in range(4):
        for j in range(4):
            qg.inc(g.dma_start(out=sb["p16"][4 * h + j:4 * h + j + 1, 1:2],
                               in_=sb["misc"][0:1, 40 + h:41 + h]), 16)
            qg.inc(g.dma_start(out=sb["p16"][4 * h + j:4 * h + j + 1, 2:3],
                               in_=sb["misc"][h:h + 1, 44 + j:45 + j]), 16)


def p1(b):
    t, v, a, g, s = b.t, b.v, b.a, b.g, b.s
    sb, pp, w = b.sb, b.pp, b.w
    pe, ve, ac, gp, qs, qg, ccs = b.pe, b.ve, b.ac, b.gp, b.qs, b.qg, b.ccs
    for blk in range(NBLK):
        w(s, ac, ve)
        qs.inc(s.dma_start(out=sb["x_t"][:, :], in_=b.dx[blk * 128:(blk + 1) * 128, :]), 16)
        w(a, qs)
        b.selfw(a, ac)
        w(a, ve)
        ac.inc(a.activation(sb["sq_t"][:, :], sb["x_t"][:, :], AF.Square,
                            accum_out=sb["lns"][:, 0:1]))
        w(v, qs)
        b.selfw(v, ve)
        ve.inc(v.reduce_sum(sb["lns"][:, 1:2], sb["x_t"][:, :], AX.X))
        w(v, ac)
        b.selfw(v, ve)
        ve.inc(v.tensor_scalar(sb["lns"][:, 2:3], sb["lns"][:, 1:2], 1.0 / D, None, ALU.mult))
        ve.inc(v.tensor_scalar(sb["lns"][:, 3:4], sb["lns"][:, 0:1], 1.0 / D, None, ALU.mult))
        b.selfw(v, ve)
        ve.inc(v.tensor_mul(sb["lns"][:, 4:5], sb["lns"][:, 2:3], sb["lns"][:, 2:3]))
        b.selfw(v, ve)
        ve.inc(v.tensor_sub(sb["lns"][:, 5:6], sb["lns"][:, 3:4], sb["lns"][:, 4:5]))
        b.selfw(v, ve)
        ve.inc(v.tensor_scalar(sb["lns"][:, 6:7], sb["lns"][:, 5:6], 1e-5, None, ALU.add))
        b.selfw(v, ve)
        ve.inc(v.reciprocal(sb["lns"][:, 7:8], sb["lns"][:, 6:7]))
        w(a, ve)
        b.selfw(a, ac)
        ac.inc(a.activation(sb["lns"][:, 8:9], sb["lns"][:, 7:8], AF.Sqrt))
        w(v, ve, pe, ac)
        b.selfw(v, ve)
        ve.inc(v.tensor_scalar(sb["lns"][:, 10:11], sb["lns"][:, 2:3], -1.0,
                               None, ALU.mult))
        b.selfw(v, ve)
        ve.inc(v.tensor_scalar(sb["xn_t"][:, :], sb["x_t"][:, :],
                               sb["lns"][:, 10:11], sb["lns"][:, 8:9],
                               ALU.add, ALU.mult))
        b.selfw(v, ve)
        ve.inc(v.tensor_reduce(sb["amax"][:, blk:blk + 1], sb["xn_t"][:, :],
                               AX.X, ALU.max, apply_absolute_value=True))
        xf = sb["xfm"] if (blk // 4) % 2 == 0 else sb["xfm2"]
        for dd in range(NDT):
            bank = b.pb[dd % 2]
            w(t, ac, ve)
            pe.inc(t.transpose(bank[:, 0:128], sb["xn_t"][:, dd * 128:(dd + 1) * 128],
                               sb["ident"][:, :]))
            if dd % 2 == 0:
                w(a, pe, qs)
                b.selfw(a, ac)
                ac.inc(a.copy(
                    xf[:, dd * 512 + (blk % 4) * 128: dd * 512 + (blk % 4) * 128 + 128],
                    bank[:, 0:128]))
            else:
                w(v, pe, qs)
                b.selfw(v, ve)
                ve.inc(v.tensor_copy(
                    xf[:, dd * 512 + (blk % 4) * 128: dd * 512 + (blk % 4) * 128 + 128],
                    bank[:, 0:128]))
        if blk % 4 == 3:
            tt = blk // 4
            for dd in range(NDT):
                w(s, ve, ac)
                qs.inc(s.dma_start(
                    out=b.h_xn[dd * 128:(dd + 1) * 128, tt * 512:(tt + 1) * 512],
                    in_=xf[:, dd * 512:(dd + 1) * 512]), 16)
    b.selfw(v, ve)
    ve.inc(v.tensor_reduce(sb["amax"][:, 32:33], sb["amax"][:, 0:32], AX.X, ALU.max))
    w(t, ve)
    pe.inc(t.transpose(pp[2][0:1, 0:128], sb["amax"][:, 32:33], sb["identf"][:, :]))
    w(v, pe)
    b.selfw(v, ve)
    ve.inc(v.tensor_reduce(sb["sc"][0:1, 1:2], pp[2][0:1, 0:128], AX.X, ALU.max))
    b.selfw(v, ve)
    ve.inc(v.tensor_copy(sb["sc"][0:1, 2:3], sb["sc"][0:1, 1:2]))
    w(g, ve)
    qg.inc(g.dma_start(out=b.ar_in[0].ap()[0:1, 0:2], in_=sb["sc"][0:1, 1:3]), 16)
    w(g, qg)
    ccs.inc(g.collective_compute("AllReduce", ALU.max,
                                 replica_groups=[list(range(8))],
                                 ins=[b.ar_in[0].ap()], outs=[b.ar_out[0].ap()]))
    w(g, ccs)
    qg.inc(g.dma_start(out=sb["sc"][0:1, 4:6], in_=b.ar_out[0].ap()[0:1, 0:2]), 16)
    w(v, qg)
    b.selfw(v, ve)
    ve.inc(v.tensor_scalar(sb["sc"][0:1, 6:7], sb["sc"][0:1, 4:5],
                           1.0 / 32767.0, 1e-12, ALU.mult, ALU.max))
    b.selfw(v, ve)
    ve.inc(v.reciprocal(sb["sc"][0:1, 7:8], sb["sc"][0:1, 6:7]))
    w(t, ve)
    pe.inc(t.matmul(pp[3][:, 0:2], sb["onesf"][0:1, 0:128], sb["sc"][0:1, 6:8],
                    start=True, stop=True))
    w(v, pe)
    b.selfw(v, ve)
    ve.inc(v.tensor_copy(sb["scb"][:, 0:2], pp[3][:, 0:2]))


def p3(b):
    t, v, a, g, s = b.t, b.v, b.a, b.g, b.s
    sb, pp, W, w = b.sb, b.pp, b.W, b.w
    pe, ve, ac, gp, qs, qa, qg = b.pe, b.ve, b.ac, b.gp, b.qs, b.qa, b.qg

    def sweep(wname, convj, dest):
        b.barrier()
        qg.inc(g.dma_start(out=sb["wbuf"][:, :].rearrange("p (kt c) -> p kt c", c=D),
                           in_=W[wname].rearrange("(kt p) c -> p kt c", p=128)), 16)
        w(g, ve, ac, pe, qs, qa)
        b.selfw(g, gp)
        gp.inc(g.memset(sb["pre"][:, :], 0.0))
        w(v, gp)
        w(a, gp)
        w(t, gp)
        bankseq = {}
        need = {}
        for tt in range(NTT):
            w(s, pe)
            qs.inc(s.dma_start(out=sb["xs"][:, :],
                               in_=b.hview(b.h_xn)[:, :, tt * 512:(tt + 1) * 512]), 16)
            for mt in range(NDT):
                bA = (mt % 2) * 2
                w(t, qs, qg)
                if bA in need:
                    t.wait_ge(ve.sem, need[bA][0])
                    t.wait_ge(ac.sem, need[bA][1])
                for kt in range(NDT):
                    bk = bA + kt % 2
                    if bk in bankseq:
                        t.wait_ge(pe.sem, bankseq[bk])
                    tok = t.matmul(
                        pp[bk][:, 0:512],
                        sb["wbuf"][:, kt * D + mt * 128: kt * D + mt * 128 + 128],
                        sb["xs"][:, :].rearrange("p (dt c) -> p dt c", dt=NDT)[:, kt, :],
                        start=kt < 2, stop=kt >= 6)
                    bankseq[bk] = b.pe.inc(tok)
                base = mt * 520
                bA, bB = (mt % 2) * 2, (mt % 2) * 2 + 1
                w(v, ac)
                b.selfw(v, ve)
                ve.inc(v.tensor_copy(sb["pre"][:, base + 0: base + 3],
                                     sb["pre"][:, base + 512: base + 515]))
                w(a, pe, ve)
                b.selfw(a, ac)
                ac.inc(a.copy(sb["ev1"][:, :], pp[bA][:, 0:512]))
                w(v, ac, pe)
                b.selfw(v, ve)
                ve.inc(v.tensor_add(sb["pre"][:, base + 3: base + 515],
                                    pp[bB][:, 0:512], sb["ev1"][:, :]))
                need[bA] = (ve.n, ac.n)
                need[bB] = (ve.n, ac.n)
                cw = sb["convw"]
                w(v, qa)
                b.selfw(v, ve)
                ve.inc(v.tensor_scalar(
                    sb["accA"][:, :], sb["pre"][:, base + 0: base + 512],
                    cw[:, mt * 16 + convj * 4 + 0: mt * 16 + convj * 4 + 1],
                    None, ALU.mult))
                for tap in range(1, 4):
                    src, dst = ("accA", "accB") if tap % 2 == 1 else ("accB", "accA")
                    b.selfw(v, ve)
                    ve.inc(v.scalar_tensor_tensor(
                        sb[dst][:, :], sb["pre"][:, base + tap: base + tap + 512],
                        cw[:, mt * 16 + convj * 4 + tap: mt * 16 + convj * 4 + tap + 1],
                        sb[src][:, :], ALU.mult, ALU.add))
                w(a, ve)
                b.selfw(a, ac)
                if dest == "v":
                    w(a, qa)
                    ac.inc(a.activation(sb["accA"][:, :], sb["accB"][:, :], AF.Silu))
                    w(a, ac)
                    qa.inc(a.dma_start(
                        out=b.h_v[mt * 128:(mt + 1) * 128, tt * 512:(tt + 1) * 512],
                        in_=sb["accA"][:, :]), 16)
                else:
                    ac.inc(a.activation(
                        sb[dest][:, mt * T + tt * 512: mt * T + (tt + 1) * 512],
                        sb["accB"][:, :], AF.Silu))

    sweep("wv", 2, "v")
    sweep("wk", 1, "k")
    sweep("wq", 0, "q")

    b.barrier()
    for tt in range(NTT):
        w(s, pe, ac)
        qs.inc(s.dma_start(out=sb["xs"][:, :],
                           in_=b.hview(b.h_xn)[:, :, tt * 512:(tt + 1) * 512]), 16)
        last = None
        for kt in range(NDT):
            w(t, qs, ve, ac)
            if last is not None:
                t.wait_ge(pe.sem, last)
            tok = t.matmul(pp[4][0:4, 0:512], sb["wb_w"][:, kt * 4:(kt + 1) * 4],
                           sb["xs"][:, :].rearrange("p (dt c) -> p dt c", dt=NDT)[:, kt, :],
                           start=kt == 0, stop=kt == 7)
            last = pe.inc(tok)
        w(a, pe)
        b.selfw(a, ac)
        ac.inc(a.activation(sb["rows1"][0:4, tt * 512:(tt + 1) * 512],
                            pp[4][0:4, 0:512], AF.Sigmoid))


    def l2pass(src, rowbase):
        b.barrier()
        for tt in range(NTT):
            last = None
            for dd in range(NDT):
                h = dd // 2
                w(a, pe, ve)
                b.selfw(a, ac)
                ac.inc(a.activation(sb["sqrow"][:, 0:512],
                                    sb[src][:, dd * T + tt * 512: dd * T + (tt + 1) * 512],
                                    AF.Square))
                w(t, ac)
                if last is not None:
                    t.wait_ge(pe.sem, last)
                last = pe.inc(t.matmul(pp[4][0:4, 0:512], sb["hsel"][:, h * 4: h * 4 + 4],
                                       sb["sqrow"][:, 0:512],
                                       start=dd == 0, stop=dd == 7))
            w(v, pe)
            b.selfw(v, ve)
            ve.inc(v.tensor_scalar(sb["ev1"][0:4, 0:512], pp[4][0:4, 0:512],
                                   1e-6, None, ALU.add))
            b.selfw(v, ve)
            ve.inc(v.reciprocal(sb["ev1"][0:4, 0:512], sb["ev1"][0:4, 0:512]))
            w(a, ve)
            b.selfw(a, ac)
            ac.inc(a.activation(sb["ev1"][0:4, 0:512], sb["ev1"][0:4, 0:512], AF.Sqrt))
            w(t, ac)
            tp = (0, rowbase) if rowbase == 96 else None
            pe.inc(t.matmul(pp[5][rowbase:rowbase + 4, 0:512],
                            sb["identf"][0:4, 0:4],
                            sb["ev1"][0:4, 0:512], start=True, stop=True,
                            tile_position=tp))
            w(v, pe)
            b.selfw(v, ve)
            ve.inc(v.tensor_copy(sb["rows1"][rowbase:rowbase + 4, tt * 512:(tt + 1) * 512],
                                 pp[5][rowbase:rowbase + 4, 0:512]))
            for h in range(4):
                bk = pp[2 + h % 2]
                w(t, ve)
                pe.inc(t.matmul(bk[:, 0:512], sb["selhf"][0:4, h * 128:(h + 1) * 128],
                                sb["ev1"][0:4, 0:512], start=True, stop=True))
                for dd in (2 * h, 2 * h + 1):
                    w(v, pe)
                    b.selfw(v, ve)
                    ve.inc(v.tensor_mul(
                        sb[src][:, dd * T + tt * 512: dd * T + (tt + 1) * 512],
                        sb[src][:, dd * T + tt * 512: dd * T + (tt + 1) * 512],
                        bk[:, 0:512]))

    l2pass("q", 64)
    l2pass("k", 96)


def p4(b, nblk=NBLK):
    t, v, a, g, s = b.t, b.v, b.a, b.g, b.s
    sb, pp, w = b.sb, b.pp, b.w
    pe, ve, ac, gp, qs, qg = b.pe, b.ve, b.ac, b.gp, b.qs, b.qg
    b.barrier()
    g.memset(sb["S_f"][:, :], 0.0)
    gp.inc(g.memset(sb["S_b"][:, :], 0.0))
    w(t, gp)

    for b4 in range(nblk):
        c0b = b4 * 128
        w(s, pe, ve, ac)
        qs.inc(s.dma_start(out=sb["vblk"][:, :],
                           in_=b.hview(b.h_v)[:, :, c0b:c0b + 128]), 16)
        w(t, qs, ve, ac)
        pe.inc(t.transpose(b.pb[0][0:128, 0:4], sb["rows1"][0:4, c0b:c0b + 128],
                           sb["ident"][0:4, 0:4]))
        w(v, pe)
        b.selfw(v, ve)
        ve.inc(v.tensor_copy(sb["btm"][:, 0:4], b.pb[0][0:128, 0:4]))
        w(t, ve, ac)
        lastk = None
        for dd in range(NDT):
            if lastk is not None:
                t.wait_ge(pe.sem, lastk)
            lastk = pe.inc(t.transpose(b.pb[0][:, dd * 128:(dd + 1) * 128],
                                       sb["k"][:, dd * T + c0b: dd * T + c0b + 128],
                                       sb["ident"][:, :]))
        w(a, pe)
        b.selfw(a, ac)
        ac.inc(a.copy(sb["ktm"][:, :], b.pb[0][:, :]))
        w(t, qs)
        lastv = None
        for dd in range(NDT):
            if lastv is not None:
                t.wait_ge(pe.sem, lastv)
            lastv = pe.inc(t.transpose(b.pb[1][:, dd * 128:(dd + 1) * 128],
                                       sb["vblk"][:, dd * 128:(dd + 1) * 128],
                                       sb["ident"][:, :]))
        w(v, pe)
        for dd in range(NDT):
            h = dd // 2
            b.selfw(v, ve)
            ve.inc(v.tensor_scalar(sb["vtm"][:, dd * 128:(dd + 1) * 128],
                                   b.pb[1][:, dd * 128:(dd + 1) * 128],
                                   sb["btm"][:, h:h + 1],
                                   None, ALU.mult))
        for h in range(4):
            bank = pp[4 + h % 2]
            w(t, ve)
            pe.inc(t.matmul(bank[:, 0:128], sb["negselh"][0:4, h * 128:(h + 1) * 128],
                            sb["rows1"][0:4, c0b:c0b + 128], start=True, stop=True))
            for dd in (2 * h, 2 * h + 1):
                w(v, pe)
                b.selfw(v, ve)
                ve.inc(v.tensor_mul(sb["kbn"][:, dd * 128:(dd + 1) * 128],
                                    sb["k"][:, dd * T + c0b: dd * T + c0b + 128],
                                    bank[:, 0:128]))
        for ci in range(4):
            c0 = ci * 32
            tpA = (0, c0) if c0 == 96 else None
            tpB = (96, 96) if c0 == 96 else None
            tpC = (96, 0) if c0 == 96 else None
            tok0 = c0b + ci * 32
            lastg = None
            for h in range(4):
                for dkt in range(2):
                    dd = 2 * h + dkt
                    w(t, ve, ac)
                    if lastg is not None and dkt == 1:
                        t.wait_ge(pe.sem, lastg)
                    tok = t.matmul(
                        pp[2][32 * h:32 * h + 32, 32 * h:32 * h + 32],
                        sb["kbn"][:, dd * 128 + ci * 32: dd * 128 + ci * 32 + 32],
                        sb["k"][:, dd * T + tok0: dd * T + tok0 + 32],
                        start=dkt == 0, stop=dkt == 1,
                        tile_position=(0, 32 * h) if h == 3 else None)
                    lastg = pe.inc(tok)
            w(v, pe)
            b.selfw(v, ve)
            ve.inc(v.scalar_tensor_tensor(sb["tmA"][:, :], pp[2][:, 0:128], 1.0,
                                          sb["mask_bdsl"][:, :], ALU.mult, ALU.mult))
            b.selfw(v, ve)
            ve.inc(v.transpose(sb["tmB"][:, :], sb["tmA"][:, :]))
            b.selfw(v, ve)
            ve.inc(v.tensor_add(sb["ttA"][:, :], sb["tmB"][:, :], sb["ident"][:, :]))
            ttc, ttn = "ttA", "ttB"
            for j in range(4):
                w(t, ve, ac)
                pe.inc(t.matmul(pp[3][:, 0:128], sb["tmB"][:, :], sb["tmA"][:, :],
                                start=True, stop=True))
                w(a, pe)
                b.selfw(a, ac)
                ac.inc(a.copy(sb["tmA"][:, :], pp[3][:, 0:128]))
                w(v, ac)
                b.selfw(v, ve)
                ve.inc(v.transpose(sb["tmB"][:, :], sb["tmA"][:, :]))
                w(t, ac, ve)
                t.wait_ge(pe.sem, pe.n)
                pe.inc(t.matmul(pp[3][:, 128:256], sb["tmA"][:, :], sb[ttc][:, :],
                                start=True, stop=True))
                w(v, pe)
                b.selfw(v, ve)
                ve.inc(v.scalar_tensor_tensor(sb[ttn][:, :], pp[3][:, 128:256], 1.0,
                                              sb[ttc][:, :], ALU.mult, ALU.add))
                ttc, ttn = ttn, ttc
            w(t, ve)
            pe.inc(t.matmul(pp[5][c0:c0 + 32, 0:128], sb["sel4"][:, :], sb[ttc][:, :],
                            start=True, stop=True, tile_position=tpA))
            w(a, pe)
            b.selfw(a, ac)
            ac.inc(a.copy(sb["ttf"][c0:c0 + 32, :], pp[5][c0:c0 + 32, 0:128]))
            lasta = None
            for h in range(4):
                for dkt in range(2):
                    dd = 2 * h + dkt
                    w(t, ac, ve)
                    if lasta is not None:
                        t.wait_ge(pe.sem, lasta)
                    tok = t.matmul(
                        pp[4][c0:c0 + 32, 32 * h:32 * h + 32],
                        sb["k"][:, dd * T + tok0: dd * T + tok0 + 32],
                        sb["q"][:, dd * T + tok0: dd * T + tok0 + 32],
                        start=dkt == 0, stop=dkt == 1, tile_position=tpA)
                    lasta = pe.inc(tok)
            w(v, pe)
            b.selfw(v, ve)
            ve.inc(v.scalar_tensor_tensor(sb["attf"][c0:c0 + 32, :],
                                          pp[4][c0:c0 + 32, 0:128], 1.0,
                                          sb["mask_triu"][c0:c0 + 32, :],
                                          ALU.mult, ALU.mult))
            zl = {}
            for h in range(4):
                for dkt in range(2):
                    dd = 2 * h + dkt
                    bank = pp[0] if h < 2 else pp[1]
                    fo = (h % 2) * 256
                    w(t, ve, ac)
                    key = (h < 2)
                    if key in zl:
                        t.wait_ge(pe.sem, zl[key])
                    tok = t.matmul(bank[c0:c0 + 32, fo:fo + 256],
                                   sb["kbn"][:, dd * 128 + ci * 32: dd * 128 + ci * 32 + 32],
                                   sb["S_b"][:, dkt * 1024 + h * 256: dkt * 1024 + (h + 1) * 256],
                                   start=dkt == 0, stop=dkt == 1, tile_position=tpA)
                    zl[key] = pe.inc(tok)
            w(v, pe)
            b.selfw(v, ve)
            ve.inc(v.scalar_tensor_tensor(sb["z_sb"][c0:c0 + 32, 0:512],
                                          pp[0][c0:c0 + 32, 0:512], 1.0,
                                          sb["vtm"][c0:c0 + 32, 0:512],
                                          ALU.mult, ALU.add))
            b.selfw(v, ve)
            ve.inc(v.scalar_tensor_tensor(sb["z_sb"][c0:c0 + 32, 512:1024],
                                          pp[1][c0:c0 + 32, 0:512], 1.0,
                                          sb["vtm"][c0:c0 + 32, 512:1024],
                                          ALU.mult, ALU.add))
            for h in range(4):
                bank = pp[4] if h < 2 else pp[5]
                fo = (h % 2) * 256
                w(t, ve)
                t.wait_ge(pe.sem, pe.n)
                pe.inc(t.matmul(bank[c0:c0 + 32, fo:fo + 256],
                                sb["ttf"][c0:c0 + 32, 32 * h:32 * h + 32],
                                sb["z_sb"][c0:c0 + 32, h * 256:(h + 1) * 256],
                                start=True, stop=True, tile_position=tpB))
            w(v, pe)
            b.selfw(v, ve)
            ve.inc(v.tensor_copy(sb["u_sb"][c0:c0 + 32, 0:512], pp[4][c0:c0 + 32, 0:512]))
            b.selfw(v, ve)
            ve.inc(v.tensor_copy(sb["u_sb"][c0:c0 + 32, 512:1024], pp[5][c0:c0 + 32, 0:512]))
            ol = {}
            for h in range(4):
                bank = pp[0] if h < 2 else pp[1]
                fo = (h % 2) * 256
                for dkt in range(2):
                    dd = 2 * h + dkt
                    w(t, ve)
                    key = (h < 2)
                    if key in ol:
                        t.wait_ge(pe.sem, ol[key])
                    ol[key] = pe.inc(t.matmul(
                        bank[c0:c0 + 32, fo:fo + 256],
                        sb["q"][:, dd * T + tok0: dd * T + tok0 + 32],
                        sb["S_b"][:, dkt * 1024 + h * 256: dkt * 1024 + (h + 1) * 256],
                        start=dkt == 0, stop=dkt == 1, tile_position=tpA))
                bank2 = pp[4] if h < 2 else pp[5]
                w(t, ve)
                t.wait_ge(pe.sem, pe.n)
                ol[("a", h < 2)] = pe.inc(t.matmul(
                    bank2[c0:c0 + 32, fo:fo + 256],
                    sb["attf"][c0:c0 + 32, 32 * h:32 * h + 32],
                    sb["u_sb"][c0:c0 + 32, h * 256:(h + 1) * 256],
                    start=True, stop=True, tile_position=tpB))
            w(v, pe)
            b.selfw(v, ve)
            ve.inc(v.tensor_copy(sb["o_tm"][c0:c0 + 32, 0:512], pp[4][c0:c0 + 32, 0:512]))
            b.selfw(v, ve)
            ve.inc(v.tensor_copy(sb["o_tm"][c0:c0 + 32, 512:1024], pp[5][c0:c0 + 32, 0:512]))
            b.selfw(v, ve)
            ve.inc(v.scalar_tensor_tensor(sb["o_tm"][c0:c0 + 32, 0:512],
                                          pp[0][c0:c0 + 32, 0:512], 1.0,
                                          sb["o_tm"][c0:c0 + 32, 0:512],
                                          ALU.mult, ALU.add))
            b.selfw(v, ve)
            ve.inc(v.scalar_tensor_tensor(sb["o_tm"][c0:c0 + 32, 512:1024],
                                          pp[1][c0:c0 + 32, 0:512], 1.0,
                                          sb["o_tm"][c0:c0 + 32, 512:1024],
                                          ALU.mult, ALU.add))
            for wave in range(2):
                for i2 in range(4):
                    h = wave * 2 + i2 // 2
                    dkt = i2 % 2
                    bank = pp[2 + i2]
                    w(t, ve, ac)
                    t.wait_ge(pe.sem, pe.n)
                    pe.inc(t.matmul(
                        bank[0:128, 0:256],
                        sb["ktm"][c0:c0 + 32, (2 * h + dkt) * 128:(2 * h + dkt + 1) * 128],
                        sb["u_sb"][c0:c0 + 32, h * 256:(h + 1) * 256],
                        start=True, stop=True, tile_position=tpC))
                for i2 in range(4):
                    h = wave * 2 + i2 // 2
                    dkt = i2 % 2
                    bank = pp[2 + i2]
                    sl = slice(dkt * 1024 + h * 256, dkt * 1024 + (h + 1) * 256)
                    w(v, pe)
                    b.selfw(v, ve)
                    ve.inc(v.scalar_tensor_tensor(sb["S_f"][:, sl], sb["S_f"][:, sl],
                                                  sb["lam"][:, h:h + 1],
                                                  bank[0:128, 0:256],
                                                  ALU.mult, ALU.add))
                    w(a, ve)
                    b.selfw(a, ac)
                    ac.inc(a.copy(sb["S_b"][:, sl], sb["S_f"][:, sl]))
        w(t, ve, ac)
        lasto = None
        for dd in range(NDT):
            if lasto is not None:
                t.wait_ge(pe.sem, lasto)
            lasto = pe.inc(t.transpose(b.pb[0][:, dd * 128:(dd + 1) * 128],
                                       sb["o_tm"][:, dd * 128:(dd + 1) * 128],
                                       sb["ident"][:, :]))
        w(v, pe, qs)
        b.selfw(v, ve)
        ve.inc(v.tensor_copy(
            sb["dfm"][:, :].rearrange("p (dt blk c) -> p dt blk c",
                                      blk=4, c=128)[:, :, b4 % 4, :],
            b.pb[0][:, :].rearrange("p (dt c) -> p dt c", c=128)))
        if b4 % 4 == 3:
            tt = b4 // 4
            for dd in range(NDT):
                w(s, ve, ac)
                qs.inc(s.dma_start(
                    out=b.h_df[dd * 128:(dd + 1) * 128, tt * 512:(tt + 1) * 512],
                    in_=sb["dfm"][:, dd * 512:(dd + 1) * 512]), 16)


def p5(b, upto="all"):
    t, v, a, g, s = b.t, b.v, b.a, b.g, b.s
    sb, pp, W, w = b.sb, b.pp, b.W, b.w
    pe, ve, ac, gp, qs, qa, qg = b.pe, b.ve, b.ac, b.gp, b.qs, b.qa, b.qg

    b.barrier()
    for dd in range(NDT):
        w(s, ve, ac, qa)
        qs.inc(s.dma_start(out=sb["vrow"][:, :], in_=b.h_v[dd * 128:(dd + 1) * 128, :]), 16)
        w(v, qs, qa)
        b.selfw(v, ve)
        ve.inc(v.tensor_scalar(sb["srow"][:, 0:T], sb["vrow"][:, 0:T],
                               sb["firs"][:, dd * 4 + 2: dd * 4 + 3], None, ALU.mult))
        for tap in range(2):
            sh = 2 - tap
            b.selfw(v, ve)
            ve.inc(v.scalar_tensor_tensor(sb["srow"][:, sh:T], sb["vrow"][:, 0:T - sh],
                                          sb["firs"][:, dd * 4 + tap: dd * 4 + tap + 1],
                                          sb["srow"][:, sh:T], ALU.mult, ALU.add))
        w(a, ve)
        qa.inc(a.dma_start(out=b.h_sh[dd * 128:(dd + 1) * 128, :], in_=sb["srow"][:, :]), 16)
        w(v, qa)
        b.selfw(v, ve)
        ve.inc(v.tensor_scalar(sb["lrow"][:, 0:T], sb["vrow"][:, 0:T],
                               sb["firl"][:, dd * 64 + 62: dd * 64 + 63], None, ALU.mult))
        for tap in range(62):
            sh = 62 - tap
            b.selfw(v, ve)
            ve.inc(v.scalar_tensor_tensor(sb["lrow"][:, sh:T], sb["vrow"][:, 0:T - sh],
                                          sb["firl"][:, dd * 64 + tap: dd * 64 + tap + 1],
                                          sb["lrow"][:, sh:T], ALU.mult, ALU.add))
        w(a, ve)
        b.selfw(a, ac)
        qa.inc(a.dma_start(out=b.h_lo[dd * 128:(dd + 1) * 128, :], in_=sb["lrow"][:, :]), 16)

    if upto == "fir":
        return

    def do_stats(pair, srcs, phase="all"):
        b.barrier()
        for sub, src_h in enumerate(srcs):
            for dd in range(NDT):
                slot = "vrow" if dd % 2 == 0 else "srow"
                w(s, ve, ac, pe)
                qs.inc(s.dma_start(out=sb[slot][:, :],
                                   in_=src_h[dd * 128:(dd + 1) * 128, :]), 16)
                for ti2 in range(32):
                    bank = b.pb[ti2 % 2]
                    c = sub * 256 + ti2 * 8 + dd
                    w(t, qs, ve, ac)
                    pe.inc(t.transpose(bank[:, 0:128],
                                       sb[slot][:, ti2 * 128:(ti2 + 1) * 128],
                                       sb["ident"][:, :]))
                    if phase != "accV":
                        w(a, pe)
                        b.selfw(a, ac)
                        ac.inc(a.activation(sb["scr"][:, :], bank[:, 0:128], AF.Copy,
                                            accum_out=sb["sumP"][:, c:c + 1]))
                        b.selfw(a, ac)
                        ac.inc(a.activation(sb["scr"][:, :], bank[:, 0:128], AF.Square,
                                            accum_out=sb["sqP"][:, c:c + 1]))
                    if phase != "accA":
                        w(v, ac, pe)
                        b.selfw(v, ve)
                        ve.inc(v.tensor_reduce(sb["maxP"][:, c:c + 1], bank[:, 0:128],
                                               AX.X, ALU.max))
        if phase == "acc":
            return
        sP = lambda nm, sub, t2: sb[nm][:, sub * 256 + t2 * 8:
                                        sub * 256 + (t2 + 1) * 8].rearrange(
            "p (h two) -> p h two", two=2)
        pbase = pair * 32
        for ti2 in range(32):
            for sub in range(2):
                c0 = ti2 * 32 + sub * 16
                stm = sb["statm"][:, c0:c0 + 16].rearrange(
                    "p (h st) -> p h st", st=4)
                se, so = sP("sumP", sub, ti2)[:, :, 0], sP("sumP", sub, ti2)[:, :, 1]
                qe, qo = sP("sqP", sub, ti2)[:, :, 0], sP("sqP", sub, ti2)[:, :, 1]
                me, mo = sP("maxP", sub, ti2)[:, :, 0], sP("maxP", sub, ti2)[:, :, 1]
                w(v, ac, pe)
                b.selfw(v, ve)
                ve.inc(v.tensor_add(stm[:, :, 0], se, so))
                b.selfw(v, ve)
                ve.inc(v.tensor_scalar(stm[:, :, 0], stm[:, :, 0], 1.0 / 256.0,
                                       None, ALU.mult))
                b.selfw(v, ve)
                ve.inc(v.tensor_add(stm[:, :, 3], qe, qo))
                b.selfw(v, ve)
                ve.inc(v.tensor_mul(stm[:, :, 1], stm[:, :, 0], stm[:, :, 0]))
                b.selfw(v, ve)
                ve.inc(v.scalar_tensor_tensor(stm[:, :, 1], stm[:, :, 3], 1.0 / 256.0,
                                              stm[:, :, 1], ALU.mult, ALU.subtract))
                b.selfw(v, ve)
                ve.inc(v.tensor_max(stm[:, :, 2], me, mo))
                w(a, ve)
                b.selfw(a, ac)
                ac.inc(a.activation(stm[:, :, 3], stm[:, :, 3], AF.Sqrt))
            bank = pp[ti2 % 2]
            w(t, ve, ac)
            t.wait_ge(pe.sem, pe.n)
            pe.inc(t.matmul(bank[pbase:pbase + 32, 0:128],
                            sb["statm"][:, ti2 * 32:(ti2 + 1) * 32],
                            sb["identf"][:, :], start=True, stop=True))
            w(v, pe)
            b.selfw(v, ve)
            ve.inc(v.tensor_copy(sb["gstats"][pbase:pbase + 32,
                                              ti2 * 128:(ti2 + 1) * 128],
                                 bank[pbase:pbase + 32, 0:128]))

    if upto in ("acc0", "accA", "accV"):
        do_stats(0, (b.h_sh, b.h_lo),
                 phase="acc" if upto == "acc0" else upto)
        return
    do_stats(0, (b.h_sh, b.h_lo))
    if upto == "stats0":
        return
    do_stats(1, (b.h_df, b.h_v))
    if upto == "stats":
        return

    b.barrier()
    qg.inc(g.dma_start(out=sb["w1b"][:, 0:8 * 2048].rearrange("p (kt c) -> p kt c", c=2048),
                       in_=W["w1"][0:1024, :].rearrange("(kt p) c -> p kt c", p=128)), 16)
    qg.inc(g.dma_start(out=sb["w1b"][0:64, 8 * 2048:9 * 2048],
                       in_=W["w1"][1024:1088, :]), 16)
    for tt in range(NTT):
        w(s, pe, ve, ac)
        qs.inc(s.dma_start(out=sb["gxs"][:, :],
                           in_=b.hview(b.h_xn)[:, :, tt * 512:(tt + 1) * 512]), 16)
        gneed = {}
        lastb = {}
        for mt in range(16):
            bA = (mt % 2) * 2
            w(t, qs, qg)
            if bA in gneed:
                t.wait_ge(ve.sem, gneed[bA][0])
                t.wait_ge(ac.sem, gneed[bA][1])
            for kt in range(9):
                bk = bA + kt % 2
                if bk in lastb:
                    t.wait_ge(pe.sem, lastb[bk])
                if kt < 8:
                    rhs = sb["gxs"][:, :].rearrange("p (dt c) -> p dt c", dt=NDT)[:, kt, :]
                    lhs = sb["w1b"][:, kt * 2048 + mt * 128: kt * 2048 + mt * 128 + 128]
                else:
                    rhs = sb["gstats"][0:64, tt * 512:(tt + 1) * 512]
                    lhs = sb["w1b"][0:64, 8 * 2048 + mt * 128: 8 * 2048 + mt * 128 + 128]
                lastb[bk] = pe.inc(t.matmul(pp[bk][:, 0:512], lhs, rhs,
                                            start=kt < 2, stop=kt >= 7))
            w(a, pe, ve)
            b.selfw(a, ac)
            ac.inc(a.copy(sb["ev5"][:, :], pp[bA][:, 0:512]))
            w(v, ac, pe)
            b.selfw(v, ve)
            ve.inc(v.tensor_add(sb["ev6"][:, :], pp[bA + 1][:, 0:512], sb["ev5"][:, :]))
            w(a, ve)
            b.selfw(a, ac)
            ac.inc(a.activation(sb["hmid"][:, mt * 512:(mt + 1) * 512], sb["ev6"][:, :],
                                AF.Gelu, bias=sb["b1t"][:, mt:mt + 1]))
            gneed[bA] = (ve.n, ac.n)
            gneed[bA + 1] = gneed[bA]
        lastw = None
        w(t, ac, ve)
        for kt in range(16):
            if lastw is not None:
                t.wait_ge(pe.sem, lastw)
            lastw = pe.inc(t.matmul(pp[2][0:16, 0:512], sb["w2w"][:, kt * 16:(kt + 1) * 16],
                                    sb["hmid"][:, kt * 512:(kt + 1) * 512],
                                    start=kt == 0, stop=kt == 15))
        w(v, pe)
        b.selfw(v, ve)
        ve.inc(v.tensor_scalar(sb["zb16"][0:16, 0:512], pp[2][0:16, 0:512],
                               sb["p16"][0:16, 0:1], sb["p16"][0:16, 1:2],
                               ALU.add, ALU.mult))
        w(a, ve)
        b.selfw(a, ac)
        ac.inc(a.activation(sb["e16"][0:16, 0:512], sb["zb16"][0:16, 0:512], AF.Exp))
        w(t, ac)
        t.wait_ge(pe.sem, pe.n)
        pe.inc(t.matmul(pp[3][0:16, 0:512], sb["sumself"][0:16, 0:16],
                        sb["e16"][0:16, 0:512], start=True, stop=True))
        w(v, pe)
        b.selfw(v, ve)
        ve.inc(v.reciprocal(sb["zb16"][0:16, 0:512], pp[3][0:16, 0:512]))
        b.selfw(v, ve)
        ve.inc(v.tensor_mul(sb["zb16"][0:16, 0:512], sb["zb16"][0:16, 0:512],
                            sb["e16"][0:16, 0:512]))
        b.selfw(v, ve)
        ve.inc(v.tensor_scalar(sb["e16"][0:16, 0:512], sb["zb16"][0:16, 0:512],
                               sb["p16"][0:16, 2:3], None, ALU.max))
        w(t, ve)
        t.wait_ge(pe.sem, pe.n)
        pe.inc(t.matmul(pp[3][0:16, 0:512], sb["sumself"][0:16, 0:16],
                        sb["e16"][0:16, 0:512], start=True, stop=True))
        w(v, pe)
        b.selfw(v, ve)
        ve.inc(v.reciprocal(sb["zb16"][0:16, 0:512], pp[3][0:16, 0:512]))
        b.selfw(v, ve)
        ve.inc(v.tensor_mul(sb["zb16"][0:16, 0:512], sb["zb16"][0:16, 0:512],
                            sb["e16"][0:16, 0:512]))
        b.selfw(v, ve)
        ve.inc(v.tensor_copy(sb["p_fm"][0:16, tt * 512:(tt + 1) * 512],
                             sb["zb16"][0:16, 0:512]))
    if upto == "gate":
        return

    b.barrier()
    qg.inc(g.dma_start(out=sb["wobuf"][:, :].rearrange("p (kt c) -> p kt c", c=D),
                       in_=W["wo"].rearrange("(kt p) c -> p kt c", p=128)), 16)
    for tt in range(NTT):
        sl = slice(tt * 512, (tt + 1) * 512)
        w(s, pe, ve, ac)
        qs.inc(s.dma_start(out=sb["vtile"][:, :], in_=b.hview(b.h_v)[:, :, sl]), 16)
        qs.inc(s.dma_start(out=sb["dtile"][:, :], in_=b.hview(b.h_df)[:, :, sl]), 16)
        qs.inc(s.dma_start(out=sb["stile"][:, :], in_=b.hview(b.h_sh)[:, :, sl]), 16)
        qs.inc(s.dma_start(out=sb["ltile"][:, :], in_=b.hview(b.h_lo)[:, :, sl]), 16)
        lastpb = {}
        for dd in range(NDT):
            h = dd // 2
            dsl = slice(dd * 512, (dd + 1) * 512)
            w(t, qs, ve)
            for j in range(4):
                if j in lastpb:
                    t.wait_ge(pe.sem, lastpb[j])
                lastpb[j] = pe.inc(t.matmul(
                    pp[j][:, 0:512],
                    sb["selh16"][0:16, (4 * h + j) * 128:(4 * h + j + 1) * 128],
                    sb["p_fm"][0:16, sl], start=True, stop=True))
            w(v, pe)
            b.selfw(v, ve)
            ve.inc(v.tensor_mul(sb["ocomb"][:, 0:512], sb["stile"][:, dsl], pp[0][:, 0:512]))
            b.selfw(v, ve)
            ve.inc(v.tensor_mul(sb["ocm2"][:, 0:512], sb["ltile"][:, dsl], pp[1][:, 0:512]))
            b.selfw(v, ve)
            ve.inc(v.tensor_add(sb["ocomb"][:, 0:512], sb["ocomb"][:, 0:512], sb["ocm2"][:, 0:512]))
            b.selfw(v, ve)
            ve.inc(v.tensor_mul(sb["ocm2"][:, 0:512], sb["dtile"][:, dsl], pp[2][:, 0:512]))
            b.selfw(v, ve)
            ve.inc(v.tensor_add(sb["ocomb"][:, 0:512], sb["ocomb"][:, 0:512], sb["ocm2"][:, 0:512]))
            b.selfw(v, ve)
            ve.inc(v.tensor_mul(sb["ocm2"][:, 0:512], sb["vtile"][:, dsl], pp[3][:, 0:512]))
            b.selfw(v, ve)
            ve.inc(v.tensor_add(sb["ocomb"][:, 0:512], sb["ocomb"][:, 0:512], sb["ocm2"][:, 0:512]))
            if dd % 2 == 1:
                w(a, ve)
                b.selfw(a, ac)
                ac.inc(a.activation(sb["sqcb"][:, 0:512], sb["ocm3"][:, 0:512],
                                    AF.Square))
                b.selfw(a, ac)
                ac.inc(a.activation(sb["sqcb"][:, 512:1024], sb["ocomb"][:, 0:512],
                                    AF.Square))
                w(t, ac)
                t.wait_ge(pe.sem, pe.n)
                rms1 = pe.inc(t.matmul(pp[5][0:1, 0:512], sb["ones"][:, 0:1],
                                       sb["sqcb"][:, 0:512], start=True, stop=False))
                t.wait_ge(pe.sem, rms1)
                pe.inc(t.matmul(pp[5][0:1, 0:512], sb["ones"][:, 0:1],
                                sb["sqcb"][:, 512:1024], start=False, stop=True))
                w(v, pe)
                b.selfw(v, ve)
                ve.inc(v.tensor_copy(sb["ev6"][0:1, 0:512], pp[5][0:1, 0:512]))
                b.selfw(v, ve)
                ve.inc(v.tensor_scalar(sb["ev6"][0:1, 0:512], sb["ev6"][0:1, 0:512],
                                       1.0 / 256.0, 1e-5, ALU.mult, ALU.add))
                b.selfw(v, ve)
                ve.inc(v.reciprocal(sb["ev6"][0:1, 0:512], sb["ev6"][0:1, 0:512]))
                w(a, ve)
                b.selfw(a, ac)
                ac.inc(a.activation(sb["ev6"][0:1, 0:512], sb["ev6"][0:1, 0:512], AF.Sqrt))
                w(t, ac, ve)
                t.wait_ge(pe.sem, pe.n)
                pe.inc(t.matmul(pp[4][:, 0:512], sb["onesf"][0:1, 0:128],
                                sb["ev6"][0:1, 0:512], start=True, stop=True))
                for d2 in (dd - 1, dd):
                    nwcol = 0 if d2 % 2 == 0 else 1
                    w(v, pe)
                    b.selfw(v, ve)
                    ve.inc(v.scalar_tensor_tensor(
                        sb["onrm"][:, d2 * 512:(d2 + 1) * 512],
                        sb["ocomb"][:, 0:512] if d2 == dd else sb["ocm3"][:, 0:512],
                        sb["normw"][:, nwcol:nwcol + 1], pp[4][:, 0:512],
                        ALU.mult, ALU.mult))
            if dd % 2 == 0:
                b.selfw(v, ve)
                ve.inc(v.tensor_copy(sb["ocm3"][:, 0:512], sb["ocomb"][:, 0:512]))
        for mt in range(NDT):
            lastb = {}
            w(t, ve, ac)
            for kt in range(NDT):
                bk = kt % 2
                if bk in lastb:
                    t.wait_ge(pe.sem, lastb[bk])
                lastb[bk] = pe.inc(t.matmul(
                    pp[bk][:, 0:512],
                    sb["wobuf"][:, kt * D + mt * 128: kt * D + mt * 128 + 128],
                    sb["onrm"][:, kt * 512:(kt + 1) * 512],
                    start=kt < 2, stop=kt >= 6))
            w(a, pe, ve)
            b.selfw(a, ac)
            ac.inc(a.copy(sb["ev5"][:, :], pp[0][:, 0:512]))
            w(v, ac, pe, qs)
            b.selfw(v, ve)
            ve.inc(v.tensor_add(sb["ev6"][:, :], pp[1][:, 0:512], sb["ev5"][:, :]))
            w(s, ve)
            qs.inc(s.dma_start(out=b.h_s[mt * 128:(mt + 1) * 128, sl],
                               in_=sb["ev6"][:, :]), 16)


def p6(b):
    t, v, a, g, s = b.t, b.v, b.a, b.g, b.s
    sb, pp, W, w = b.sb, b.pp, b.W, b.w
    pe, ve, ac, gp, qs, qa, qg, ccs = (b.pe, b.ve, b.ac, b.gp, b.qs, b.qa,
                                       b.qg, b.ccs)

    def dequant_t(wsrc, dstname):
        b.barrier()
        for it in range(NI):
            w(s, ve, ac, pe)
            qs.inc(s.dma_start(out=sb["dqs"][:, :], in_=wsrc[it * 128:(it + 1) * 128, :]), 16)
            w(v, qs)
            b.selfw(v, ve)
            ve.inc(v.tensor_reduce(sb["dqm"][:, 0:32],
                                   sb["dqs"][:, :].rearrange("p (bk e) -> p bk e", e=32),
                                   AX.X, ALU.max, apply_absolute_value=True))
            b.selfw(v, ve)
            ve.inc(v.tensor_scalar(sb["dqm"][:, 0:32], sb["dqm"][:, 0:32],
                                   1.0 / 7.0, 1e-12, ALU.mult, ALU.max))
            b.selfw(v, ve)
            ve.inc(v.reciprocal(sb["dqm"][:, 32:64], sb["dqm"][:, 0:32]))
            for bk in range(32):
                b.selfw(v, ve)
                ve.inc(v.tensor_scalar(sb["dqs"][:, bk * 32:(bk + 1) * 32],
                                       sb["dqs"][:, bk * 32:(bk + 1) * 32],
                                       sb["dqm"][:, 32 + bk:33 + bk], MAGIC,
                                       ALU.mult, ALU.add))
                b.selfw(v, ve)
                ve.inc(v.tensor_scalar(sb["dqs"][:, bk * 32:(bk + 1) * 32],
                                       sb["dqs"][:, bk * 32:(bk + 1) * 32],
                                       MAGIC, sb["dqm"][:, bk:bk + 1],
                                       ALU.subtract, ALU.mult))
            w(a, ve)
            b.selfw(a, ac)
            ac.inc(a.copy(sb["dqb"][:, :], sb["dqs"][:, :]))
            for kt in range(NDT):
                bank = b.pb[kt % 2]
                w(t, ac, ve)
                t.wait_ge(pe.sem, pe.n)
                pe.inc(t.transpose(bank[:, 0:128], sb["dqb"][:, kt * 128:(kt + 1) * 128],
                                   sb["ident"][:, :]))
                if kt % 2 == 0:
                    w(a, pe)
                    b.selfw(a, ac)
                    ac.inc(a.copy(
                        sb[dstname][:, kt * 3072 + it * 128: kt * 3072 + it * 128 + 128],
                        bank[:, 0:128]))
                else:
                    w(v, pe)
                    b.selfw(v, ve)
                    ve.inc(v.tensor_copy(
                        sb[dstname][:, kt * 3072 + it * 128: kt * 3072 + it * 128 + 128],
                        bank[:, 0:128]))

    dequant_t(W["wu_f"], "wut")
    dequant_t(W["wg_f"], "wgt")

    b.barrier()
    amax_col = [33, 34]
    w(v, qs)
    b.selfw(v, ve)
    ve.inc(v.memset(sb["amax"][:, 33:35], 0.0))
    for tt in range(NTT):
        w(s, pe, ve, ac)
        qs.inc(s.dma_start(out=sb["gxs"][:, :],
                           in_=b.hview(b.h_xn)[:, :, tt * 512:(tt + 1) * 512]), 16)
        for dd in range(NDT):
            w(v, qs)
            b.selfw(v, ve)
            ve.inc(v.tensor_scalar(sb["ev7"][:, :],
                                   sb["gxs"][:, :].rearrange("p (dt c) -> p dt c", dt=NDT)[:, dd, :],
                                   sb["scb"][:, 1:2], MAGIC, ALU.mult, ALU.add))
            b.selfw(v, ve)
            ve.inc(v.tensor_scalar(sb["xq_t"][:, dd * 512:(dd + 1) * 512],
                                   sb["ev7"][:, :], MAGIC, sb["scb"][:, 0:1],
                                   ALU.subtract, ALU.mult))
        w(t, ve)
        fneed = {}
        lastb = {}
        for which, wname, hdst, acol in (("up", "wut", b.h_up, 33), ("ga", "wgt", b.h_ga, 34)):
            for it in range(NI):
                bA = (it % 2) * 2
                w(t, qs)
                if bA in fneed:
                    t.wait_ge(ve.sem, fneed[bA][0])
                    t.wait_ge(ac.sem, fneed[bA][1])
                for kt in range(NDT):
                    bk = bA + kt % 2
                    if bk in lastb:
                        t.wait_ge(pe.sem, lastb[bk])
                    lastb[bk] = pe.inc(t.matmul(
                        pp[bk][:, 0:512],
                        sb[wname][:, kt * 3072 + it * 128: kt * 3072 + it * 128 + 128],
                        sb["xq_t"][:, kt * 512:(kt + 1) * 512],
                        start=kt < 2, stop=kt >= 6))
                w(a, pe, ve)
                b.selfw(a, ac)
                ac.inc(a.copy(sb["ev7"][:, :], pp[bA][:, 0:512]))
                w(v, ac, pe, qs)
                b.selfw(v, ve)
                ve.inc(v.tensor_add(sb["ev8"][:, :], pp[bA + 1][:, 0:512], sb["ev7"][:, :]))
                w(s, ve)
                qs.inc(s.dma_start(out=hdst[it * 128:(it + 1) * 128, tt * 512:(tt + 1) * 512],
                                   in_=sb["ev8"][:, :]), 16)
                b.selfw(v, ve)
                ve.inc(v.tensor_reduce(sb["amax"][:, 40:41], sb["ev8"][:, :], AX.X,
                                       ALU.max, apply_absolute_value=True))
                b.selfw(v, ve)
                ve.inc(v.tensor_max(sb["amax"][:, acol:acol + 1],
                                    sb["amax"][:, acol:acol + 1], sb["amax"][:, 40:41]))
                fneed[bA] = (ve.n, ac.n)
                fneed[bA + 1] = fneed[bA]

    def fold_and_ar(cols, aridx, scslots):
        for i, cn in enumerate(cols):
            w(t, ve, ac)
            t.wait_ge(pe.sem, pe.n)
            pe.inc(t.transpose(pp[2][0:1, 0:128], sb["amax"][:, cn:cn + 1],
                               sb["identf"][:, :]))
            w(v, pe)
            b.selfw(v, ve)
            ve.inc(v.tensor_reduce(sb["sc"][0:1, 10 + i:11 + i], pp[2][0:1, 0:128],
                                   AX.X, ALU.max))
        w(g, ve)
        b.selfw(g, gp)
        qg.inc(g.dma_start(out=b.ar_in[aridx].ap()[0:1, 0:2],
                           in_=sb["sc"][0:1, 10:12]), 16)
        w(g, qg)
        ccs.inc(g.collective_compute("AllReduce", ALU.max,
                                     replica_groups=[list(range(8))],
                                     ins=[b.ar_in[aridx].ap()],
                                     outs=[b.ar_out[aridx].ap()]))
        w(g, ccs)
        qg.inc(g.dma_start(out=sb["sc"][0:1, 12:14], in_=b.ar_out[aridx].ap()[0:1, 0:2]), 16)
        w(v, qg)
        for i, (scol, icol) in enumerate(scslots):
            b.selfw(v, ve)
            ve.inc(v.tensor_scalar(sb["sc"][0:1, 14 + i:15 + i], sb["sc"][0:1, 12 + i:13 + i],
                                   1.0 / 32767.0, 1e-12, ALU.mult, ALU.max))
            b.selfw(v, ve)
            ve.inc(v.reciprocal(sb["sc"][0:1, 16 + i:17 + i], sb["sc"][0:1, 14 + i:15 + i]))
        w(t, ve)
        t.wait_ge(pe.sem, pe.n)
        pe.inc(t.matmul(pp[3][:, 0:4], sb["onesf"][0:1, 0:128], sb["sc"][0:1, 14:18],
                        start=True, stop=True))
        w(v, pe)
        for i, (scol, icol) in enumerate(scslots):
            b.selfw(v, ve)
            ve.inc(v.tensor_copy(sb["scb"][:, scol:scol + 1], pp[3][:, i:i + 1]))
            b.selfw(v, ve)
            ve.inc(v.tensor_copy(sb["scb"][:, icol:icol + 1], pp[3][:, 2 + i:3 + i]))

    fold_and_ar([33, 34], 1, [(2, 3), (4, 5)])
    b.selfw(v, ve)
    ve.inc(v.tensor_scalar(sb["scb"][:, 10:11], sb["scb"][:, 4:5], -MAGIC, None, ALU.mult))
    b.selfw(v, ve)
    ve.inc(v.tensor_scalar(sb["scb"][:, 12:13], sb["scb"][:, 2:3], -MAGIC, None, ALU.mult))

    b.barrier()
    b.selfw(v, ve)
    ve.inc(v.memset(sb["amax"][:, 35:36], 0.0))
    for it in range(NI):
        w(s, ve, ac)
        qs.inc(s.dma_start(out=sb["ffa"][:, :], in_=b.h_ga[it * 128:(it + 1) * 128, :]), 16)
        w(a, qs)
        b.selfw(a, ac)
        ac.inc(a.activation(sb["ffb"][:, :], sb["ffa"][:, :], AF.Copy,
                            scale=sb["scb"][:, 5:6], bias=float(MAGIC)))
        w(v, ac)
        b.selfw(v, ve)
        ve.inc(v.tensor_scalar(sb["ffb"][:, :], sb["ffb"][:, :], MAGIC,
                               sb["scb"][:, 4:5], ALU.subtract, ALU.mult))
        w(a, ve)
        b.selfw(a, ac)
        ac.inc(a.activation(sb["ffc"][:, :], sb["ffb"][:, :], AF.Sigmoid))
        w(v, ac)
        b.selfw(v, ve)
        ve.inc(v.tensor_scalar(sb["ffc"][:, :], sb["ffc"][:, :], 65536.0, MAGIC,
                               ALU.mult, ALU.add))
        b.selfw(v, ve)
        ve.inc(v.tensor_scalar(sb["ffc"][:, :], sb["ffc"][:, :], MAGIC, 65535.0,
                               ALU.subtract, ALU.min))
        b.selfw(v, ve)
        ve.inc(v.scalar_tensor_tensor(sb["ffa"][:, :], sb["ffc"][:, :], float(2.0 ** -16),
                                      sb["ffb"][:, :], ALU.mult, ALU.mult))
        w(s, ve)
        qs.inc(s.dma_start(out=b.h_t[it * 128:(it + 1) * 128, :], in_=sb["ffa"][:, :]), 16)
        b.selfw(v, ve)
        ve.inc(v.tensor_reduce(sb["amax"][:, 40:41], sb["ffa"][:, :], AX.X,
                               ALU.max, apply_absolute_value=True))
        b.selfw(v, ve)
        ve.inc(v.tensor_max(sb["amax"][:, 35:36], sb["amax"][:, 35:36],
                            sb["amax"][:, 40:41]))
    fold_and_ar([35], 2, [(6, 7)])
    b.selfw(v, ve)
    ve.inc(v.tensor_scalar(sb["scb"][:, 11:12], sb["scb"][:, 6:7], -MAGIC, None, ALU.mult))

    b.barrier()
    b.selfw(v, ve)
    ve.inc(v.memset(sb["amax"][:, 36:37], 0.0))
    for it in range(NI):
        w(s, ve, ac)
        qs.inc(s.dma_start(out=sb["ffa"][:, :], in_=b.h_t[it * 128:(it + 1) * 128, :]), 16)
        qs.inc(s.dma_start(out=sb["ffb"][:, :], in_=b.h_up[it * 128:(it + 1) * 128, :]), 16)
        w(a, qs, ve)
        b.selfw(a, ac)
        ac.inc(a.activation(sb["ffa"][:, :], sb["ffa"][:, :], AF.Copy,
                            scale=sb["scb"][:, 7:8], bias=float(MAGIC)))
        b.selfw(a, ac)
        ac.inc(a.activation(sb["ffb"][:, :], sb["ffb"][:, :], AF.Copy,
                            scale=sb["scb"][:, 3:4], bias=float(MAGIC)))
        w(v, qs, ac)
        b.selfw(v, ve)
        ve.inc(v.tensor_scalar(sb["ffa"][:, :], sb["ffa"][:, :], MAGIC, sb["scb"][:, 6:7],
                               ALU.subtract, ALU.mult))
        b.selfw(v, ve)
        ve.inc(v.tensor_scalar(sb["ffb"][:, :], sb["ffb"][:, :], MAGIC, sb["scb"][:, 2:3],
                               ALU.subtract, ALU.mult))
        b.selfw(v, ve)
        ve.inc(v.tensor_mul(sb["ffc"][:, :], sb["ffa"][:, :], sb["ffb"][:, :]))
        w(s, ve)
        qs.inc(s.dma_start(out=b.h_ot[it * 128:(it + 1) * 128, :], in_=sb["ffc"][:, :]), 16)
        b.selfw(v, ve)
        ve.inc(v.tensor_reduce(sb["amax"][:, 40:41], sb["ffc"][:, :], AX.X,
                               ALU.max, apply_absolute_value=True))
        b.selfw(v, ve)
        ve.inc(v.tensor_max(sb["amax"][:, 36:37], sb["amax"][:, 36:37],
                            sb["amax"][:, 40:41]))
    fold_and_ar([36], 3, [(8, 9)])
    b.selfw(v, ve)
    ve.inc(v.tensor_scalar(sb["scb"][:, 13:14], sb["scb"][:, 8:9], -MAGIC, None, ALU.mult))

    b.barrier()
    for pt in range(NDT):
        w(s, ve, ac, pe)
        qs.inc(s.dma_start(out=sb["ffa"][:, 0:3072], in_=W["wd_f"][pt * 128:(pt + 1) * 128, :]), 16)
        w(v, qs)
        b.selfw(v, ve)
        ve.inc(v.tensor_reduce(sb["dqm"][:, 0:96] if False else sb["ffb"][:, 0:96],
                               sb["ffa"][:, 0:3072].rearrange("p (bk e) -> p bk e", e=32),
                               AX.X, ALU.max, apply_absolute_value=True))
        b.selfw(v, ve)
        ve.inc(v.tensor_scalar(sb["ffb"][:, 0:96], sb["ffb"][:, 0:96],
                               1.0 / 7.0, 1e-12, ALU.mult, ALU.max))
        b.selfw(v, ve)
        ve.inc(v.reciprocal(sb["ffb"][:, 96:192], sb["ffb"][:, 0:96]))
        for bk in range(96):
            b.selfw(v, ve)
            ve.inc(v.tensor_scalar(sb["ffa"][:, bk * 32:(bk + 1) * 32],
                                   sb["ffa"][:, bk * 32:(bk + 1) * 32],
                                   sb["ffb"][:, 96 + bk:97 + bk], MAGIC,
                                   ALU.mult, ALU.add))
            b.selfw(v, ve)
            ve.inc(v.tensor_scalar(sb["ffa"][:, bk * 32:(bk + 1) * 32],
                                   sb["ffa"][:, bk * 32:(bk + 1) * 32],
                                   MAGIC, sb["ffb"][:, bk:bk + 1],
                                   ALU.subtract, ALU.mult))
        w(a, ve)
        b.selfw(a, ac)
        ac.inc(a.copy(sb["ffc"][:, 0:3072], sb["ffa"][:, 0:3072]))
        for c3 in range(3):
            w(v, ac)
            b.selfw(v, ve)
            ve.inc(v.tensor_copy(sb["dqb"][:, :], sb["ffc"][:, c3 * 1024:(c3 + 1) * 1024]))
            for kt in range(8):
                it = c3 * 8 + kt
                bank = b.pb[kt % 2]
                w(t, ve, ac)
                t.wait_ge(pe.sem, pe.n)
                pe.inc(t.transpose(bank[:, 0:128], sb["dqb"][:, kt * 128:(kt + 1) * 128],
                                   sb["ident"][:, :]))
                w(v, pe)
                b.selfw(v, ve)
                ve.inc(v.tensor_copy(
                    sb["wdt"][:, it * 1024 + pt * 128: it * 1024 + pt * 128 + 128],
                    bank[:, 0:128]))
    for tt in range(NTT):
        sl = slice(tt * 512, (tt + 1) * 512)
        for blk in range(4):
            w(s, ve, ac)
            qs.inc(s.dma_start(out=sb["x_t"][:, :],
                               in_=b.dx[tt * 512 + blk * 128: tt * 512 + (blk + 1) * 128, :]), 16)
            for dd in range(NDT):
                bank = pp[2 + dd % 2]
                w(t, qs, ve, ac)
                t.wait_ge(pe.sem, pe.n)
                pe.inc(t.transpose(bank[:, 0:128], sb["x_t"][:, dd * 128:(dd + 1) * 128],
                                   sb["identf"][:, :]))
                w(v, pe)
                b.selfw(v, ve)
                ve.inc(v.tensor_copy(
                    sb["xpt"][:, dd * 512 + blk * 128: dd * 512 + blk * 128 + 128],
                    bank[:, 0:128]))
        lastd = {}
        for half in range(2):
            for it in range(NI):
                w(s, ve, ac)
                qs.inc(s.dma_start(out=sb["sgt"][:, :], in_=b.h_ot[it * 128:(it + 1) * 128, sl]), 16)
                w(a, qs, pe)
                b.selfw(a, ac)
                ac.inc(a.activation(sb["sgt"][:, :], sb["sgt"][:, :], AF.Copy,
                                    scale=sb["scb"][:, 9:10], bias=float(MAGIC)))
                w(v, ac, pe)
                b.selfw(v, ve)
                ve.inc(v.tensor_scalar(sb["o2t"][:, :], sb["sgt"][:, :], MAGIC,
                                       sb["scb"][:, 8:9], ALU.subtract, ALU.mult))
                w(t, ve)
                if it == 0:
                    w(t, ac)
                for mi in range(4):
                    mt = half * 4 + mi
                    bank = pp[mi]
                    if mi in lastd:
                        t.wait_ge(pe.sem, lastd[mi])
                    lastd[mi] = pe.inc(t.matmul(
                        bank[:, 0:512],
                        sb["wdt"][:, it * 1024 + mt * 128: it * 1024 + mt * 128 + 128],
                        sb["o2t"][:, :],
                        start=it == 0, stop=it == NI - 1))
            for mi in range(4):
                mt = half * 4 + mi
                w(s, ve, ac)
                qs.inc(s.dma_start(out=sb["sgt"][:, :], in_=b.h_s[mt * 128:(mt + 1) * 128, sl]), 16)
                w(a, pe, qs)
                b.selfw(a, ac)
                ac.inc(a.activation(sb["gt"][:, :], pp[mi][:, 0:512], AF.Silu))
                w(v, ac, qs)
                b.selfw(v, ve)
                ve.inc(v.tensor_mul(sb["gt"][:, :], sb["gt"][:, :], sb["sgt"][:, :]))
                b.selfw(v, ve)
                ve.inc(v.tensor_add(sb["outf"][:, mt * 512:(mt + 1) * 512],
                                    sb["gt"][:, :], sb["xpt"][:, mt * 512:(mt + 1) * 512]))
        for blk in range(4):
            for dd in range(NDT):
                bank = pp[2 + dd % 2]
                w(t, ve, ac)
                t.wait_ge(pe.sem, pe.n)
                pe.inc(t.transpose(bank[:, 0:128],
                                   sb["outf"][:, dd * 512 + blk * 128: dd * 512 + blk * 128 + 128],
                                   sb["identf"][:, :]))
                if dd % 2 == 0:
                    w(a, pe, qs)
                    b.selfw(a, ac)
                    ac.inc(a.copy(sb["obuf"][:, dd * 128:(dd + 1) * 128], bank[:, 0:128]))
                else:
                    w(v, pe, qs)
                    b.selfw(v, ve)
                    ve.inc(v.tensor_copy(sb["obuf"][:, dd * 128:(dd + 1) * 128], bank[:, 0:128]))
            w(s, ve, ac)
            qs.inc(s.dma_start(
                out=b.dy[tt * 512 + blk * 128: tt * 512 + (blk + 1) * 128, :],
                in_=sb["obuf"][:, 0:1024]), 16)



def fixup_bias_note():
    pass


def finish(b):
    w = b.w
    b.barrier()
    b.g.wait_ge(b.qs.sem, b.qs.n)
    b.g.wait_ge(b.qa.sem, b.qa.n)
    b.g.wait_ge(b.qg.sem, b.qg.n)


def build(dump=None):
    b = mkbuilder(dump)
    p0(b)
    p1(b)
    p3(b)
    if dump == "xn":
        dbg_dump(b, b.h_xn, 0)
    elif dump == "q":
        dbg_dump_sb(b, "q", 0)
    elif dump == "v":
        dbg_dump(b, b.h_v, 0)
    elif dump == "S1":
        p4(b, nblk=1)
        sb = b.sb
        b.barrier()
        b.selfw(b.v, b.ve)
        b.ve.inc(b.v.memset(sb["ffb"][:, :], 0.0))
        b.selfw(b.v, b.ve)
        b.ve.inc(b.v.tensor_copy(sb["ffb"][:, 0:2048], sb["S_f"][:, :]))
        b.w(b.s, b.ve)
        b.qs.inc(b.s.dma_start(out=b.d_dbg, in_=sb["ffb"][:, :]), 16)
    else:
        p4(b)
        if dump == "delta":
            dbg_dump(b, b.h_df, 0)
        elif dump in ("fir", "acc0", "accA", "accV", "stats0", "stats", "gate"):
            p5(b, upto=dump)
            sb = b.sb
            if dump == "fir":
                dbg_dump(b, b.h_sh, 0)
            else:
                buf = "ffa" if dump == "gate" else "ffb"
                b.barrier()
                b.selfw(b.v, b.ve)
                b.ve.inc(b.v.memset(sb[buf][:, :], 0.0))
                b.selfw(b.v, b.ve)
                if dump in ("acc0", "accA", "accV"):
                    src = "maxP" if dump == "accV" else "sumP"
                    b.ve.inc(b.v.tensor_copy(sb[buf][0:64, 0:512], sb[src][0:64, :]))
                elif dump in ("stats", "stats0"):
                    b.ve.inc(b.v.tensor_copy(sb[buf][0:64, :], sb["gstats"][:, :]))
                else:
                    b.ve.inc(b.v.tensor_copy(sb[buf][0:16, :], sb["p_fm"][0:16, :]))
                b.w(b.s, b.ve)
                b.qs.inc(b.s.dma_start(out=b.d_dbg, in_=sb[buf][:, :]), 16)
        else:
            p5(b)
            if dump == "s":
                dbg_dump_f32(b, b.h_s, 0)
            else:
                p6(b)
    finish(b)
    nc, t, v, a, g, s = b.nc, b.t, b.v, b.a, b.g, b.s
    with nc.Block() as block:
        @block.sync
        def _(eng):
            s.replay(eng)

        @block.tensor
        def _(eng):
            t.replay(eng)

        @block.vector
        def _(eng):
            v.replay(eng)

        @block.scalar
        def _(eng):
            a.replay(eng)

        @block.gpsimd
        def _(eng):
            g.replay(eng)
    return b


def dbg_dump(b, hsrc, row):
    sb, w = b.sb, b.w
    b.barrier()
    b.qs.inc(b.s.dma_start(out=sb["vrow"][:, :], in_=hsrc[row * 128:(row + 1) * 128, :]), 16)
    w(b.v, b.qs)
    b.selfw(b.v, b.ve)
    b.ve.inc(b.v.tensor_copy(sb["ffb"][:, :], sb["vrow"][:, :]))
    w(b.s, b.ve)
    b.qs.inc(b.s.dma_start(out=b.d_dbg, in_=sb["ffb"][:, :]), 16)


def dbg_dump_sb(b, name, dd):
    sb, w = b.sb, b.w
    b.barrier()
    b.selfw(b.v, b.ve)
    b.ve.inc(b.v.tensor_copy(sb["ffb"][:, :], sb[name][:, dd * T:(dd + 1) * T]))
    w(b.s, b.ve)
    b.qs.inc(b.s.dma_start(out=b.d_dbg, in_=sb["ffb"][:, :]), 16)


def dbg_dump_f32(b, hsrc, row):
    sb, w = b.sb, b.w
    b.barrier()
    b.qs.inc(b.s.dma_start(out=sb["ffa"][:, :], in_=hsrc[row * 128:(row + 1) * 128, :]), 16)
    w(b.s, b.qs)
    b.qs.inc(b.s.dma_start(out=b.d_dbg, in_=sb["ffa"][:, :]), 16)


def inputs_to_maps(inputs):
    ins = {k: np.ascontiguousarray(np.asarray(vv, np.float32))
           for k, vv in inputs.items()}
    nmap = dict(
        wq="L0_attn_Wq", wk="L0_attn_Wk", wv="L0_attn_Wv", wb="L0_attn_Wb",
        conv_q="L0_attn_conv_q", conv_k="L0_attn_conv_k", conv_v="L0_attn_conv_v",
        w1="L0_attn_W1", w2="L0_attn_W2", wo="L0_attn_Wo",
        wg_f="L0_ffn_w_gate", wu_f="L0_ffn_w_up", wd_f="L0_ffn_w_down")
    base = {k: ins[v2] for k, v2 in nmap.items()}
    base["forget"] = ins["L0_attn_forget_param"].reshape(1, H)
    base["log_temp"] = ins["L0_attn_log_temp"].reshape(1, H)
    base["floor"] = np.ascontiguousarray(ins["L0_attn_floor_param"].reshape(H, 4))
    base["b1"] = ins["L0_attn_b1"].reshape(1, 2048)
    base["b2"] = ins["L0_attn_b2"].reshape(1, 16)
    base["norm_w"] = ins["L0_attn_norm_w"].reshape(1, DV)
    base["fir_s"] = np.ascontiguousarray(ins["L0_attn_fir_short_w"].reshape(D, 3))
    base["fir_l"] = np.ascontiguousarray(ins["L0_attn_fir_long_w"].reshape(D, 63))
    x = ins["L0_attn_hidden_states"]
    return [dict(base, x=np.ascontiguousarray(x[i])) for i in range(8)]


def kernel(**inputs):
    b = build()
    maps = inputs_to_maps(inputs)
    res = run_bass_kernel_spmd(b.nc, maps, core_ids=list(range(8)))
    return np.stack([r["y"] for r in res.results], axis=0)


# version 05
# speedup vs baseline: 1.0167x; 1.0086x over previous
import numpy as np
import ml_dtypes
import concourse.bass as bass
import concourse.mybir as mybir
from bass_rust import ReduceOp
from concourse.bass_utils import run_bass_kernel_spmd

F32 = mybir.dt.float32
BF16 = mybir.dt.bfloat16
AF = mybir.ActivationFunctionType
ALU = mybir.AluOpType
AX = mybir.AxisListType

T, D, H, DK, DV, C = 4096, 1024, 4, 256, 256, 32
NCH, NDT, NBLK, NTT = 128, 8, 32, 8
MAGIC = float(np.float32(1.5 * 2.0 ** 23))
FORGET_MIN, MAX_FLOOR = 0.55, 0.075
NI = 24
KB = 1024


class Cnt:
    def __init__(self, es, nc, name):
        self.sem = es.enter_context(nc.semaphore(name))
        self.n = 0

    def inc(self, tok, k=1):
        tok.then_inc(self.sem, k)
        self.n += k
        return self.n


class Rec:
    def __init__(self):
        self.calls = []

    def __getattr__(self, name):
        if name.startswith("_"):
            raise AttributeError(name)
        calls = self.calls

        def call(*a, **k):
            rec = {"m": name, "a": a, "k": k, "incs": []}
            calls.append(rec)

            class Tok:
                def then_inc(self, sem, n=1):
                    rec["incs"].append((sem, n))
                    return self
            return Tok()
        return call

    def replay(self, eng):
        for idx, rec in enumerate(self.calls):
            try:
                instr = getattr(eng, rec["m"])(*rec["a"], **rec["k"])
            except Exception:
                import sys as _s
                print("REPLAY FAIL", rec["m"], "idx", idx,
                      [getattr(x, "tensor", None) and x.tensor.name or str(x)[:40]
                       for x in rec["a"] if hasattr(x, "tensor") or True][:4],
                      file=_s.stderr)
                raise
            for sem, n in rec["incs"]:
                instr.then_inc(sem, n)


class B:
    pass


def mkbuilder(dump):
    import contextlib
    b = B()
    b.dump = dump
    b.nc = nc = bass.Bass("TRN2", target_bir_lowering=False)
    b.es = es = contextlib.ExitStack()
    b.t, b.v, b.a, b.g, b.s = Rec(), Rec(), Rec(), Rec(), Rec()

    b.dx = nc.dram_tensor("x", [T, D], F32, kind="ExternalInput").ap()
    b.dy = nc.dram_tensor("y", [T, D], F32, kind="ExternalOutput").ap()
    b.W = {}
    for nm, shape in [
        ("wq", [D, D]), ("wk", [D, D]), ("wv", [D, D]), ("wb", [D, H]),
        ("forget", [1, H]), ("conv_q", [D, 4]), ("conv_k", [D, 4]),
        ("conv_v", [D, 4]), ("fir_s", [D, 3]), ("fir_l", [D, 63]),
        ("log_temp", [1, H]), ("floor", [H, 4]),
        ("w1", [1088, 2048]), ("b1", [1, 2048]), ("w2", [2048, 16]),
        ("b2", [1, 16]), ("norm_w", [1, DV]), ("wo", [D, D]),
        ("wg_f", [3072, D]), ("wu_f", [3072, D]), ("wd_f", [D, 3072]),
    ]:
        b.W[nm] = nc.dram_tensor(nm, shape, F32, kind="ExternalInput").ap()

    def dram2(name, rows, cols, dt):
        return nc.dram_tensor(name, [rows, cols], dt).ap()
    b.h_xn = dram2("h_xn", NDT * 128, T, BF16)
    b.h_v = dram2("h_v", NDT * 128, T, BF16)
    b.h_df = dram2("h_df", NDT * 128, T, BF16)
    b.h_sh = dram2("h_sh", NDT * 128, T, BF16)
    b.h_lo = dram2("h_lo", NDT * 128, T, BF16)
    b.h_s = dram2("h_s", NDT * 128, T, F32)
    b.h_up = dram2("h_up", NI * 128, T, F32)
    b.h_ga = dram2("h_ga", NI * 128, T, F32)
    b.h_t = dram2("h_t", NI * 128, T, F32)
    b.h_ot = dram2("h_ot", NI * 128, T, F32)
    b.ar_in = [nc.dram_tensor(f"ar_in{i}", [1, 2], F32) for i in range(4)]
    b.ar_out = [nc.dram_tensor(f"ar_out{i}", [1, 2], F32) for i in range(4)]
    b.d_dbg = (nc.dram_tensor("dbg", [128, T], F32, kind="ExternalOutput").ap()
               if dump is not None else None)

    for nm in ["pe", "ve", "ac", "gp", "qs", "qa", "qg", "ccs"]:
        setattr(b, nm, Cnt(es, nc, nm))

    b.sb = {}
    BLOB = 176 * KB
    blob0, _ = nc.bump_sbuf(BLOB)
    b.AQ, b.AK, b.R2 = blob0, blob0 + 64 * KB, blob0 + 128 * KB

    def At(name, shape, dt, off):
        b.sb[name] = nc.alloc_sbuf_tensor_at("sb_" + name, shape, dt, offset=off)
        return b.sb[name]

    def S(name, shape, dt=BF16):
        b.sb[name] = nc.alloc_sbuf_tensor("sb_" + name, shape, dt)
        return b.sb[name]
    b.At, b.S = At, S

    S("ones", [128, 128], BF16); S("onesf", [128, 128], F32)
    S("ident", [128, 128], BF16); S("identf", [128, 128], F32)
    S("sel4", [128, 32], BF16)
    S("selh", [4, 4 * 128], BF16)
    S("selhf", [4, 4 * 128], F32)
    S("negselh", [4, 4 * 128], BF16)
    S("selh16", [16, 16 * 128], BF16)
    S("hsel", [128, 32], BF16)
    S("sumsel", [16, 16], BF16)
    S("sumself", [16, 16], F32)
    S("mask_bdsl", [128, 128], BF16)
    S("mask_triu", [128, 128], BF16)
    S("convw", [128, 128], F32)
    S("firs", [128, 32], F32)
    S("firl", [128, 512], F32)
    S("normw", [128, 2], F32)
    S("b1t", [128, 16], F32)
    S("misc", [4, 64], F32)
    S("lam", [128, 8], F32)
    S("sc", [1, 64], F32)
    S("scb", [128, 24], F32)
    S("lns", [128, 16], F32)
    S("amax", [128, 48], F32)
    S("wb_w", [128, 32], BF16)
    S("w2w", [128, 256], BF16)
    S("rows1", [128, T], BF16)
    S("gstats", [64, T], BF16)
    S("p16", [16, 16], F32)

    At("q", [128, NDT * T], BF16, b.AQ)
    At("k", [128, NDT * T], BF16, b.AK)
    R2 = b.R2
    At("x_t", [128, D], F32, R2)
    At("xn_t", [128, D], BF16, R2 + 4 * KB)
    At("sq_t", [128, D], BF16, R2 + 6 * KB)
    At("xfm", [128, NDT * 512], BF16, R2 + 8 * KB)
    At("xfm2", [128, NDT * 512], BF16, R2 + 16 * KB)
    At("xs", [128, NDT * 512], BF16, R2)
    At("pre", [128, NDT * 520], BF16, R2 + 8 * KB)
    At("wbuf", [128, NDT * D], BF16, R2 + 17 * KB)
    At("sqrow", [128, T], BF16, R2 + 33 * KB)
    At("accA", [128, 512], BF16, R2 + 41 * KB)
    At("accB", [128, 512], BF16, R2 + 42 * KB)
    At("ev1", [128, 512], F32, R2 + 43 * KB)
    At("ev2", [128, 512], F32, R2 + 45 * KB)
    At("S_f", [128, 2 * 1024], F32, R2)
    At("S_b", [128, 2 * 1024], BF16, R2 + 8 * KB)
    At("ktm", [128, 1024], BF16, R2 + 12 * KB)
    At("vtm", [128, 1024], BF16, R2 + 14 * KB)
    At("kbn", [128, 1024], BF16, R2 + 16 * KB)
    At("vblk", [128, 1024], BF16, R2 + 18 * KB)
    At("o_tm", [128, 1024], BF16, R2 + 20 * KB)
    At("z_sb", [128, 1024], BF16, R2 + 22 * KB)
    At("u_sb", [128, 1024], BF16, R2 + 24 * KB)
    At("ttf", [128, 128], BF16, R2 + 26 * KB)
    At("attf", [128, 128], BF16, R2 + 27 * KB)
    At("btm", [128, 8], F32, R2 + 28 * KB)
    At("dfm", [128, NDT * 512], BF16, R2 + 30 * KB)
    At("tmA", [128, 128], BF16, R2 + 44 * KB)
    At("tmB", [128, 128], BF16, R2 + 44 * KB + 256)
    At("ttA", [128, 128], BF16, R2 + 44 * KB + 512)
    At("ttB", [128, 128], BF16, R2 + 44 * KB + 768)
    At("vrow", [128, T], BF16, R2)
    At("srow", [128, T], BF16, R2 + 8 * KB)
    At("lrow", [128, T], BF16, R2 + 16 * KB)
    At("scr", [128, 128], F32, R2 + 24 * KB)
    At("sumP", [128, 512], F32, b.AQ)
    At("sqP", [128, 512], F32, b.AQ + 2 * KB)
    At("maxP", [128, 512], F32, b.AQ + 4 * KB)
    At("statm", [128, 1024], F32, b.AQ + 6 * KB)
    At("sqcb", [128, 1024], BF16, b.AQ + 54 * KB)
    At("gxs", [128, NDT * 512], BF16, R2)
    At("hmid", [128, 16 * 512], BF16, R2 + 8 * KB)
    At("w1b", [128, 9 * 2048], BF16, b.AK)
    At("p_fm", [16, T], BF16, R2 + 24 * KB)
    At("e16", [16, 512], F32, R2 + 32 * KB)
    At("zb16", [16, 512], F32, R2 + 34 * KB)
    At("vtile", [128, NDT * 512], BF16, R2 + 32 * KB)
    At("dtile", [128, NDT * 512], BF16, R2 + 40 * KB)
    At("stile", [128, NDT * 512], BF16, b.AQ)
    At("ltile", [128, NDT * 512], BF16, b.AQ + 8 * KB)
    At("onrm", [128, NDT * 512], BF16, b.AQ + 16 * KB)
    At("wobuf", [128, NDT * D], BF16, b.AQ + 24 * KB)
    At("ocomb", [128, 512], F32, b.AQ + 40 * KB)
    At("ocm2", [128, 512], F32, b.AQ + 42 * KB)
    At("ocm3", [128, 512], F32, b.AQ + 44 * KB)
    At("ev5", [128, 512], F32, b.AQ + 46 * KB)
    At("ev6", [128, 512], F32, b.AQ + 48 * KB)
    At("obuf", [128, 1024], F32, b.AQ + 50 * KB)
    At("ev7", [128, 512], F32, R2 + 8 * KB)
    At("ev8", [128, 512], F32, R2 + 10 * KB)
    At("wut", [128, NDT * 3072], BF16, b.AQ)
    At("wgt", [128, NDT * 3072], BF16, b.AK)
    At("wdt", [128, NI * 1024], BF16, b.AK)
    At("xq_t", [128, NDT * 512], BF16, b.AQ + 48 * KB)
    At("dqs", [128, 1024], F32, b.AQ + 56 * KB)
    At("dqm", [128, 64], F32, b.AQ + 60 * KB)
    At("dqb", [128, 1024], BF16, b.AQ + 62 * KB)
    At("ffa", [128, T], F32, R2)
    At("ffb", [128, T], F32, R2 + 16 * KB)
    At("ffc", [128, T], F32, R2 + 32 * KB)
    At("o2t", [128, 512], BF16, b.AQ)
    At("sgt", [128, 512], F32, b.AQ + 1 * KB)
    At("gt", [128, 512], F32, b.AQ + 3 * KB)
    At("xpt", [128, NDT * 512], F32, b.AQ + 8 * KB)
    At("outf", [128, NDT * 512], F32, b.AQ + 24 * KB)

    b.pp = [nc.alloc_psum_tensor(f"pp{i}", [128, 512], F32) for i in range(6)]
    b.pb = [nc.alloc_psum_tensor(f"pb{i}", [128, 1024], BF16) for i in range(2)]

    def w(rec, *deps):
        for d in deps:
            if d.n > 0:
                rec.wait_ge(d.sem, d.n)
    b.w = w

    def selfw(rec, cnt):
        rec.wait_ge(cnt.sem, cnt.n)
    b.selfw = selfw

    def barrier():
        for rec in (b.t, b.v, b.a, b.g, b.s):
            w(rec, b.pe, b.ve, b.ac, b.gp, b.qs, b.qa, b.qg, b.ccs)
    b.barrier = barrier

    def hview(hbm):
        return hbm.rearrange("(dt p) t -> p dt t", p=128)
    b.hview = hview
    return b


def p0(b):
    t, v, a, g, s = b.t, b.v, b.a, b.g, b.s
    sb, pp, W, w = b.sb, b.pp, b.W, b.w
    pe, ve, ac, gp, qs, qg = b.pe, b.ve, b.ac, b.gp, b.qs, b.qg
    g.memset(sb["ones"][:, :], 1.0)
    g.memset(sb["onesf"][:, :], 1.0)
    g.memset(sb["amax"][:, :], 0.0)
    g.memset(sb["pre"][:, :], 0.0)
    gp.inc(g.memset(sb["lns"][:, :], 0.0))
    w(g, gp)
    g.affine_select(sb["identf"][:, :], sb["onesf"][:, :], pattern=[[-1, 128]],
                    compare_op=ALU.is_equal, fill=0.0, base=0,
                    channel_multiplier=1)
    bd4 = sb["mask_bdsl"][:, :].rearrange("p (hf cf) -> p hf cf", cf=32)
    on4 = sb["ones"][:, :].rearrange("p (hf cf) -> p hf cf", cf=32)
    gp.inc(g.affine_select(bd4, on4, pattern=[[-32, 4], [-1, 32]],
                           compare_op=ALU.is_ge, fill=0.0, base=-1,
                           channel_multiplier=1))
    w(g, gp)
    gp.inc(g.affine_select(bd4, bd4, pattern=[[-32, 4], [0, 32]],
                           compare_op=ALU.is_ge, fill=0.0, base=0,
                           channel_multiplier=1))
    w(g, gp)
    gp.inc(g.affine_select(bd4, bd4, pattern=[[32, 4], [0, 32]],
                           compare_op=ALU.is_ge, fill=0.0, base=31,
                           channel_multiplier=-1))
    w(g, gp)
    for r0 in range(0, 128, 32):
        mt4 = sb["mask_triu"][r0:r0 + 32, :].rearrange("p (hf cf) -> p hf cf", cf=32)
        on32 = sb["ones"][0:32, :].rearrange("p (hf cf) -> p hf cf", cf=32)
        gp.inc(g.affine_select(mt4, on32, pattern=[[0, 4], [1, 32]],
                               compare_op=ALU.is_ge, fill=0.0, base=0,
                               channel_multiplier=-1))
        w(g, gp)
    gp.inc(g.memset(sb["sel4"][:, :], 0.0))
    gp.inc(g.memset(sb["selh"][:, :], 0.0))
    gp.inc(g.memset(sb["selh16"][:, :], 0.0))
    gp.inc(g.memset(sb["hsel"][:, :], 0.0))
    gp.inc(g.memset(sb["sumsel"][:, :], 0.0))
    w(v, gp)
    ve.inc(v.tensor_copy(sb["ident"][:, :], sb["identf"][:, :]))
    w(g, ve)
    for h in range(H):
        qg.inc(g.dma_start(out=sb["sel4"][32 * h:32 * h + 32, :],
                           in_=sb["ident"][0:32, 0:32]), 16)
        qg.inc(g.dma_start(out=sb["selh"][h:h + 1, 128 * h:128 * h + 128],
                           in_=sb["ones"][0:1, 0:128]), 16)
        qg.inc(g.dma_start(out=sb["hsel"][0:128, h * 4 + h: h * 4 + h + 1],
                           in_=sb["ones"][0:128, 0:1]), 16)
    for j in range(16):
        qg.inc(g.dma_start(out=sb["selh16"][j:j + 1, 128 * j:128 * j + 128],
                           in_=sb["ones"][0:1, 0:128]), 16)
        h = j // 4
        qg.inc(g.dma_start(out=sb["sumsel"][4 * h:4 * h + 4, j:j + 1],
                           in_=sb["ones"][0:4, 0:1]), 16)

    w(v, qg)
    b.selfw(v, ve)
    ve.inc(v.tensor_copy(sb["sumself"][:, :], sb["sumsel"][:, :]))
    b.selfw(v, ve)
    ve.inc(v.tensor_copy(sb["selhf"][:, :], sb["selh"][:, :]))
    b.selfw(v, ve)
    ve.inc(v.tensor_scalar(sb["negselh"][:, :], sb["selh"][:, :], -1.0, None, ALU.mult))
    for j, nm in enumerate(["conv_q", "conv_k", "conv_v"]):
        for dd in range(NDT):
            qs.inc(s.dma_start(
                out=sb["convw"][0:128, dd * 16 + j * 4: dd * 16 + j * 4 + 4],
                in_=W[nm][dd * 128:(dd + 1) * 128, :]), 16)
    for dd in range(NDT):
        qs.inc(s.dma_start(out=sb["firs"][0:128, dd * 4: dd * 4 + 3],
                           in_=W["fir_s"][dd * 128:(dd + 1) * 128, :]), 16)
        qs.inc(s.dma_start(out=sb["firl"][0:128, dd * 64: dd * 64 + 63],
                           in_=W["fir_l"][dd * 128:(dd + 1) * 128, :]), 16)
    qs.inc(s.dma_start(out=sb["normw"][0:128, 0:1],
                       in_=W["norm_w"][0:1, 0:128].rearrange("a b -> b a")), 16)
    qs.inc(s.dma_start(out=sb["normw"][0:128, 1:2],
                       in_=W["norm_w"][0:1, 128:256].rearrange("a b -> b a")), 16)
    for mt in range(16):
        qs.inc(s.dma_start(
            out=sb["b1t"][0:128, mt:mt + 1],
            in_=W["b1"][0:1, mt * 128:(mt + 1) * 128].rearrange("a b -> b a")), 16)
    qs.inc(s.dma_start(out=sb["misc"][0:1, 0:4], in_=W["forget"]), 16)
    qs.inc(s.dma_start(out=sb["misc"][0:1, 4:8], in_=W["log_temp"]), 16)
    qs.inc(s.dma_start(out=sb["misc"][0:4, 8:12], in_=W["floor"]), 16)
    qs.inc(s.dma_start(out=sb["misc"][0:1, 12:28], in_=W["b2"]), 16)
    qg.inc(g.dma_start(out=sb["wb_w"][:, :].rearrange("p (kt h) -> p kt h", h=H),
                       in_=W["wb"].rearrange("(kt p) h -> p kt h", p=128)), 16)
    qg.inc(g.dma_start(out=sb["w2w"][:, :].rearrange("p (kt h) -> p kt h", h=16),
                       in_=W["w2"].rearrange("(kt p) h -> p kt h", p=128)), 16)
    w(a, qs)
    ac.inc(a.activation(sb["misc"][0:1, 32:36], sb["misc"][0:1, 0:4], AF.Sigmoid))
    w(v, ac)
    b.selfw(v, ve)
    ve.inc(v.tensor_scalar(sb["misc"][0:1, 32:36], sb["misc"][0:1, 32:36],
                           1.0 - FORGET_MIN, FORGET_MIN, ALU.mult, ALU.add))
    w(a, ve)
    b.selfw(a, ac)
    ac.inc(a.activation(sb["misc"][0:1, 36:40], sb["misc"][0:1, 4:8], AF.Exp))
    w(v, ac)
    b.selfw(v, ve)
    ve.inc(v.reciprocal(sb["misc"][0:1, 40:44], sb["misc"][0:1, 36:40]))
    w(a, ve)
    b.selfw(a, ac)
    ac.inc(a.activation(sb["misc"][0:4, 44:48], sb["misc"][0:4, 8:12], AF.Sigmoid))
    w(v, ac)
    b.selfw(v, ve)
    ve.inc(v.tensor_scalar(sb["misc"][0:4, 44:48], sb["misc"][0:4, 44:48],
                           MAX_FLOOR, None, ALU.mult))
    w(t, ve, gp)
    pe.inc(t.matmul(pp[2][:, 0:4], sb["onesf"][0:1, 0:128],
                    sb["misc"][0:1, 32:36], start=True, stop=True))
    w(v, pe)
    b.selfw(v, ve)
    ve.inc(v.tensor_copy(sb["lam"][:, 0:4], pp[2][:, 0:4]))
    w(t, ve)
    pe.inc(t.transpose(pp[3][0:16, 0:1], sb["misc"][0:1, 12:28], sb["identf"][0:1, 0:1]))
    w(v, pe)
    b.selfw(v, ve)
    ve.inc(v.tensor_copy(sb["p16"][0:16, 0:1], pp[3][0:16, 0:1]))
    w(g, ve, ac)
    b.selfw(g, gp)
    for h in range(4):
        for j in range(4):
            qg.inc(g.dma_start(out=sb["p16"][4 * h + j:4 * h + j + 1, 1:2],
                               in_=sb["misc"][0:1, 40 + h:41 + h]), 16)
            qg.inc(g.dma_start(out=sb["p16"][4 * h + j:4 * h + j + 1, 2:3],
                               in_=sb["misc"][h:h + 1, 44 + j:45 + j]), 16)


def p1(b):
    t, v, a, g, s = b.t, b.v, b.a, b.g, b.s
    sb, pp, w = b.sb, b.pp, b.w
    pe, ve, ac, gp, qs, qg, ccs = b.pe, b.ve, b.ac, b.gp, b.qs, b.qg, b.ccs
    for blk in range(NBLK):
        w(s, ac, ve)
        qs.inc(s.dma_start(out=sb["x_t"][:, :], in_=b.dx[blk * 128:(blk + 1) * 128, :]), 16)
        w(a, qs)
        b.selfw(a, ac)
        w(a, ve)
        ac.inc(a.activation(sb["sq_t"][:, :], sb["x_t"][:, :], AF.Square,
                            accum_out=sb["lns"][:, 0:1]))
        w(v, qs)
        b.selfw(v, ve)
        ve.inc(v.reduce_sum(sb["lns"][:, 1:2], sb["x_t"][:, :], AX.X))
        w(v, ac)
        b.selfw(v, ve)
        ve.inc(v.tensor_scalar(sb["lns"][:, 2:3], sb["lns"][:, 1:2], 1.0 / D, None, ALU.mult))
        ve.inc(v.tensor_scalar(sb["lns"][:, 3:4], sb["lns"][:, 0:1], 1.0 / D, None, ALU.mult))
        b.selfw(v, ve)
        ve.inc(v.tensor_mul(sb["lns"][:, 4:5], sb["lns"][:, 2:3], sb["lns"][:, 2:3]))
        b.selfw(v, ve)
        ve.inc(v.tensor_sub(sb["lns"][:, 5:6], sb["lns"][:, 3:4], sb["lns"][:, 4:5]))
        b.selfw(v, ve)
        ve.inc(v.tensor_scalar(sb["lns"][:, 6:7], sb["lns"][:, 5:6], 1e-5, None, ALU.add))
        b.selfw(v, ve)
        ve.inc(v.reciprocal(sb["lns"][:, 7:8], sb["lns"][:, 6:7]))
        w(a, ve)
        b.selfw(a, ac)
        ac.inc(a.activation(sb["lns"][:, 8:9], sb["lns"][:, 7:8], AF.Sqrt))
        w(v, ve, pe, ac)
        b.selfw(v, ve)
        ve.inc(v.tensor_scalar(sb["lns"][:, 10:11], sb["lns"][:, 2:3], -1.0,
                               None, ALU.mult))
        b.selfw(v, ve)
        ve.inc(v.tensor_scalar(sb["xn_t"][:, :], sb["x_t"][:, :],
                               sb["lns"][:, 10:11], sb["lns"][:, 8:9],
                               ALU.add, ALU.mult))
        b.selfw(v, ve)
        ve.inc(v.tensor_reduce(sb["amax"][:, blk:blk + 1], sb["xn_t"][:, :],
                               AX.X, ALU.max, apply_absolute_value=True))
        xf = sb["xfm"] if (blk // 4) % 2 == 0 else sb["xfm2"]
        for dd in range(NDT):
            bank = b.pb[dd % 2]
            w(t, ac, ve)
            pe.inc(t.transpose(bank[:, 0:128], sb["xn_t"][:, dd * 128:(dd + 1) * 128],
                               sb["ident"][:, :]))
            if dd % 2 == 0:
                w(a, pe, qs)
                b.selfw(a, ac)
                ac.inc(a.copy(
                    xf[:, dd * 512 + (blk % 4) * 128: dd * 512 + (blk % 4) * 128 + 128],
                    bank[:, 0:128]))
            else:
                w(v, pe, qs)
                b.selfw(v, ve)
                ve.inc(v.tensor_copy(
                    xf[:, dd * 512 + (blk % 4) * 128: dd * 512 + (blk % 4) * 128 + 128],
                    bank[:, 0:128]))
        if blk % 4 == 3:
            tt = blk // 4
            for dd in range(NDT):
                w(s, ve, ac)
                qs.inc(s.dma_start(
                    out=b.h_xn[dd * 128:(dd + 1) * 128, tt * 512:(tt + 1) * 512],
                    in_=xf[:, dd * 512:(dd + 1) * 512]), 16)
    b.selfw(v, ve)
    ve.inc(v.tensor_reduce(sb["amax"][:, 32:33], sb["amax"][:, 0:32], AX.X, ALU.max))
    w(t, ve)
    pe.inc(t.transpose(pp[2][0:1, 0:128], sb["amax"][:, 32:33], sb["identf"][:, :]))
    w(v, pe)
    b.selfw(v, ve)
    ve.inc(v.tensor_reduce(sb["sc"][0:1, 1:2], pp[2][0:1, 0:128], AX.X, ALU.max))
    b.selfw(v, ve)
    ve.inc(v.tensor_copy(sb["sc"][0:1, 2:3], sb["sc"][0:1, 1:2]))
    w(g, ve)
    qg.inc(g.dma_start(out=b.ar_in[0].ap()[0:1, 0:2], in_=sb["sc"][0:1, 1:3]), 16)
    w(g, qg)
    ccs.inc(g.collective_compute("AllReduce", ALU.max,
                                 replica_groups=[list(range(8))],
                                 ins=[b.ar_in[0].ap()], outs=[b.ar_out[0].ap()]))
    w(g, ccs)
    qg.inc(g.dma_start(out=sb["sc"][0:1, 4:6], in_=b.ar_out[0].ap()[0:1, 0:2]), 16)
    w(v, qg)
    b.selfw(v, ve)
    ve.inc(v.tensor_scalar(sb["sc"][0:1, 6:7], sb["sc"][0:1, 4:5],
                           1.0 / 32767.0, 1e-12, ALU.mult, ALU.max))
    b.selfw(v, ve)
    ve.inc(v.reciprocal(sb["sc"][0:1, 7:8], sb["sc"][0:1, 6:7]))
    w(t, ve)
    pe.inc(t.matmul(pp[3][:, 0:2], sb["onesf"][0:1, 0:128], sb["sc"][0:1, 6:8],
                    start=True, stop=True))
    w(v, pe)
    b.selfw(v, ve)
    ve.inc(v.tensor_copy(sb["scb"][:, 0:2], pp[3][:, 0:2]))


def p3(b):
    t, v, a, g, s = b.t, b.v, b.a, b.g, b.s
    sb, pp, W, w = b.sb, b.pp, b.W, b.w
    pe, ve, ac, gp, qs, qa, qg = b.pe, b.ve, b.ac, b.gp, b.qs, b.qa, b.qg

    def sweep(wname, convj, dest):
        b.barrier()
        qg.inc(g.dma_start(out=sb["wbuf"][:, :].rearrange("p (kt c) -> p kt c", c=D),
                           in_=W[wname].rearrange("(kt p) c -> p kt c", p=128)), 16)
        w(g, ve, ac, pe, qs, qa)
        b.selfw(g, gp)
        gp.inc(g.memset(sb["pre"][:, :], 0.0))
        w(v, gp)
        w(a, gp)
        w(t, gp)
        bankseq = {}
        need = {}
        for tt in range(NTT):
            w(s, pe)
            qs.inc(s.dma_start(out=sb["xs"][:, :],
                               in_=b.hview(b.h_xn)[:, :, tt * 512:(tt + 1) * 512]), 16)
            for mt in range(NDT):
                bA = (mt % 2) * 2
                w(t, qs, qg)
                if bA in need:
                    t.wait_ge(ve.sem, need[bA][0])
                    t.wait_ge(ac.sem, need[bA][1])
                for kt in range(NDT):
                    bk = bA + kt % 2
                    if bk in bankseq:
                        t.wait_ge(pe.sem, bankseq[bk])
                    tok = t.matmul(
                        pp[bk][:, 0:512],
                        sb["wbuf"][:, kt * D + mt * 128: kt * D + mt * 128 + 128],
                        sb["xs"][:, :].rearrange("p (dt c) -> p dt c", dt=NDT)[:, kt, :],
                        start=kt < 2, stop=kt >= 6)
                    bankseq[bk] = b.pe.inc(tok)
                base = mt * 520
                bA, bB = (mt % 2) * 2, (mt % 2) * 2 + 1
                w(v, ac)
                b.selfw(v, ve)
                ve.inc(v.tensor_copy(sb["pre"][:, base + 0: base + 3],
                                     sb["pre"][:, base + 512: base + 515]))
                w(a, pe, ve)
                b.selfw(a, ac)
                ac.inc(a.copy(sb["ev1"][:, :], pp[bA][:, 0:512]))
                w(v, ac, pe)
                b.selfw(v, ve)
                ve.inc(v.tensor_add(sb["pre"][:, base + 3: base + 515],
                                    pp[bB][:, 0:512], sb["ev1"][:, :]))
                need[bA] = (ve.n, ac.n)
                need[bB] = (ve.n, ac.n)
                cw = sb["convw"]
                w(v, qa)
                b.selfw(v, ve)
                ve.inc(v.tensor_scalar(
                    sb["accA"][:, :], sb["pre"][:, base + 0: base + 512],
                    cw[:, mt * 16 + convj * 4 + 0: mt * 16 + convj * 4 + 1],
                    None, ALU.mult))
                for tap in range(1, 4):
                    src, dst = ("accA", "accB") if tap % 2 == 1 else ("accB", "accA")
                    b.selfw(v, ve)
                    ve.inc(v.scalar_tensor_tensor(
                        sb[dst][:, :], sb["pre"][:, base + tap: base + tap + 512],
                        cw[:, mt * 16 + convj * 4 + tap: mt * 16 + convj * 4 + tap + 1],
                        sb[src][:, :], ALU.mult, ALU.add))
                w(a, ve)
                b.selfw(a, ac)
                if dest == "v":
                    w(a, qa)
                    ac.inc(a.activation(sb["accA"][:, :], sb["accB"][:, :], AF.Silu))
                    w(a, ac)
                    qa.inc(a.dma_start(
                        out=b.h_v[mt * 128:(mt + 1) * 128, tt * 512:(tt + 1) * 512],
                        in_=sb["accA"][:, :]), 16)
                else:
                    ac.inc(a.activation(
                        sb[dest][:, mt * T + tt * 512: mt * T + (tt + 1) * 512],
                        sb["accB"][:, :], AF.Silu))

    sweep("wv", 2, "v")
    sweep("wk", 1, "k")
    sweep("wq", 0, "q")

    b.barrier()
    for tt in range(NTT):
        w(s, pe, ac)
        qs.inc(s.dma_start(out=sb["xs"][:, :],
                           in_=b.hview(b.h_xn)[:, :, tt * 512:(tt + 1) * 512]), 16)
        last = None
        for kt in range(NDT):
            w(t, qs, ve, ac)
            if last is not None:
                t.wait_ge(pe.sem, last)
            tok = t.matmul(pp[4][0:4, 0:512], sb["wb_w"][:, kt * 4:(kt + 1) * 4],
                           sb["xs"][:, :].rearrange("p (dt c) -> p dt c", dt=NDT)[:, kt, :],
                           start=kt == 0, stop=kt == 7)
            last = pe.inc(tok)
        w(a, pe)
        b.selfw(a, ac)
        ac.inc(a.activation(sb["rows1"][0:4, tt * 512:(tt + 1) * 512],
                            pp[4][0:4, 0:512], AF.Sigmoid))


    def l2pass(src, rowbase):
        b.barrier()
        for tt in range(NTT):
            last = None
            for dd in range(NDT):
                h = dd // 2
                w(a, pe, ve)
                b.selfw(a, ac)
                ac.inc(a.activation(sb["sqrow"][:, 0:512],
                                    sb[src][:, dd * T + tt * 512: dd * T + (tt + 1) * 512],
                                    AF.Square))
                w(t, ac)
                if last is not None:
                    t.wait_ge(pe.sem, last)
                last = pe.inc(t.matmul(pp[4][0:4, 0:512], sb["hsel"][:, h * 4: h * 4 + 4],
                                       sb["sqrow"][:, 0:512],
                                       start=dd == 0, stop=dd == 7))
            w(v, pe)
            b.selfw(v, ve)
            ve.inc(v.tensor_scalar(sb["ev1"][0:4, 0:512], pp[4][0:4, 0:512],
                                   1e-6, None, ALU.add))
            b.selfw(v, ve)
            ve.inc(v.reciprocal(sb["ev1"][0:4, 0:512], sb["ev1"][0:4, 0:512]))
            w(a, ve)
            b.selfw(a, ac)
            ac.inc(a.activation(sb["ev1"][0:4, 0:512], sb["ev1"][0:4, 0:512], AF.Sqrt))
            w(t, ac)
            tp = (0, rowbase) if rowbase == 96 else None
            pe.inc(t.matmul(pp[5][rowbase:rowbase + 4, 0:512],
                            sb["identf"][0:4, 0:4],
                            sb["ev1"][0:4, 0:512], start=True, stop=True,
                            tile_position=tp))
            w(v, pe)
            b.selfw(v, ve)
            ve.inc(v.tensor_copy(sb["rows1"][rowbase:rowbase + 4, tt * 512:(tt + 1) * 512],
                                 pp[5][rowbase:rowbase + 4, 0:512]))
            for h in range(4):
                bk = pp[2 + h % 2]
                w(t, ve)
                pe.inc(t.matmul(bk[:, 0:512], sb["selhf"][0:4, h * 128:(h + 1) * 128],
                                sb["ev1"][0:4, 0:512], start=True, stop=True))
                for dd in (2 * h, 2 * h + 1):
                    w(v, pe)
                    b.selfw(v, ve)
                    ve.inc(v.tensor_mul(
                        sb[src][:, dd * T + tt * 512: dd * T + (tt + 1) * 512],
                        sb[src][:, dd * T + tt * 512: dd * T + (tt + 1) * 512],
                        bk[:, 0:512]))

    l2pass("q", 64)
    l2pass("k", 96)


def p4(b, nblk=NBLK):
    t, v, a, g, s = b.t, b.v, b.a, b.g, b.s
    sb, pp, w = b.sb, b.pp, b.w
    pe, ve, ac, gp, qs, qg = b.pe, b.ve, b.ac, b.gp, b.qs, b.qg
    b.barrier()
    g.memset(sb["S_f"][:, :], 0.0)
    gp.inc(g.memset(sb["S_b"][:, :], 0.0))
    w(t, gp)

    for b4 in range(nblk):
        c0b = b4 * 128
        w(s, pe, ve, ac)
        qs.inc(s.dma_start(out=sb["vblk"][:, :],
                           in_=b.hview(b.h_v)[:, :, c0b:c0b + 128]), 16)
        w(t, qs, ve, ac)
        pe.inc(t.transpose(b.pb[0][0:128, 0:4], sb["rows1"][0:4, c0b:c0b + 128],
                           sb["ident"][0:4, 0:4]))
        w(v, pe)
        b.selfw(v, ve)
        ve.inc(v.tensor_copy(sb["btm"][:, 0:4], b.pb[0][0:128, 0:4]))
        w(t, ve, ac)
        lastk = None
        for dd in range(NDT):
            if lastk is not None:
                t.wait_ge(pe.sem, lastk)
            lastk = pe.inc(t.transpose(b.pb[0][:, dd * 128:(dd + 1) * 128],
                                       sb["k"][:, dd * T + c0b: dd * T + c0b + 128],
                                       sb["ident"][:, :]))
        w(a, pe)
        b.selfw(a, ac)
        ac.inc(a.copy(sb["ktm"][:, :], b.pb[0][:, :]))
        w(t, qs)
        lastv = None
        for dd in range(NDT):
            if lastv is not None:
                t.wait_ge(pe.sem, lastv)
            lastv = pe.inc(t.transpose(b.pb[1][:, dd * 128:(dd + 1) * 128],
                                       sb["vblk"][:, dd * 128:(dd + 1) * 128],
                                       sb["ident"][:, :]))
        w(v, pe)
        for dd in range(NDT):
            h = dd // 2
            b.selfw(v, ve)
            ve.inc(v.tensor_scalar(sb["vtm"][:, dd * 128:(dd + 1) * 128],
                                   b.pb[1][:, dd * 128:(dd + 1) * 128],
                                   sb["btm"][:, h:h + 1],
                                   None, ALU.mult))
        for h in range(4):
            bank = pp[4 + h % 2]
            w(t, ve)
            pe.inc(t.matmul(bank[:, 0:128], sb["negselh"][0:4, h * 128:(h + 1) * 128],
                            sb["rows1"][0:4, c0b:c0b + 128], start=True, stop=True))
            for dd in (2 * h, 2 * h + 1):
                w(v, pe)
                b.selfw(v, ve)
                ve.inc(v.tensor_mul(sb["kbn"][:, dd * 128:(dd + 1) * 128],
                                    sb["k"][:, dd * T + c0b: dd * T + c0b + 128],
                                    bank[:, 0:128]))
        for ci in range(4):
            c0 = ci * 32
            tpA = (0, c0) if c0 == 96 else None
            tpB = (96, 96) if c0 == 96 else None
            tpC = (96, 0) if c0 == 96 else None
            tok0 = c0b + ci * 32
            lastg = None
            w(t, ve, ac)
            for h in range(4):
                for dkt in range(2):
                    dd = 2 * h + dkt
                    if lastg is not None and dkt == 1:
                        t.wait_ge(pe.sem, lastg)
                    tok = t.matmul(
                        pp[2][32 * h:32 * h + 32, 32 * h:32 * h + 32],
                        sb["kbn"][:, dd * 128 + ci * 32: dd * 128 + ci * 32 + 32],
                        sb["k"][:, dd * T + tok0: dd * T + tok0 + 32],
                        start=dkt == 0, stop=dkt == 1,
                        tile_position=(0, 32 * h) if h == 3 else None)
                    lastg = pe.inc(tok)
            w(v, pe)
            b.selfw(v, ve)
            ve.inc(v.scalar_tensor_tensor(sb["tmA"][:, :], pp[2][:, 0:128], 1.0,
                                          sb["mask_bdsl"][:, :], ALU.mult, ALU.mult))
            b.selfw(v, ve)
            ve.inc(v.transpose(sb["tmB"][:, :], sb["tmA"][:, :]))
            b.selfw(v, ve)
            ve.inc(v.tensor_add(sb["ttA"][:, :], sb["tmB"][:, :], sb["ident"][:, :]))
            ttc, ttn = "ttA", "ttB"
            for j in range(4):
                w(t, ve, ac)
                pe.inc(t.matmul(pp[3][:, 0:128], sb["tmB"][:, :], sb["tmA"][:, :],
                                start=True, stop=True))
                w(a, pe)
                b.selfw(a, ac)
                ac.inc(a.copy(sb["tmA"][:, :], pp[3][:, 0:128]))
                w(v, ac)
                b.selfw(v, ve)
                ve.inc(v.transpose(sb["tmB"][:, :], sb["tmA"][:, :]))
                w(t, ac, ve)
                t.wait_ge(pe.sem, pe.n)
                pe.inc(t.matmul(pp[3][:, 128:256], sb["tmA"][:, :], sb[ttc][:, :],
                                start=True, stop=True))
                w(v, pe)
                b.selfw(v, ve)
                ve.inc(v.scalar_tensor_tensor(sb[ttn][:, :], pp[3][:, 128:256], 1.0,
                                              sb[ttc][:, :], ALU.mult, ALU.add))
                ttc, ttn = ttn, ttc
            w(t, ve)
            pe.inc(t.matmul(pp[5][c0:c0 + 32, 0:128], sb["sel4"][:, :], sb[ttc][:, :],
                            start=True, stop=True, tile_position=tpA))
            w(a, pe)
            b.selfw(a, ac)
            ac.inc(a.copy(sb["ttf"][c0:c0 + 32, :], pp[5][c0:c0 + 32, 0:128]))
            lasta = None
            w(t, ac, ve)
            for h in range(4):
                for dkt in range(2):
                    dd = 2 * h + dkt
                    if lasta is not None:
                        t.wait_ge(pe.sem, lasta)
                    tok = t.matmul(
                        pp[4][c0:c0 + 32, 32 * h:32 * h + 32],
                        sb["k"][:, dd * T + tok0: dd * T + tok0 + 32],
                        sb["q"][:, dd * T + tok0: dd * T + tok0 + 32],
                        start=dkt == 0, stop=dkt == 1, tile_position=tpA)
                    lasta = pe.inc(tok)
            w(v, pe)
            b.selfw(v, ve)
            ve.inc(v.scalar_tensor_tensor(sb["attf"][c0:c0 + 32, :],
                                          pp[4][c0:c0 + 32, 0:128], 1.0,
                                          sb["mask_triu"][c0:c0 + 32, :],
                                          ALU.mult, ALU.mult))
            zl = {}
            w(t, ve, ac)
            for h in range(4):
                for dkt in range(2):
                    dd = 2 * h + dkt
                    bank = pp[0] if h < 2 else pp[1]
                    fo = (h % 2) * 256
                    key = (h < 2)
                    if key in zl:
                        t.wait_ge(pe.sem, zl[key])
                    tok = t.matmul(bank[c0:c0 + 32, fo:fo + 256],
                                   sb["kbn"][:, dd * 128 + ci * 32: dd * 128 + ci * 32 + 32],
                                   sb["S_b"][:, dkt * 1024 + h * 256: dkt * 1024 + (h + 1) * 256],
                                   start=dkt == 0, stop=dkt == 1, tile_position=tpA)
                    zl[key] = pe.inc(tok)
            w(v, pe)
            b.selfw(v, ve)
            ve.inc(v.scalar_tensor_tensor(sb["z_sb"][c0:c0 + 32, 0:512],
                                          pp[0][c0:c0 + 32, 0:512], 1.0,
                                          sb["vtm"][c0:c0 + 32, 0:512],
                                          ALU.mult, ALU.add))
            b.selfw(v, ve)
            ve.inc(v.scalar_tensor_tensor(sb["z_sb"][c0:c0 + 32, 512:1024],
                                          pp[1][c0:c0 + 32, 0:512], 1.0,
                                          sb["vtm"][c0:c0 + 32, 512:1024],
                                          ALU.mult, ALU.add))
            w(t, ve)
            for h in range(4):
                bank = pp[4] if h < 2 else pp[5]
                fo = (h % 2) * 256
                t.wait_ge(pe.sem, pe.n)
                pe.inc(t.matmul(bank[c0:c0 + 32, fo:fo + 256],
                                sb["ttf"][c0:c0 + 32, 32 * h:32 * h + 32],
                                sb["z_sb"][c0:c0 + 32, h * 256:(h + 1) * 256],
                                start=True, stop=True, tile_position=tpB))
            w(v, pe)
            b.selfw(v, ve)
            ve.inc(v.tensor_copy(sb["u_sb"][c0:c0 + 32, 0:512], pp[4][c0:c0 + 32, 0:512]))
            b.selfw(v, ve)
            ve.inc(v.tensor_copy(sb["u_sb"][c0:c0 + 32, 512:1024], pp[5][c0:c0 + 32, 0:512]))
            ol = {}
            w(t, ve)
            for h in range(4):
                bank = pp[0] if h < 2 else pp[1]
                fo = (h % 2) * 256
                for dkt in range(2):
                    dd = 2 * h + dkt
                    key = (h < 2)
                    if key in ol:
                        t.wait_ge(pe.sem, ol[key])
                    ol[key] = pe.inc(t.matmul(
                        bank[c0:c0 + 32, fo:fo + 256],
                        sb["q"][:, dd * T + tok0: dd * T + tok0 + 32],
                        sb["S_b"][:, dkt * 1024 + h * 256: dkt * 1024 + (h + 1) * 256],
                        start=dkt == 0, stop=dkt == 1, tile_position=tpA))
                bank2 = pp[4] if h < 2 else pp[5]
                t.wait_ge(pe.sem, pe.n)
                ol[("a", h < 2)] = pe.inc(t.matmul(
                    bank2[c0:c0 + 32, fo:fo + 256],
                    sb["attf"][c0:c0 + 32, 32 * h:32 * h + 32],
                    sb["u_sb"][c0:c0 + 32, h * 256:(h + 1) * 256],
                    start=True, stop=True, tile_position=tpB))
            w(v, pe)
            b.selfw(v, ve)
            ve.inc(v.tensor_copy(sb["o_tm"][c0:c0 + 32, 0:512], pp[4][c0:c0 + 32, 0:512]))
            b.selfw(v, ve)
            ve.inc(v.tensor_copy(sb["o_tm"][c0:c0 + 32, 512:1024], pp[5][c0:c0 + 32, 0:512]))
            b.selfw(v, ve)
            ve.inc(v.scalar_tensor_tensor(sb["o_tm"][c0:c0 + 32, 0:512],
                                          pp[0][c0:c0 + 32, 0:512], 1.0,
                                          sb["o_tm"][c0:c0 + 32, 0:512],
                                          ALU.mult, ALU.add))
            b.selfw(v, ve)
            ve.inc(v.scalar_tensor_tensor(sb["o_tm"][c0:c0 + 32, 512:1024],
                                          pp[1][c0:c0 + 32, 0:512], 1.0,
                                          sb["o_tm"][c0:c0 + 32, 512:1024],
                                          ALU.mult, ALU.add))
            for wave in range(2):
                w(t, ve, ac)
                for i2 in range(4):
                    h = wave * 2 + i2 // 2
                    dkt = i2 % 2
                    bank = pp[2 + i2]
                    t.wait_ge(pe.sem, pe.n)
                    pe.inc(t.matmul(
                        bank[0:128, 0:256],
                        sb["ktm"][c0:c0 + 32, (2 * h + dkt) * 128:(2 * h + dkt + 1) * 128],
                        sb["u_sb"][c0:c0 + 32, h * 256:(h + 1) * 256],
                        start=True, stop=True, tile_position=tpC))
                for i2 in range(4):
                    h = wave * 2 + i2 // 2
                    dkt = i2 % 2
                    bank = pp[2 + i2]
                    sl = slice(dkt * 1024 + h * 256, dkt * 1024 + (h + 1) * 256)
                    w(v, pe)
                    b.selfw(v, ve)
                    ve.inc(v.scalar_tensor_tensor(sb["S_f"][:, sl], sb["S_f"][:, sl],
                                                  sb["lam"][:, h:h + 1],
                                                  bank[0:128, 0:256],
                                                  ALU.mult, ALU.add))
                    w(a, ve)
                    b.selfw(a, ac)
                    ac.inc(a.copy(sb["S_b"][:, sl], sb["S_f"][:, sl]))
        w(t, ve, ac)
        lasto = None
        for dd in range(NDT):
            if lasto is not None:
                t.wait_ge(pe.sem, lasto)
            lasto = pe.inc(t.transpose(b.pb[0][:, dd * 128:(dd + 1) * 128],
                                       sb["o_tm"][:, dd * 128:(dd + 1) * 128],
                                       sb["ident"][:, :]))
        w(v, pe, qs)
        b.selfw(v, ve)
        ve.inc(v.tensor_copy(
            sb["dfm"][:, :].rearrange("p (dt blk c) -> p dt blk c",
                                      blk=4, c=128)[:, :, b4 % 4, :],
            b.pb[0][:, :].rearrange("p (dt c) -> p dt c", c=128)))
        if b4 % 4 == 3:
            tt = b4 // 4
            for dd in range(NDT):
                w(s, ve, ac)
                qs.inc(s.dma_start(
                    out=b.h_df[dd * 128:(dd + 1) * 128, tt * 512:(tt + 1) * 512],
                    in_=sb["dfm"][:, dd * 512:(dd + 1) * 512]), 16)


def p5(b, upto="all"):
    t, v, a, g, s = b.t, b.v, b.a, b.g, b.s
    sb, pp, W, w = b.sb, b.pp, b.W, b.w
    pe, ve, ac, gp, qs, qa, qg = b.pe, b.ve, b.ac, b.gp, b.qs, b.qa, b.qg

    b.barrier()
    for dd in range(NDT):
        w(s, ve, ac, qa)
        qs.inc(s.dma_start(out=sb["vrow"][:, :], in_=b.h_v[dd * 128:(dd + 1) * 128, :]), 16)
        w(v, qs, qa)
        b.selfw(v, ve)
        ve.inc(v.tensor_scalar(sb["srow"][:, 0:T], sb["vrow"][:, 0:T],
                               sb["firs"][:, dd * 4 + 2: dd * 4 + 3], None, ALU.mult))
        for tap in range(2):
            sh = 2 - tap
            b.selfw(v, ve)
            ve.inc(v.scalar_tensor_tensor(sb["srow"][:, sh:T], sb["vrow"][:, 0:T - sh],
                                          sb["firs"][:, dd * 4 + tap: dd * 4 + tap + 1],
                                          sb["srow"][:, sh:T], ALU.mult, ALU.add))
        w(a, ve)
        qa.inc(a.dma_start(out=b.h_sh[dd * 128:(dd + 1) * 128, :], in_=sb["srow"][:, :]), 16)
        w(v, qa)
        b.selfw(v, ve)
        ve.inc(v.tensor_scalar(sb["lrow"][:, 0:T], sb["vrow"][:, 0:T],
                               sb["firl"][:, dd * 64 + 62: dd * 64 + 63], None, ALU.mult))
        for tap in range(62):
            sh = 62 - tap
            b.selfw(v, ve)
            ve.inc(v.scalar_tensor_tensor(sb["lrow"][:, sh:T], sb["vrow"][:, 0:T - sh],
                                          sb["firl"][:, dd * 64 + tap: dd * 64 + tap + 1],
                                          sb["lrow"][:, sh:T], ALU.mult, ALU.add))
        w(a, ve)
        b.selfw(a, ac)
        qa.inc(a.dma_start(out=b.h_lo[dd * 128:(dd + 1) * 128, :], in_=sb["lrow"][:, :]), 16)

    if upto == "fir":
        return

    def do_stats(pair, srcs, phase="all"):
        b.barrier()
        for sub, src_h in enumerate(srcs):
            for dd in range(NDT):
                slot = "vrow" if dd % 2 == 0 else "srow"
                w(s, ve, ac, pe)
                qs.inc(s.dma_start(out=sb[slot][:, :],
                                   in_=src_h[dd * 128:(dd + 1) * 128, :]), 16)
                for ti2 in range(32):
                    bank = b.pb[ti2 % 2]
                    c = sub * 256 + ti2 * 8 + dd
                    w(t, qs, ve, ac)
                    pe.inc(t.transpose(bank[:, 0:128],
                                       sb[slot][:, ti2 * 128:(ti2 + 1) * 128],
                                       sb["ident"][:, :]))
                    if phase != "accV":
                        w(a, pe)
                        b.selfw(a, ac)
                        ac.inc(a.activation(sb["scr"][:, :], bank[:, 0:128], AF.Copy,
                                            accum_out=sb["sumP"][:, c:c + 1]))
                        b.selfw(a, ac)
                        ac.inc(a.activation(sb["scr"][:, :], bank[:, 0:128], AF.Square,
                                            accum_out=sb["sqP"][:, c:c + 1]))
                    if phase != "accA":
                        w(v, ac, pe)
                        b.selfw(v, ve)
                        ve.inc(v.tensor_reduce(sb["maxP"][:, c:c + 1], bank[:, 0:128],
                                               AX.X, ALU.max))
        if phase == "acc":
            return
        sP = lambda nm, sub, t2: sb[nm][:, sub * 256 + t2 * 8:
                                        sub * 256 + (t2 + 1) * 8].rearrange(
            "p (h two) -> p h two", two=2)
        pbase = pair * 32
        for ti2 in range(32):
            for sub in range(2):
                c0 = ti2 * 32 + sub * 16
                stm = sb["statm"][:, c0:c0 + 16].rearrange(
                    "p (h st) -> p h st", st=4)
                se, so = sP("sumP", sub, ti2)[:, :, 0], sP("sumP", sub, ti2)[:, :, 1]
                qe, qo = sP("sqP", sub, ti2)[:, :, 0], sP("sqP", sub, ti2)[:, :, 1]
                me, mo = sP("maxP", sub, ti2)[:, :, 0], sP("maxP", sub, ti2)[:, :, 1]
                w(v, ac, pe)
                b.selfw(v, ve)
                ve.inc(v.tensor_add(stm[:, :, 0], se, so))
                b.selfw(v, ve)
                ve.inc(v.tensor_scalar(stm[:, :, 0], stm[:, :, 0], 1.0 / 256.0,
                                       None, ALU.mult))
                b.selfw(v, ve)
                ve.inc(v.tensor_add(stm[:, :, 3], qe, qo))
                b.selfw(v, ve)
                ve.inc(v.tensor_mul(stm[:, :, 1], stm[:, :, 0], stm[:, :, 0]))
                b.selfw(v, ve)
                ve.inc(v.scalar_tensor_tensor(stm[:, :, 1], stm[:, :, 3], 1.0 / 256.0,
                                              stm[:, :, 1], ALU.mult, ALU.subtract))
                b.selfw(v, ve)
                ve.inc(v.tensor_max(stm[:, :, 2], me, mo))
                w(a, ve)
                b.selfw(a, ac)
                ac.inc(a.activation(stm[:, :, 3], stm[:, :, 3], AF.Sqrt))
            bank = pp[ti2 % 2]
            w(t, ve, ac)
            t.wait_ge(pe.sem, pe.n)
            pe.inc(t.matmul(bank[pbase:pbase + 32, 0:128],
                            sb["statm"][:, ti2 * 32:(ti2 + 1) * 32],
                            sb["identf"][:, :], start=True, stop=True))
            w(v, pe)
            b.selfw(v, ve)
            ve.inc(v.tensor_copy(sb["gstats"][pbase:pbase + 32,
                                              ti2 * 128:(ti2 + 1) * 128],
                                 bank[pbase:pbase + 32, 0:128]))

    if upto in ("acc0", "accA", "accV"):
        do_stats(0, (b.h_sh, b.h_lo),
                 phase="acc" if upto == "acc0" else upto)
        return
    do_stats(0, (b.h_sh, b.h_lo))
    if upto == "stats0":
        return
    do_stats(1, (b.h_df, b.h_v))
    if upto == "stats":
        return

    b.barrier()
    qg.inc(g.dma_start(out=sb["w1b"][:, 0:8 * 2048].rearrange("p (kt c) -> p kt c", c=2048),
                       in_=W["w1"][0:1024, :].rearrange("(kt p) c -> p kt c", p=128)), 16)
    qg.inc(g.dma_start(out=sb["w1b"][0:64, 8 * 2048:9 * 2048],
                       in_=W["w1"][1024:1088, :]), 16)
    for tt in range(NTT):
        w(s, pe, ve, ac)
        qs.inc(s.dma_start(out=sb["gxs"][:, :],
                           in_=b.hview(b.h_xn)[:, :, tt * 512:(tt + 1) * 512]), 16)
        gneed = {}
        lastb = {}
        for mt in range(16):
            bA = (mt % 2) * 2
            w(t, qs, qg)
            if bA in gneed:
                t.wait_ge(ve.sem, gneed[bA][0])
                t.wait_ge(ac.sem, gneed[bA][1])
            for kt in range(9):
                bk = bA + kt % 2
                if bk in lastb:
                    t.wait_ge(pe.sem, lastb[bk])
                if kt < 8:
                    rhs = sb["gxs"][:, :].rearrange("p (dt c) -> p dt c", dt=NDT)[:, kt, :]
                    lhs = sb["w1b"][:, kt * 2048 + mt * 128: kt * 2048 + mt * 128 + 128]
                else:
                    rhs = sb["gstats"][0:64, tt * 512:(tt + 1) * 512]
                    lhs = sb["w1b"][0:64, 8 * 2048 + mt * 128: 8 * 2048 + mt * 128 + 128]
                lastb[bk] = pe.inc(t.matmul(pp[bk][:, 0:512], lhs, rhs,
                                            start=kt < 2, stop=kt >= 7))
            w(a, pe, ve)
            b.selfw(a, ac)
            ac.inc(a.copy(sb["ev5"][:, :], pp[bA][:, 0:512]))
            w(v, ac, pe)
            b.selfw(v, ve)
            ve.inc(v.tensor_add(sb["ev6"][:, :], pp[bA + 1][:, 0:512], sb["ev5"][:, :]))
            w(a, ve)
            b.selfw(a, ac)
            ac.inc(a.activation(sb["hmid"][:, mt * 512:(mt + 1) * 512], sb["ev6"][:, :],
                                AF.Gelu, bias=sb["b1t"][:, mt:mt + 1]))
            gneed[bA] = (ve.n, ac.n)
            gneed[bA + 1] = gneed[bA]
        lastw = None
        w(t, ac, ve)
        for kt in range(16):
            if lastw is not None:
                t.wait_ge(pe.sem, lastw)
            lastw = pe.inc(t.matmul(pp[2][0:16, 0:512], sb["w2w"][:, kt * 16:(kt + 1) * 16],
                                    sb["hmid"][:, kt * 512:(kt + 1) * 512],
                                    start=kt == 0, stop=kt == 15))
        w(v, pe)
        b.selfw(v, ve)
        ve.inc(v.tensor_scalar(sb["zb16"][0:16, 0:512], pp[2][0:16, 0:512],
                               sb["p16"][0:16, 0:1], sb["p16"][0:16, 1:2],
                               ALU.add, ALU.mult))
        w(a, ve)
        b.selfw(a, ac)
        ac.inc(a.activation(sb["e16"][0:16, 0:512], sb["zb16"][0:16, 0:512], AF.Exp))
        w(t, ac)
        t.wait_ge(pe.sem, pe.n)
        pe.inc(t.matmul(pp[3][0:16, 0:512], sb["sumself"][0:16, 0:16],
                        sb["e16"][0:16, 0:512], start=True, stop=True))
        w(v, pe)
        b.selfw(v, ve)
        ve.inc(v.reciprocal(sb["zb16"][0:16, 0:512], pp[3][0:16, 0:512]))
        b.selfw(v, ve)
        ve.inc(v.tensor_mul(sb["zb16"][0:16, 0:512], sb["zb16"][0:16, 0:512],
                            sb["e16"][0:16, 0:512]))
        b.selfw(v, ve)
        ve.inc(v.tensor_scalar(sb["e16"][0:16, 0:512], sb["zb16"][0:16, 0:512],
                               sb["p16"][0:16, 2:3], None, ALU.max))
        w(t, ve)
        t.wait_ge(pe.sem, pe.n)
        pe.inc(t.matmul(pp[3][0:16, 0:512], sb["sumself"][0:16, 0:16],
                        sb["e16"][0:16, 0:512], start=True, stop=True))
        w(v, pe)
        b.selfw(v, ve)
        ve.inc(v.reciprocal(sb["zb16"][0:16, 0:512], pp[3][0:16, 0:512]))
        b.selfw(v, ve)
        ve.inc(v.tensor_mul(sb["zb16"][0:16, 0:512], sb["zb16"][0:16, 0:512],
                            sb["e16"][0:16, 0:512]))
        b.selfw(v, ve)
        ve.inc(v.tensor_copy(sb["p_fm"][0:16, tt * 512:(tt + 1) * 512],
                             sb["zb16"][0:16, 0:512]))
    if upto == "gate":
        return

    b.barrier()
    qg.inc(g.dma_start(out=sb["wobuf"][:, :].rearrange("p (kt c) -> p kt c", c=D),
                       in_=W["wo"].rearrange("(kt p) c -> p kt c", p=128)), 16)
    for tt in range(NTT):
        sl = slice(tt * 512, (tt + 1) * 512)
        w(s, pe, ve, ac)
        qs.inc(s.dma_start(out=sb["vtile"][:, :], in_=b.hview(b.h_v)[:, :, sl]), 16)
        qs.inc(s.dma_start(out=sb["dtile"][:, :], in_=b.hview(b.h_df)[:, :, sl]), 16)
        qs.inc(s.dma_start(out=sb["stile"][:, :], in_=b.hview(b.h_sh)[:, :, sl]), 16)
        qs.inc(s.dma_start(out=sb["ltile"][:, :], in_=b.hview(b.h_lo)[:, :, sl]), 16)
        lastpb = {}
        for dd in range(NDT):
            h = dd // 2
            dsl = slice(dd * 512, (dd + 1) * 512)
            w(t, qs, ve)
            for j in range(4):
                if j in lastpb:
                    t.wait_ge(pe.sem, lastpb[j])
                lastpb[j] = pe.inc(t.matmul(
                    pp[j][:, 0:512],
                    sb["selh16"][0:16, (4 * h + j) * 128:(4 * h + j + 1) * 128],
                    sb["p_fm"][0:16, sl], start=True, stop=True))
            w(v, pe)
            b.selfw(v, ve)
            ve.inc(v.tensor_mul(sb["ocomb"][:, 0:512], sb["stile"][:, dsl], pp[0][:, 0:512]))
            b.selfw(v, ve)
            ve.inc(v.tensor_mul(sb["ocm2"][:, 0:512], sb["ltile"][:, dsl], pp[1][:, 0:512]))
            b.selfw(v, ve)
            ve.inc(v.tensor_add(sb["ocomb"][:, 0:512], sb["ocomb"][:, 0:512], sb["ocm2"][:, 0:512]))
            b.selfw(v, ve)
            ve.inc(v.tensor_mul(sb["ocm2"][:, 0:512], sb["dtile"][:, dsl], pp[2][:, 0:512]))
            b.selfw(v, ve)
            ve.inc(v.tensor_add(sb["ocomb"][:, 0:512], sb["ocomb"][:, 0:512], sb["ocm2"][:, 0:512]))
            b.selfw(v, ve)
            ve.inc(v.tensor_mul(sb["ocm2"][:, 0:512], sb["vtile"][:, dsl], pp[3][:, 0:512]))
            b.selfw(v, ve)
            ve.inc(v.tensor_add(sb["ocomb"][:, 0:512], sb["ocomb"][:, 0:512], sb["ocm2"][:, 0:512]))
            if dd % 2 == 1:
                w(a, ve)
                b.selfw(a, ac)
                ac.inc(a.activation(sb["sqcb"][:, 0:512], sb["ocm3"][:, 0:512],
                                    AF.Square))
                b.selfw(a, ac)
                ac.inc(a.activation(sb["sqcb"][:, 512:1024], sb["ocomb"][:, 0:512],
                                    AF.Square))
                w(t, ac)
                t.wait_ge(pe.sem, pe.n)
                rms1 = pe.inc(t.matmul(pp[5][0:1, 0:512], sb["ones"][:, 0:1],
                                       sb["sqcb"][:, 0:512], start=True, stop=False))
                t.wait_ge(pe.sem, rms1)
                pe.inc(t.matmul(pp[5][0:1, 0:512], sb["ones"][:, 0:1],
                                sb["sqcb"][:, 512:1024], start=False, stop=True))
                w(v, pe)
                b.selfw(v, ve)
                ve.inc(v.tensor_copy(sb["ev6"][0:1, 0:512], pp[5][0:1, 0:512]))
                b.selfw(v, ve)
                ve.inc(v.tensor_scalar(sb["ev6"][0:1, 0:512], sb["ev6"][0:1, 0:512],
                                       1.0 / 256.0, 1e-5, ALU.mult, ALU.add))
                b.selfw(v, ve)
                ve.inc(v.reciprocal(sb["ev6"][0:1, 0:512], sb["ev6"][0:1, 0:512]))
                w(a, ve)
                b.selfw(a, ac)
                ac.inc(a.activation(sb["ev6"][0:1, 0:512], sb["ev6"][0:1, 0:512], AF.Sqrt))
                w(t, ac, ve)
                t.wait_ge(pe.sem, pe.n)
                pe.inc(t.matmul(pp[4][:, 0:512], sb["onesf"][0:1, 0:128],
                                sb["ev6"][0:1, 0:512], start=True, stop=True))
                for d2 in (dd - 1, dd):
                    nwcol = 0 if d2 % 2 == 0 else 1
                    w(v, pe)
                    b.selfw(v, ve)
                    ve.inc(v.scalar_tensor_tensor(
                        sb["onrm"][:, d2 * 512:(d2 + 1) * 512],
                        sb["ocomb"][:, 0:512] if d2 == dd else sb["ocm3"][:, 0:512],
                        sb["normw"][:, nwcol:nwcol + 1], pp[4][:, 0:512],
                        ALU.mult, ALU.mult))
            if dd % 2 == 0:
                b.selfw(v, ve)
                ve.inc(v.tensor_copy(sb["ocm3"][:, 0:512], sb["ocomb"][:, 0:512]))
        for mt in range(NDT):
            lastb = {}
            w(t, ve, ac)
            for kt in range(NDT):
                bk = kt % 2
                if bk in lastb:
                    t.wait_ge(pe.sem, lastb[bk])
                lastb[bk] = pe.inc(t.matmul(
                    pp[bk][:, 0:512],
                    sb["wobuf"][:, kt * D + mt * 128: kt * D + mt * 128 + 128],
                    sb["onrm"][:, kt * 512:(kt + 1) * 512],
                    start=kt < 2, stop=kt >= 6))
            w(a, pe, ve)
            b.selfw(a, ac)
            ac.inc(a.copy(sb["ev5"][:, :], pp[0][:, 0:512]))
            w(v, ac, pe, qs)
            b.selfw(v, ve)
            ve.inc(v.tensor_add(sb["ev6"][:, :], pp[1][:, 0:512], sb["ev5"][:, :]))
            w(s, ve)
            qs.inc(s.dma_start(out=b.h_s[mt * 128:(mt + 1) * 128, sl],
                               in_=sb["ev6"][:, :]), 16)


def p6(b):
    t, v, a, g, s = b.t, b.v, b.a, b.g, b.s
    sb, pp, W, w = b.sb, b.pp, b.W, b.w
    pe, ve, ac, gp, qs, qa, qg, ccs = (b.pe, b.ve, b.ac, b.gp, b.qs, b.qa,
                                       b.qg, b.ccs)

    def dequant_t(wsrc, dstname):
        b.barrier()
        for it in range(NI):
            w(s, ve, ac, pe)
            qs.inc(s.dma_start(out=sb["dqs"][:, :], in_=wsrc[it * 128:(it + 1) * 128, :]), 16)
            w(v, qs)
            b.selfw(v, ve)
            ve.inc(v.tensor_reduce(sb["dqm"][:, 0:32],
                                   sb["dqs"][:, :].rearrange("p (bk e) -> p bk e", e=32),
                                   AX.X, ALU.max, apply_absolute_value=True))
            b.selfw(v, ve)
            ve.inc(v.tensor_scalar(sb["dqm"][:, 0:32], sb["dqm"][:, 0:32],
                                   1.0 / 7.0, 1e-12, ALU.mult, ALU.max))
            b.selfw(v, ve)
            ve.inc(v.reciprocal(sb["dqm"][:, 32:64], sb["dqm"][:, 0:32]))
            for bk in range(32):
                b.selfw(v, ve)
                ve.inc(v.tensor_scalar(sb["dqs"][:, bk * 32:(bk + 1) * 32],
                                       sb["dqs"][:, bk * 32:(bk + 1) * 32],
                                       sb["dqm"][:, 32 + bk:33 + bk], MAGIC,
                                       ALU.mult, ALU.add))
                b.selfw(v, ve)
                ve.inc(v.tensor_scalar(sb["dqs"][:, bk * 32:(bk + 1) * 32],
                                       sb["dqs"][:, bk * 32:(bk + 1) * 32],
                                       MAGIC, sb["dqm"][:, bk:bk + 1],
                                       ALU.subtract, ALU.mult))
            w(a, ve)
            b.selfw(a, ac)
            ac.inc(a.copy(sb["dqb"][:, :], sb["dqs"][:, :]))
            for kt in range(NDT):
                bank = b.pb[kt % 2]
                w(t, ac, ve)
                t.wait_ge(pe.sem, pe.n)
                pe.inc(t.transpose(bank[:, 0:128], sb["dqb"][:, kt * 128:(kt + 1) * 128],
                                   sb["ident"][:, :]))
                if kt % 2 == 0:
                    w(a, pe)
                    b.selfw(a, ac)
                    ac.inc(a.copy(
                        sb[dstname][:, kt * 3072 + it * 128: kt * 3072 + it * 128 + 128],
                        bank[:, 0:128]))
                else:
                    w(v, pe)
                    b.selfw(v, ve)
                    ve.inc(v.tensor_copy(
                        sb[dstname][:, kt * 3072 + it * 128: kt * 3072 + it * 128 + 128],
                        bank[:, 0:128]))

    dequant_t(W["wu_f"], "wut")
    dequant_t(W["wg_f"], "wgt")

    b.barrier()
    amax_col = [33, 34]
    w(v, qs)
    b.selfw(v, ve)
    ve.inc(v.memset(sb["amax"][:, 33:35], 0.0))
    for tt in range(NTT):
        w(s, pe, ve, ac)
        qs.inc(s.dma_start(out=sb["gxs"][:, :],
                           in_=b.hview(b.h_xn)[:, :, tt * 512:(tt + 1) * 512]), 16)
        for dd in range(NDT):
            w(v, qs)
            b.selfw(v, ve)
            ve.inc(v.tensor_scalar(sb["ev7"][:, :],
                                   sb["gxs"][:, :].rearrange("p (dt c) -> p dt c", dt=NDT)[:, dd, :],
                                   sb["scb"][:, 1:2], MAGIC, ALU.mult, ALU.add))
            b.selfw(v, ve)
            ve.inc(v.tensor_scalar(sb["xq_t"][:, dd * 512:(dd + 1) * 512],
                                   sb["ev7"][:, :], MAGIC, sb["scb"][:, 0:1],
                                   ALU.subtract, ALU.mult))
        w(t, ve)
        fneed = {}
        lastb = {}
        for which, wname, hdst, acol in (("up", "wut", b.h_up, 33), ("ga", "wgt", b.h_ga, 34)):
            for it in range(NI):
                bA = (it % 2) * 2
                w(t, qs)
                if bA in fneed:
                    t.wait_ge(ve.sem, fneed[bA][0])
                    t.wait_ge(ac.sem, fneed[bA][1])
                for kt in range(NDT):
                    bk = bA + kt % 2
                    if bk in lastb:
                        t.wait_ge(pe.sem, lastb[bk])
                    lastb[bk] = pe.inc(t.matmul(
                        pp[bk][:, 0:512],
                        sb[wname][:, kt * 3072 + it * 128: kt * 3072 + it * 128 + 128],
                        sb["xq_t"][:, kt * 512:(kt + 1) * 512],
                        start=kt < 2, stop=kt >= 6))
                w(a, pe, ve)
                b.selfw(a, ac)
                ac.inc(a.copy(sb["ev7"][:, :], pp[bA][:, 0:512]))
                w(v, ac, pe, qs)
                b.selfw(v, ve)
                ve.inc(v.tensor_add(sb["ev8"][:, :], pp[bA + 1][:, 0:512], sb["ev7"][:, :]))
                w(s, ve)
                qs.inc(s.dma_start(out=hdst[it * 128:(it + 1) * 128, tt * 512:(tt + 1) * 512],
                                   in_=sb["ev8"][:, :]), 16)
                b.selfw(v, ve)
                ve.inc(v.tensor_reduce(sb["amax"][:, 40:41], sb["ev8"][:, :], AX.X,
                                       ALU.max, apply_absolute_value=True))
                b.selfw(v, ve)
                ve.inc(v.tensor_max(sb["amax"][:, acol:acol + 1],
                                    sb["amax"][:, acol:acol + 1], sb["amax"][:, 40:41]))
                fneed[bA] = (ve.n, ac.n)
                fneed[bA + 1] = fneed[bA]

    def fold_and_ar(cols, aridx, scslots):
        for i, cn in enumerate(cols):
            w(t, ve, ac)
            t.wait_ge(pe.sem, pe.n)
            pe.inc(t.transpose(pp[2][0:1, 0:128], sb["amax"][:, cn:cn + 1],
                               sb["identf"][:, :]))
            w(v, pe)
            b.selfw(v, ve)
            ve.inc(v.tensor_reduce(sb["sc"][0:1, 10 + i:11 + i], pp[2][0:1, 0:128],
                                   AX.X, ALU.max))
        w(g, ve)
        b.selfw(g, gp)
        qg.inc(g.dma_start(out=b.ar_in[aridx].ap()[0:1, 0:2],
                           in_=sb["sc"][0:1, 10:12]), 16)
        w(g, qg)
        ccs.inc(g.collective_compute("AllReduce", ALU.max,
                                     replica_groups=[list(range(8))],
                                     ins=[b.ar_in[aridx].ap()],
                                     outs=[b.ar_out[aridx].ap()]))
        w(g, ccs)
        qg.inc(g.dma_start(out=sb["sc"][0:1, 12:14], in_=b.ar_out[aridx].ap()[0:1, 0:2]), 16)
        w(v, qg)
        for i, (scol, icol) in enumerate(scslots):
            b.selfw(v, ve)
            ve.inc(v.tensor_scalar(sb["sc"][0:1, 14 + i:15 + i], sb["sc"][0:1, 12 + i:13 + i],
                                   1.0 / 32767.0, 1e-12, ALU.mult, ALU.max))
            b.selfw(v, ve)
            ve.inc(v.reciprocal(sb["sc"][0:1, 16 + i:17 + i], sb["sc"][0:1, 14 + i:15 + i]))
        w(t, ve)
        t.wait_ge(pe.sem, pe.n)
        pe.inc(t.matmul(pp[3][:, 0:4], sb["onesf"][0:1, 0:128], sb["sc"][0:1, 14:18],
                        start=True, stop=True))
        w(v, pe)
        for i, (scol, icol) in enumerate(scslots):
            b.selfw(v, ve)
            ve.inc(v.tensor_copy(sb["scb"][:, scol:scol + 1], pp[3][:, i:i + 1]))
            b.selfw(v, ve)
            ve.inc(v.tensor_copy(sb["scb"][:, icol:icol + 1], pp[3][:, 2 + i:3 + i]))

    fold_and_ar([33, 34], 1, [(2, 3), (4, 5)])
    b.selfw(v, ve)
    ve.inc(v.tensor_scalar(sb["scb"][:, 10:11], sb["scb"][:, 4:5], -MAGIC, None, ALU.mult))
    b.selfw(v, ve)
    ve.inc(v.tensor_scalar(sb["scb"][:, 12:13], sb["scb"][:, 2:3], -MAGIC, None, ALU.mult))

    b.barrier()
    b.selfw(v, ve)
    ve.inc(v.memset(sb["amax"][:, 35:36], 0.0))
    for it in range(NI):
        w(s, ve, ac)
        qs.inc(s.dma_start(out=sb["ffa"][:, :], in_=b.h_ga[it * 128:(it + 1) * 128, :]), 16)
        w(a, qs)
        b.selfw(a, ac)
        ac.inc(a.activation(sb["ffb"][:, :], sb["ffa"][:, :], AF.Copy,
                            scale=sb["scb"][:, 5:6], bias=float(MAGIC)))
        w(v, ac)
        b.selfw(v, ve)
        ve.inc(v.tensor_scalar(sb["ffb"][:, :], sb["ffb"][:, :], MAGIC,
                               sb["scb"][:, 4:5], ALU.subtract, ALU.mult))
        w(a, ve)
        b.selfw(a, ac)
        ac.inc(a.activation(sb["ffc"][:, :], sb["ffb"][:, :], AF.Sigmoid))
        w(v, ac)
        b.selfw(v, ve)
        ve.inc(v.tensor_scalar(sb["ffc"][:, :], sb["ffc"][:, :], 65536.0, MAGIC,
                               ALU.mult, ALU.add))
        b.selfw(v, ve)
        ve.inc(v.tensor_scalar(sb["ffc"][:, :], sb["ffc"][:, :], MAGIC, 65535.0,
                               ALU.subtract, ALU.min))
        b.selfw(v, ve)
        ve.inc(v.scalar_tensor_tensor(sb["ffa"][:, :], sb["ffc"][:, :], float(2.0 ** -16),
                                      sb["ffb"][:, :], ALU.mult, ALU.mult))
        w(s, ve)
        qs.inc(s.dma_start(out=b.h_t[it * 128:(it + 1) * 128, :], in_=sb["ffa"][:, :]), 16)
        b.selfw(v, ve)
        ve.inc(v.tensor_reduce(sb["amax"][:, 40:41], sb["ffa"][:, :], AX.X,
                               ALU.max, apply_absolute_value=True))
        b.selfw(v, ve)
        ve.inc(v.tensor_max(sb["amax"][:, 35:36], sb["amax"][:, 35:36],
                            sb["amax"][:, 40:41]))
    fold_and_ar([35], 2, [(6, 7)])
    b.selfw(v, ve)
    ve.inc(v.tensor_scalar(sb["scb"][:, 11:12], sb["scb"][:, 6:7], -MAGIC, None, ALU.mult))

    b.barrier()
    b.selfw(v, ve)
    ve.inc(v.memset(sb["amax"][:, 36:37], 0.0))
    for it in range(NI):
        w(s, ve, ac)
        qs.inc(s.dma_start(out=sb["ffa"][:, :], in_=b.h_t[it * 128:(it + 1) * 128, :]), 16)
        qs.inc(s.dma_start(out=sb["ffb"][:, :], in_=b.h_up[it * 128:(it + 1) * 128, :]), 16)
        w(a, qs, ve)
        b.selfw(a, ac)
        ac.inc(a.activation(sb["ffa"][:, :], sb["ffa"][:, :], AF.Copy,
                            scale=sb["scb"][:, 7:8], bias=float(MAGIC)))
        b.selfw(a, ac)
        ac.inc(a.activation(sb["ffb"][:, :], sb["ffb"][:, :], AF.Copy,
                            scale=sb["scb"][:, 3:4], bias=float(MAGIC)))
        w(v, qs, ac)
        b.selfw(v, ve)
        ve.inc(v.tensor_scalar(sb["ffa"][:, :], sb["ffa"][:, :], MAGIC, sb["scb"][:, 6:7],
                               ALU.subtract, ALU.mult))
        b.selfw(v, ve)
        ve.inc(v.tensor_scalar(sb["ffb"][:, :], sb["ffb"][:, :], MAGIC, sb["scb"][:, 2:3],
                               ALU.subtract, ALU.mult))
        b.selfw(v, ve)
        ve.inc(v.tensor_mul(sb["ffc"][:, :], sb["ffa"][:, :], sb["ffb"][:, :]))
        w(s, ve)
        qs.inc(s.dma_start(out=b.h_ot[it * 128:(it + 1) * 128, :], in_=sb["ffc"][:, :]), 16)
        b.selfw(v, ve)
        ve.inc(v.tensor_reduce(sb["amax"][:, 40:41], sb["ffc"][:, :], AX.X,
                               ALU.max, apply_absolute_value=True))
        b.selfw(v, ve)
        ve.inc(v.tensor_max(sb["amax"][:, 36:37], sb["amax"][:, 36:37],
                            sb["amax"][:, 40:41]))
    fold_and_ar([36], 3, [(8, 9)])
    b.selfw(v, ve)
    ve.inc(v.tensor_scalar(sb["scb"][:, 13:14], sb["scb"][:, 8:9], -MAGIC, None, ALU.mult))

    b.barrier()
    for pt in range(NDT):
        w(s, ve, ac, pe)
        qs.inc(s.dma_start(out=sb["ffa"][:, 0:3072], in_=W["wd_f"][pt * 128:(pt + 1) * 128, :]), 16)
        w(v, qs)
        b.selfw(v, ve)
        ve.inc(v.tensor_reduce(sb["dqm"][:, 0:96] if False else sb["ffb"][:, 0:96],
                               sb["ffa"][:, 0:3072].rearrange("p (bk e) -> p bk e", e=32),
                               AX.X, ALU.max, apply_absolute_value=True))
        b.selfw(v, ve)
        ve.inc(v.tensor_scalar(sb["ffb"][:, 0:96], sb["ffb"][:, 0:96],
                               1.0 / 7.0, 1e-12, ALU.mult, ALU.max))
        b.selfw(v, ve)
        ve.inc(v.reciprocal(sb["ffb"][:, 96:192], sb["ffb"][:, 0:96]))
        for bk in range(96):
            b.selfw(v, ve)
            ve.inc(v.tensor_scalar(sb["ffa"][:, bk * 32:(bk + 1) * 32],
                                   sb["ffa"][:, bk * 32:(bk + 1) * 32],
                                   sb["ffb"][:, 96 + bk:97 + bk], MAGIC,
                                   ALU.mult, ALU.add))
            b.selfw(v, ve)
            ve.inc(v.tensor_scalar(sb["ffa"][:, bk * 32:(bk + 1) * 32],
                                   sb["ffa"][:, bk * 32:(bk + 1) * 32],
                                   MAGIC, sb["ffb"][:, bk:bk + 1],
                                   ALU.subtract, ALU.mult))
        w(a, ve)
        b.selfw(a, ac)
        ac.inc(a.copy(sb["ffc"][:, 0:3072], sb["ffa"][:, 0:3072]))
        for c3 in range(3):
            w(v, ac)
            b.selfw(v, ve)
            ve.inc(v.tensor_copy(sb["dqb"][:, :], sb["ffc"][:, c3 * 1024:(c3 + 1) * 1024]))
            for kt in range(8):
                it = c3 * 8 + kt
                bank = b.pb[kt % 2]
                w(t, ve, ac)
                t.wait_ge(pe.sem, pe.n)
                pe.inc(t.transpose(bank[:, 0:128], sb["dqb"][:, kt * 128:(kt + 1) * 128],
                                   sb["ident"][:, :]))
                w(v, pe)
                b.selfw(v, ve)
                ve.inc(v.tensor_copy(
                    sb["wdt"][:, it * 1024 + pt * 128: it * 1024 + pt * 128 + 128],
                    bank[:, 0:128]))
    for tt in range(NTT):
        sl = slice(tt * 512, (tt + 1) * 512)
        for blk in range(4):
            w(s, ve, ac)
            qs.inc(s.dma_start(out=sb["x_t"][:, :],
                               in_=b.dx[tt * 512 + blk * 128: tt * 512 + (blk + 1) * 128, :]), 16)
            for dd in range(NDT):
                bank = pp[2 + dd % 2]
                w(t, qs, ve, ac)
                t.wait_ge(pe.sem, pe.n)
                pe.inc(t.transpose(bank[:, 0:128], sb["x_t"][:, dd * 128:(dd + 1) * 128],
                                   sb["identf"][:, :]))
                w(v, pe)
                b.selfw(v, ve)
                ve.inc(v.tensor_copy(
                    sb["xpt"][:, dd * 512 + blk * 128: dd * 512 + blk * 128 + 128],
                    bank[:, 0:128]))
        lastd = {}
        for half in range(2):
            for it in range(NI):
                w(s, ve, ac)
                qs.inc(s.dma_start(out=sb["sgt"][:, :], in_=b.h_ot[it * 128:(it + 1) * 128, sl]), 16)
                w(a, qs, pe)
                b.selfw(a, ac)
                ac.inc(a.activation(sb["sgt"][:, :], sb["sgt"][:, :], AF.Copy,
                                    scale=sb["scb"][:, 9:10], bias=float(MAGIC)))
                w(v, ac, pe)
                b.selfw(v, ve)
                ve.inc(v.tensor_scalar(sb["o2t"][:, :], sb["sgt"][:, :], MAGIC,
                                       sb["scb"][:, 8:9], ALU.subtract, ALU.mult))
                w(t, ve)
                if it == 0:
                    w(t, ac)
                for mi in range(4):
                    mt = half * 4 + mi
                    bank = pp[mi]
                    if mi in lastd:
                        t.wait_ge(pe.sem, lastd[mi])
                    lastd[mi] = pe.inc(t.matmul(
                        bank[:, 0:512],
                        sb["wdt"][:, it * 1024 + mt * 128: it * 1024 + mt * 128 + 128],
                        sb["o2t"][:, :],
                        start=it == 0, stop=it == NI - 1))
            for mi in range(4):
                mt = half * 4 + mi
                w(s, ve, ac)
                qs.inc(s.dma_start(out=sb["sgt"][:, :], in_=b.h_s[mt * 128:(mt + 1) * 128, sl]), 16)
                w(a, pe, qs)
                b.selfw(a, ac)
                ac.inc(a.activation(sb["gt"][:, :], pp[mi][:, 0:512], AF.Silu))
                w(v, ac, qs)
                b.selfw(v, ve)
                ve.inc(v.tensor_mul(sb["gt"][:, :], sb["gt"][:, :], sb["sgt"][:, :]))
                b.selfw(v, ve)
                ve.inc(v.tensor_add(sb["outf"][:, mt * 512:(mt + 1) * 512],
                                    sb["gt"][:, :], sb["xpt"][:, mt * 512:(mt + 1) * 512]))
        for blk in range(4):
            for dd in range(NDT):
                bank = pp[2 + dd % 2]
                w(t, ve, ac)
                t.wait_ge(pe.sem, pe.n)
                pe.inc(t.transpose(bank[:, 0:128],
                                   sb["outf"][:, dd * 512 + blk * 128: dd * 512 + blk * 128 + 128],
                                   sb["identf"][:, :]))
                if dd % 2 == 0:
                    w(a, pe, qs)
                    b.selfw(a, ac)
                    ac.inc(a.copy(sb["obuf"][:, dd * 128:(dd + 1) * 128], bank[:, 0:128]))
                else:
                    w(v, pe, qs)
                    b.selfw(v, ve)
                    ve.inc(v.tensor_copy(sb["obuf"][:, dd * 128:(dd + 1) * 128], bank[:, 0:128]))
            w(s, ve, ac)
            qs.inc(s.dma_start(
                out=b.dy[tt * 512 + blk * 128: tt * 512 + (blk + 1) * 128, :],
                in_=sb["obuf"][:, 0:1024]), 16)



def fixup_bias_note():
    pass


def finish(b):
    w = b.w
    b.barrier()
    b.g.wait_ge(b.qs.sem, b.qs.n)
    b.g.wait_ge(b.qa.sem, b.qa.n)
    b.g.wait_ge(b.qg.sem, b.qg.n)


def build(dump=None):
    b = mkbuilder(dump)
    p0(b)
    p1(b)
    p3(b)
    if dump == "xn":
        dbg_dump(b, b.h_xn, 0)
    elif dump == "q":
        dbg_dump_sb(b, "q", 0)
    elif dump == "v":
        dbg_dump(b, b.h_v, 0)
    elif dump == "S1":
        p4(b, nblk=1)
        sb = b.sb
        b.barrier()
        b.selfw(b.v, b.ve)
        b.ve.inc(b.v.memset(sb["ffb"][:, :], 0.0))
        b.selfw(b.v, b.ve)
        b.ve.inc(b.v.tensor_copy(sb["ffb"][:, 0:2048], sb["S_f"][:, :]))
        b.w(b.s, b.ve)
        b.qs.inc(b.s.dma_start(out=b.d_dbg, in_=sb["ffb"][:, :]), 16)
    else:
        p4(b)
        if dump == "delta":
            dbg_dump(b, b.h_df, 0)
        elif dump in ("fir", "acc0", "accA", "accV", "stats0", "stats", "gate"):
            p5(b, upto=dump)
            sb = b.sb
            if dump == "fir":
                dbg_dump(b, b.h_sh, 0)
            else:
                buf = "ffa" if dump == "gate" else "ffb"
                b.barrier()
                b.selfw(b.v, b.ve)
                b.ve.inc(b.v.memset(sb[buf][:, :], 0.0))
                b.selfw(b.v, b.ve)
                if dump in ("acc0", "accA", "accV"):
                    src = "maxP" if dump == "accV" else "sumP"
                    b.ve.inc(b.v.tensor_copy(sb[buf][0:64, 0:512], sb[src][0:64, :]))
                elif dump in ("stats", "stats0"):
                    b.ve.inc(b.v.tensor_copy(sb[buf][0:64, :], sb["gstats"][:, :]))
                else:
                    b.ve.inc(b.v.tensor_copy(sb[buf][0:16, :], sb["p_fm"][0:16, :]))
                b.w(b.s, b.ve)
                b.qs.inc(b.s.dma_start(out=b.d_dbg, in_=sb[buf][:, :]), 16)
        else:
            p5(b)
            if dump == "s":
                dbg_dump_f32(b, b.h_s, 0)
            else:
                p6(b)
    finish(b)
    nc, t, v, a, g, s = b.nc, b.t, b.v, b.a, b.g, b.s
    with nc.Block() as block:
        @block.sync
        def _(eng):
            s.replay(eng)

        @block.tensor
        def _(eng):
            t.replay(eng)

        @block.vector
        def _(eng):
            v.replay(eng)

        @block.scalar
        def _(eng):
            a.replay(eng)

        @block.gpsimd
        def _(eng):
            g.replay(eng)
    return b


def dbg_dump(b, hsrc, row):
    sb, w = b.sb, b.w
    b.barrier()
    b.qs.inc(b.s.dma_start(out=sb["vrow"][:, :], in_=hsrc[row * 128:(row + 1) * 128, :]), 16)
    w(b.v, b.qs)
    b.selfw(b.v, b.ve)
    b.ve.inc(b.v.tensor_copy(sb["ffb"][:, :], sb["vrow"][:, :]))
    w(b.s, b.ve)
    b.qs.inc(b.s.dma_start(out=b.d_dbg, in_=sb["ffb"][:, :]), 16)


def dbg_dump_sb(b, name, dd):
    sb, w = b.sb, b.w
    b.barrier()
    b.selfw(b.v, b.ve)
    b.ve.inc(b.v.tensor_copy(sb["ffb"][:, :], sb[name][:, dd * T:(dd + 1) * T]))
    w(b.s, b.ve)
    b.qs.inc(b.s.dma_start(out=b.d_dbg, in_=sb["ffb"][:, :]), 16)


def dbg_dump_f32(b, hsrc, row):
    sb, w = b.sb, b.w
    b.barrier()
    b.qs.inc(b.s.dma_start(out=sb["ffa"][:, :], in_=hsrc[row * 128:(row + 1) * 128, :]), 16)
    w(b.s, b.qs)
    b.qs.inc(b.s.dma_start(out=b.d_dbg, in_=sb["ffa"][:, :]), 16)


def inputs_to_maps(inputs):
    ins = {k: np.ascontiguousarray(np.asarray(vv, np.float32))
           for k, vv in inputs.items()}
    nmap = dict(
        wq="L0_attn_Wq", wk="L0_attn_Wk", wv="L0_attn_Wv", wb="L0_attn_Wb",
        conv_q="L0_attn_conv_q", conv_k="L0_attn_conv_k", conv_v="L0_attn_conv_v",
        w1="L0_attn_W1", w2="L0_attn_W2", wo="L0_attn_Wo",
        wg_f="L0_ffn_w_gate", wu_f="L0_ffn_w_up", wd_f="L0_ffn_w_down")
    base = {k: ins[v2] for k, v2 in nmap.items()}
    base["forget"] = ins["L0_attn_forget_param"].reshape(1, H)
    base["log_temp"] = ins["L0_attn_log_temp"].reshape(1, H)
    base["floor"] = np.ascontiguousarray(ins["L0_attn_floor_param"].reshape(H, 4))
    base["b1"] = ins["L0_attn_b1"].reshape(1, 2048)
    base["b2"] = ins["L0_attn_b2"].reshape(1, 16)
    base["norm_w"] = ins["L0_attn_norm_w"].reshape(1, DV)
    base["fir_s"] = np.ascontiguousarray(ins["L0_attn_fir_short_w"].reshape(D, 3))
    base["fir_l"] = np.ascontiguousarray(ins["L0_attn_fir_long_w"].reshape(D, 63))
    x = ins["L0_attn_hidden_states"]
    return [dict(base, x=np.ascontiguousarray(x[i])) for i in range(8)]


def kernel(**inputs):
    b = build()
    maps = inputs_to_maps(inputs)
    res = run_bass_kernel_spmd(b.nc, maps, core_ids=list(range(8)))
    return np.stack([r["y"] for r in res.results], axis=0)


# version 06 (final)
# speedup vs baseline: 1.0768x; 1.0591x over previous
import numpy as np
import ml_dtypes
import concourse.bass as bass
import concourse.mybir as mybir
from bass_rust import ReduceOp
from concourse.bass_utils import run_bass_kernel_spmd

F32 = mybir.dt.float32
BF16 = mybir.dt.bfloat16
AF = mybir.ActivationFunctionType
ALU = mybir.AluOpType
AX = mybir.AxisListType

T, D, H, DK, DV, C = 4096, 1024, 4, 256, 256, 32
NCH, NDT, NBLK, NTT = 128, 8, 32, 8
MAGIC = float(np.float32(1.5 * 2.0 ** 23))
FORGET_MIN, MAX_FLOOR = 0.55, 0.075
NI = 24
KB = 1024


class Cnt:
    def __init__(self, es, nc, name):
        self.sem = es.enter_context(nc.semaphore(name))
        self.n = 0

    def inc(self, tok, k=1):
        tok.then_inc(self.sem, k)
        self.n += k
        return self.n


class Rec:
    def __init__(self):
        self.calls = []

    def __getattr__(self, name):
        if name.startswith("_"):
            raise AttributeError(name)
        calls = self.calls

        def call(*a, **k):
            rec = {"m": name, "a": a, "k": k, "incs": []}
            calls.append(rec)

            class Tok:
                def then_inc(self, sem, n=1):
                    rec["incs"].append((sem, n))
                    return self
            return Tok()
        return call

    def replay(self, eng):
        for idx, rec in enumerate(self.calls):
            try:
                instr = getattr(eng, rec["m"])(*rec["a"], **rec["k"])
            except Exception:
                import sys as _s
                print("REPLAY FAIL", rec["m"], "idx", idx,
                      [getattr(x, "tensor", None) and x.tensor.name or str(x)[:40]
                       for x in rec["a"] if hasattr(x, "tensor") or True][:4],
                      file=_s.stderr)
                raise
            for sem, n in rec["incs"]:
                instr.then_inc(sem, n)


class B:
    pass


def mkbuilder(dump):
    import contextlib
    b = B()
    b.dump = dump
    b.nc = nc = bass.Bass("TRN2", target_bir_lowering=False)
    b.es = es = contextlib.ExitStack()
    b.t, b.v, b.a, b.g, b.s = Rec(), Rec(), Rec(), Rec(), Rec()

    b.dx = nc.dram_tensor("x", [T, D], F32, kind="ExternalInput").ap()
    b.dy = nc.dram_tensor("y", [T, D], F32, kind="ExternalOutput").ap()
    b.W = {}
    for nm, shape in [
        ("wq", [D, D]), ("wk", [D, D]), ("wv", [D, D]), ("wb", [D, H]),
        ("forget", [1, H]), ("conv_q", [D, 4]), ("conv_k", [D, 4]),
        ("conv_v", [D, 4]), ("fir_s", [D, 3]), ("fir_l", [D, 63]),
        ("log_temp", [1, H]), ("floor", [H, 4]),
        ("w1", [1088, 2048]), ("b1", [1, 2048]), ("w2", [2048, 16]),
        ("b2", [1, 16]), ("norm_w", [1, DV]), ("wo", [D, D]),
        ("wg_f", [3072, D]), ("wu_f", [3072, D]), ("wd_f", [D, 3072]),
    ]:
        b.W[nm] = nc.dram_tensor(nm, shape, F32, kind="ExternalInput").ap()

    def dram2(name, rows, cols, dt):
        return nc.dram_tensor(name, [rows, cols], dt).ap()
    b.h_xn = dram2("h_xn", NDT * 128, T, BF16)
    b.h_v = dram2("h_v", NDT * 128, T, BF16)
    b.h_df = dram2("h_df", NDT * 128, T, BF16)
    b.h_sh = dram2("h_sh", NDT * 128, T, BF16)
    b.h_lo = dram2("h_lo", NDT * 128, T, BF16)
    b.h_s = dram2("h_s", NDT * 128, T, F32)
    b.h_up = dram2("h_up", NI * 128, T, F32)
    b.h_ga = dram2("h_ga", NI * 128, T, F32)
    b.h_t = dram2("h_t", NI * 128, T, F32)
    b.h_ot = dram2("h_ot", NI * 128, T, F32)
    b.ar_in = [nc.dram_tensor(f"ar_in{i}", [1, 2], F32) for i in range(4)]
    b.ar_out = [nc.dram_tensor(f"ar_out{i}", [1, 2], F32) for i in range(4)]
    b.d_dbg = (nc.dram_tensor("dbg", [128, T], F32, kind="ExternalOutput").ap()
               if dump is not None else None)

    for nm in ["pe", "ve", "ac", "gp", "qs", "qa", "qg", "ccs"]:
        setattr(b, nm, Cnt(es, nc, nm))

    b.sb = {}
    BLOB = 176 * KB
    blob0, _ = nc.bump_sbuf(BLOB)
    b.AQ, b.AK, b.R2 = blob0, blob0 + 64 * KB, blob0 + 128 * KB

    def At(name, shape, dt, off):
        b.sb[name] = nc.alloc_sbuf_tensor_at("sb_" + name, shape, dt, offset=off)
        return b.sb[name]

    def S(name, shape, dt=BF16):
        b.sb[name] = nc.alloc_sbuf_tensor("sb_" + name, shape, dt)
        return b.sb[name]
    b.At, b.S = At, S

    S("ones", [128, 128], BF16); S("onesf", [128, 128], F32)
    S("ident", [128, 128], BF16); S("identf", [128, 128], F32)
    S("sel4", [128, 32], BF16)
    S("selh", [4, 4 * 128], BF16)
    S("selhf", [4, 4 * 128], F32)
    S("negselh", [4, 4 * 128], BF16)
    S("selh16", [16, 16 * 128], BF16)
    S("hsel", [128, 32], BF16)
    S("sumsel", [16, 16], BF16)
    S("sumself", [16, 16], F32)
    S("mask_bdsl", [128, 128], BF16)
    S("mask_triu", [128, 128], BF16)
    S("convw", [128, 128], F32)
    S("firs", [128, 32], F32)
    S("firl", [128, 512], F32)
    S("normw", [128, 2], F32)
    S("b1t", [128, 16], F32)
    S("misc", [4, 64], F32)
    S("lam", [128, 8], F32)
    S("sc", [1, 64], F32)
    S("scb", [128, 24], F32)
    S("lns", [128, 16], F32)
    S("amax", [128, 48], F32)
    S("wb_w", [128, 32], BF16)
    S("w2w", [128, 256], BF16)
    S("rows1", [128, T], BF16)
    S("gstats", [64, T], BF16)
    S("p16", [16, 16], F32)

    At("q", [128, NDT * T], BF16, b.AQ)
    At("k", [128, NDT * T], BF16, b.AK)
    R2 = b.R2
    At("x_t", [128, D], F32, R2)
    At("xn_t", [128, D], BF16, R2 + 4 * KB)
    At("sq_t", [128, D], BF16, R2 + 6 * KB)
    At("xfm", [128, NDT * 512], BF16, R2 + 8 * KB)
    At("xfm2", [128, NDT * 512], BF16, R2 + 16 * KB)
    At("xs", [128, NDT * 512], BF16, R2)
    At("pre", [128, NDT * 520], BF16, R2 + 8 * KB)
    At("wbuf", [128, NDT * D], BF16, R2 + 17 * KB)
    At("sqrow", [128, T], BF16, R2 + 33 * KB)
    At("accA", [128, 512], BF16, R2 + 41 * KB)
    At("accB", [128, 512], BF16, R2 + 42 * KB)
    At("ev1", [128, 512], F32, R2 + 43 * KB)
    At("ev2", [128, 512], F32, R2 + 45 * KB)
    At("S_f", [128, 2 * 1024], F32, R2)
    At("S_b", [128, 2 * 1024], BF16, R2 + 8 * KB)
    At("ktm", [128, 1024], BF16, R2 + 12 * KB)
    At("vtm", [128, 1024], BF16, R2 + 14 * KB)
    At("kbn", [128, 1024], BF16, R2 + 16 * KB)
    At("vblk", [128, 1024], BF16, R2 + 18 * KB)
    At("o_tm", [128, 1024], BF16, R2 + 20 * KB)
    At("z_sb", [128, 1024], BF16, R2 + 22 * KB)
    At("u_sb", [128, 1024], BF16, R2 + 24 * KB)
    At("ttf", [128, 128], BF16, R2 + 26 * KB)
    At("attf", [128, 128], BF16, R2 + 27 * KB)
    At("btm", [128, 8], F32, R2 + 28 * KB)
    At("dfm", [128, NDT * 512], BF16, R2 + 30 * KB)
    At("tmA", [128, 128], BF16, R2 + 44 * KB)
    At("tmB", [128, 128], BF16, R2 + 44 * KB + 256)
    At("ttA", [128, 128], BF16, R2 + 44 * KB + 512)
    At("ttB", [128, 128], BF16, R2 + 44 * KB + 768)
    At("vrow", [128, T], BF16, R2)
    At("srow", [128, T], BF16, R2 + 8 * KB)
    At("lrow", [128, T], BF16, R2 + 16 * KB)
    At("scr", [128, 128], F32, R2 + 24 * KB)
    At("sumP", [128, 512], F32, b.AQ)
    At("sqP", [128, 512], F32, b.AQ + 2 * KB)
    At("maxP", [128, 512], F32, b.AQ + 4 * KB)
    At("statm", [128, 1024], F32, b.AQ + 6 * KB)
    At("sqcb", [128, 1024], BF16, b.AQ + 54 * KB)
    At("gxs", [128, NDT * 512], BF16, R2)
    At("hmid", [128, 16 * 512], BF16, R2 + 8 * KB)
    At("w1b", [128, 9 * 2048], BF16, b.AK)
    At("p_fm", [16, T], BF16, R2 + 24 * KB)
    At("e16", [16, 512], F32, R2 + 32 * KB)
    At("zb16", [16, 512], F32, R2 + 34 * KB)
    At("vtile", [128, NDT * 512], BF16, R2 + 32 * KB)
    At("dtile", [128, NDT * 512], BF16, R2 + 40 * KB)
    At("stile", [128, NDT * 512], BF16, b.AQ)
    At("ltile", [128, NDT * 512], BF16, b.AQ + 8 * KB)
    At("onrm", [128, NDT * 512], BF16, b.AQ + 16 * KB)
    At("wobuf", [128, NDT * D], BF16, b.AQ + 24 * KB)
    At("ocomb", [128, 512], F32, b.AQ + 40 * KB)
    At("ocm2", [128, 512], F32, b.AQ + 42 * KB)
    At("ocm3", [128, 512], F32, b.AQ + 44 * KB)
    At("ev5", [128, 512], F32, b.AQ + 46 * KB)
    At("ev6", [128, 512], F32, b.AQ + 48 * KB)
    At("obuf", [128, 1024], F32, b.AQ + 50 * KB)
    At("ev7", [128, 512], F32, R2 + 8 * KB)
    At("ev8", [128, 512], F32, R2 + 10 * KB)
    At("wut", [128, NDT * 3072], BF16, b.AQ)
    At("wgt", [128, NDT * 3072], BF16, b.AK)
    At("wdt", [128, NI * 1024], BF16, b.AK)
    At("xq_t", [128, NDT * 512], BF16, b.AQ + 48 * KB)
    At("dqs", [128, 1024], F32, b.AQ + 56 * KB)
    At("dqm", [128, 64], F32, b.AQ + 60 * KB)
    At("dqb", [128, 1024], BF16, b.AQ + 62 * KB)
    At("ffa", [128, T], F32, R2)
    At("ffb", [128, T], F32, R2 + 16 * KB)
    At("ffc", [128, T], F32, R2 + 32 * KB)
    At("o2t", [128, 512], BF16, b.AQ)
    At("sgt", [128, 512], F32, b.AQ + 1 * KB)
    At("gt", [128, 512], F32, b.AQ + 3 * KB)
    At("xpt", [128, NDT * 512], F32, b.AQ + 8 * KB)
    At("outf", [128, NDT * 512], F32, b.AQ + 24 * KB)

    b.pp = [nc.alloc_psum_tensor(f"pp{i}", [128, 512], F32) for i in range(6)]
    b.pb = [nc.alloc_psum_tensor(f"pb{i}", [128, 1024], BF16) for i in range(2)]

    def w(rec, *deps):
        for d in deps:
            if d.n > 0:
                rec.wait_ge(d.sem, d.n)
    b.w = w

    def selfw(rec, cnt):
        rec.wait_ge(cnt.sem, cnt.n)
    b.selfw = selfw

    def barrier():
        for rec in (b.t, b.v, b.a, b.g, b.s):
            w(rec, b.pe, b.ve, b.ac, b.gp, b.qs, b.qa, b.qg, b.ccs)
    b.barrier = barrier

    def hview(hbm):
        return hbm.rearrange("(dt p) t -> p dt t", p=128)
    b.hview = hview
    return b


def p0(b):
    t, v, a, g, s = b.t, b.v, b.a, b.g, b.s
    sb, pp, W, w = b.sb, b.pp, b.W, b.w
    pe, ve, ac, gp, qs, qg = b.pe, b.ve, b.ac, b.gp, b.qs, b.qg
    g.memset(sb["ones"][:, :], 1.0)
    g.memset(sb["onesf"][:, :], 1.0)
    g.memset(sb["amax"][:, :], 0.0)
    g.memset(sb["pre"][:, :], 0.0)
    gp.inc(g.memset(sb["lns"][:, :], 0.0))
    w(g, gp)
    g.affine_select(sb["identf"][:, :], sb["onesf"][:, :], pattern=[[-1, 128]],
                    compare_op=ALU.is_equal, fill=0.0, base=0,
                    channel_multiplier=1)
    bd4 = sb["mask_bdsl"][:, :].rearrange("p (hf cf) -> p hf cf", cf=32)
    on4 = sb["ones"][:, :].rearrange("p (hf cf) -> p hf cf", cf=32)
    gp.inc(g.affine_select(bd4, on4, pattern=[[-32, 4], [-1, 32]],
                           compare_op=ALU.is_ge, fill=0.0, base=-1,
                           channel_multiplier=1))
    w(g, gp)
    gp.inc(g.affine_select(bd4, bd4, pattern=[[-32, 4], [0, 32]],
                           compare_op=ALU.is_ge, fill=0.0, base=0,
                           channel_multiplier=1))
    w(g, gp)
    gp.inc(g.affine_select(bd4, bd4, pattern=[[32, 4], [0, 32]],
                           compare_op=ALU.is_ge, fill=0.0, base=31,
                           channel_multiplier=-1))
    w(g, gp)
    for r0 in range(0, 128, 32):
        mt4 = sb["mask_triu"][r0:r0 + 32, :].rearrange("p (hf cf) -> p hf cf", cf=32)
        on32 = sb["ones"][0:32, :].rearrange("p (hf cf) -> p hf cf", cf=32)
        gp.inc(g.affine_select(mt4, on32, pattern=[[0, 4], [1, 32]],
                               compare_op=ALU.is_ge, fill=0.0, base=0,
                               channel_multiplier=-1))
        w(g, gp)
    gp.inc(g.memset(sb["sel4"][:, :], 0.0))
    gp.inc(g.memset(sb["selh"][:, :], 0.0))
    gp.inc(g.memset(sb["selh16"][:, :], 0.0))
    gp.inc(g.memset(sb["hsel"][:, :], 0.0))
    gp.inc(g.memset(sb["sumsel"][:, :], 0.0))
    w(v, gp)
    ve.inc(v.tensor_copy(sb["ident"][:, :], sb["identf"][:, :]))
    w(g, ve)
    for h in range(H):
        qg.inc(g.dma_start(out=sb["sel4"][32 * h:32 * h + 32, :],
                           in_=sb["ident"][0:32, 0:32]), 16)
        qg.inc(g.dma_start(out=sb["selh"][h:h + 1, 128 * h:128 * h + 128],
                           in_=sb["ones"][0:1, 0:128]), 16)
        qg.inc(g.dma_start(out=sb["hsel"][0:128, h * 4 + h: h * 4 + h + 1],
                           in_=sb["ones"][0:128, 0:1]), 16)
    for j in range(16):
        qg.inc(g.dma_start(out=sb["selh16"][j:j + 1, 128 * j:128 * j + 128],
                           in_=sb["ones"][0:1, 0:128]), 16)
        h = j // 4
        qg.inc(g.dma_start(out=sb["sumsel"][4 * h:4 * h + 4, j:j + 1],
                           in_=sb["ones"][0:4, 0:1]), 16)

    w(v, qg)
    b.selfw(v, ve)
    ve.inc(v.tensor_copy(sb["sumself"][:, :], sb["sumsel"][:, :]))
    b.selfw(v, ve)
    ve.inc(v.tensor_copy(sb["selhf"][:, :], sb["selh"][:, :]))
    b.selfw(v, ve)
    ve.inc(v.tensor_scalar(sb["negselh"][:, :], sb["selh"][:, :], -1.0, None, ALU.mult))
    for j, nm in enumerate(["conv_q", "conv_k", "conv_v"]):
        for dd in range(NDT):
            qs.inc(s.dma_start(
                out=sb["convw"][0:128, dd * 16 + j * 4: dd * 16 + j * 4 + 4],
                in_=W[nm][dd * 128:(dd + 1) * 128, :]), 16)
    for dd in range(NDT):
        qs.inc(s.dma_start(out=sb["firs"][0:128, dd * 4: dd * 4 + 3],
                           in_=W["fir_s"][dd * 128:(dd + 1) * 128, :]), 16)
        qs.inc(s.dma_start(out=sb["firl"][0:128, dd * 64: dd * 64 + 63],
                           in_=W["fir_l"][dd * 128:(dd + 1) * 128, :]), 16)
    qs.inc(s.dma_start(out=sb["normw"][0:128, 0:1],
                       in_=W["norm_w"][0:1, 0:128].rearrange("a b -> b a")), 16)
    qs.inc(s.dma_start(out=sb["normw"][0:128, 1:2],
                       in_=W["norm_w"][0:1, 128:256].rearrange("a b -> b a")), 16)
    for mt in range(16):
        qs.inc(s.dma_start(
            out=sb["b1t"][0:128, mt:mt + 1],
            in_=W["b1"][0:1, mt * 128:(mt + 1) * 128].rearrange("a b -> b a")), 16)
    qs.inc(s.dma_start(out=sb["misc"][0:1, 0:4], in_=W["forget"]), 16)
    qs.inc(s.dma_start(out=sb["misc"][0:1, 4:8], in_=W["log_temp"]), 16)
    qs.inc(s.dma_start(out=sb["misc"][0:4, 8:12], in_=W["floor"]), 16)
    qs.inc(s.dma_start(out=sb["misc"][0:1, 12:28], in_=W["b2"]), 16)
    qg.inc(g.dma_start(out=sb["wb_w"][:, :].rearrange("p (kt h) -> p kt h", h=H),
                       in_=W["wb"].rearrange("(kt p) h -> p kt h", p=128)), 16)
    qg.inc(g.dma_start(out=sb["w2w"][:, :].rearrange("p (kt h) -> p kt h", h=16),
                       in_=W["w2"].rearrange("(kt p) h -> p kt h", p=128)), 16)
    w(a, qs)
    ac.inc(a.activation(sb["misc"][0:1, 32:36], sb["misc"][0:1, 0:4], AF.Sigmoid))
    w(v, ac)
    b.selfw(v, ve)
    ve.inc(v.tensor_scalar(sb["misc"][0:1, 32:36], sb["misc"][0:1, 32:36],
                           1.0 - FORGET_MIN, FORGET_MIN, ALU.mult, ALU.add))
    w(a, ve)
    b.selfw(a, ac)
    ac.inc(a.activation(sb["misc"][0:1, 36:40], sb["misc"][0:1, 4:8], AF.Exp))
    w(v, ac)
    b.selfw(v, ve)
    ve.inc(v.reciprocal(sb["misc"][0:1, 40:44], sb["misc"][0:1, 36:40]))
    w(a, ve)
    b.selfw(a, ac)
    ac.inc(a.activation(sb["misc"][0:4, 44:48], sb["misc"][0:4, 8:12], AF.Sigmoid))
    w(v, ac)
    b.selfw(v, ve)
    ve.inc(v.tensor_scalar(sb["misc"][0:4, 44:48], sb["misc"][0:4, 44:48],
                           MAX_FLOOR, None, ALU.mult))
    w(t, ve, gp)
    pe.inc(t.matmul(pp[2][:, 0:4], sb["onesf"][0:1, 0:128],
                    sb["misc"][0:1, 32:36], start=True, stop=True))
    w(v, pe)
    b.selfw(v, ve)
    ve.inc(v.tensor_copy(sb["lam"][:, 0:4], pp[2][:, 0:4]))
    w(t, ve)
    pe.inc(t.transpose(pp[3][0:16, 0:1], sb["misc"][0:1, 12:28], sb["identf"][0:1, 0:1]))
    w(v, pe)
    b.selfw(v, ve)
    ve.inc(v.tensor_copy(sb["p16"][0:16, 0:1], pp[3][0:16, 0:1]))
    w(g, ve, ac)
    b.selfw(g, gp)
    for h in range(4):
        for j in range(4):
            qg.inc(g.dma_start(out=sb["p16"][4 * h + j:4 * h + j + 1, 1:2],
                               in_=sb["misc"][0:1, 40 + h:41 + h]), 16)
            qg.inc(g.dma_start(out=sb["p16"][4 * h + j:4 * h + j + 1, 2:3],
                               in_=sb["misc"][h:h + 1, 44 + j:45 + j]), 16)


def p1(b):
    t, v, a, g, s = b.t, b.v, b.a, b.g, b.s
    sb, pp, w = b.sb, b.pp, b.w
    pe, ve, ac, gp, qs, qg, ccs = b.pe, b.ve, b.ac, b.gp, b.qs, b.qg, b.ccs
    for blk in range(NBLK):
        w(s, ac, ve)
        qs.inc(s.dma_start(out=sb["x_t"][:, :], in_=b.dx[blk * 128:(blk + 1) * 128, :]), 16)
        w(a, qs)
        b.selfw(a, ac)
        w(a, ve)
        ac.inc(a.activation(sb["sq_t"][:, :], sb["x_t"][:, :], AF.Square,
                            accum_out=sb["lns"][:, 0:1]))
        w(v, qs)
        b.selfw(v, ve)
        ve.inc(v.reduce_sum(sb["lns"][:, 1:2], sb["x_t"][:, :], AX.X))
        w(v, ac)
        b.selfw(v, ve)
        ve.inc(v.tensor_scalar(sb["lns"][:, 2:3], sb["lns"][:, 1:2], 1.0 / D, None, ALU.mult))
        ve.inc(v.tensor_scalar(sb["lns"][:, 3:4], sb["lns"][:, 0:1], 1.0 / D, None, ALU.mult))
        b.selfw(v, ve)
        ve.inc(v.tensor_mul(sb["lns"][:, 4:5], sb["lns"][:, 2:3], sb["lns"][:, 2:3]))
        b.selfw(v, ve)
        ve.inc(v.tensor_sub(sb["lns"][:, 5:6], sb["lns"][:, 3:4], sb["lns"][:, 4:5]))
        b.selfw(v, ve)
        ve.inc(v.tensor_scalar(sb["lns"][:, 6:7], sb["lns"][:, 5:6], 1e-5, None, ALU.add))
        b.selfw(v, ve)
        ve.inc(v.reciprocal(sb["lns"][:, 7:8], sb["lns"][:, 6:7]))
        w(a, ve)
        b.selfw(a, ac)
        ac.inc(a.activation(sb["lns"][:, 8:9], sb["lns"][:, 7:8], AF.Sqrt))
        w(v, ve, pe, ac)
        b.selfw(v, ve)
        ve.inc(v.tensor_scalar(sb["lns"][:, 10:11], sb["lns"][:, 2:3], -1.0,
                               None, ALU.mult))
        b.selfw(v, ve)
        ve.inc(v.tensor_scalar(sb["xn_t"][:, :], sb["x_t"][:, :],
                               sb["lns"][:, 10:11], sb["lns"][:, 8:9],
                               ALU.add, ALU.mult))
        b.selfw(v, ve)
        ve.inc(v.tensor_reduce(sb["amax"][:, blk:blk + 1], sb["xn_t"][:, :],
                               AX.X, ALU.max, apply_absolute_value=True))
        xf = sb["xfm"] if (blk // 4) % 2 == 0 else sb["xfm2"]
        for dd in range(NDT):
            bank = b.pb[dd % 2]
            w(t, ac, ve)
            pe.inc(t.transpose(bank[:, 0:128], sb["xn_t"][:, dd * 128:(dd + 1) * 128],
                               sb["ident"][:, :]))
            if dd % 2 == 0:
                w(a, pe, qs)
                b.selfw(a, ac)
                ac.inc(a.copy(
                    xf[:, dd * 512 + (blk % 4) * 128: dd * 512 + (blk % 4) * 128 + 128],
                    bank[:, 0:128]))
            else:
                w(v, pe, qs)
                b.selfw(v, ve)
                ve.inc(v.tensor_copy(
                    xf[:, dd * 512 + (blk % 4) * 128: dd * 512 + (blk % 4) * 128 + 128],
                    bank[:, 0:128]))
        if blk % 4 == 3:
            tt = blk // 4
            for dd in range(NDT):
                w(s, ve, ac)
                qs.inc(s.dma_start(
                    out=b.h_xn[dd * 128:(dd + 1) * 128, tt * 512:(tt + 1) * 512],
                    in_=xf[:, dd * 512:(dd + 1) * 512]), 16)
    b.selfw(v, ve)
    ve.inc(v.tensor_reduce(sb["amax"][:, 32:33], sb["amax"][:, 0:32], AX.X, ALU.max))
    w(t, ve)
    pe.inc(t.transpose(pp[2][0:1, 0:128], sb["amax"][:, 32:33], sb["identf"][:, :]))
    w(v, pe)
    b.selfw(v, ve)
    ve.inc(v.tensor_reduce(sb["sc"][0:1, 1:2], pp[2][0:1, 0:128], AX.X, ALU.max))
    b.selfw(v, ve)
    ve.inc(v.tensor_copy(sb["sc"][0:1, 2:3], sb["sc"][0:1, 1:2]))
    w(g, ve)
    qg.inc(g.dma_start(out=b.ar_in[0].ap()[0:1, 0:2], in_=sb["sc"][0:1, 1:3]), 16)
    w(g, qg)
    ccs.inc(g.collective_compute("AllReduce", ALU.max,
                                 replica_groups=[list(range(8))],
                                 ins=[b.ar_in[0].ap()], outs=[b.ar_out[0].ap()]))
    w(g, ccs)
    qg.inc(g.dma_start(out=sb["sc"][0:1, 4:6], in_=b.ar_out[0].ap()[0:1, 0:2]), 16)
    w(v, qg)
    b.selfw(v, ve)
    ve.inc(v.tensor_scalar(sb["sc"][0:1, 6:7], sb["sc"][0:1, 4:5],
                           1.0 / 32767.0, 1e-12, ALU.mult, ALU.max))
    b.selfw(v, ve)
    ve.inc(v.reciprocal(sb["sc"][0:1, 7:8], sb["sc"][0:1, 6:7]))
    w(t, ve)
    pe.inc(t.matmul(pp[3][:, 0:2], sb["onesf"][0:1, 0:128], sb["sc"][0:1, 6:8],
                    start=True, stop=True))
    w(v, pe)
    b.selfw(v, ve)
    ve.inc(v.tensor_copy(sb["scb"][:, 0:2], pp[3][:, 0:2]))


def p3(b):
    t, v, a, g, s = b.t, b.v, b.a, b.g, b.s
    sb, pp, W, w = b.sb, b.pp, b.W, b.w
    pe, ve, ac, gp, qs, qa, qg = b.pe, b.ve, b.ac, b.gp, b.qs, b.qa, b.qg

    def sweep(wname, convj, dest):
        b.barrier()
        qg.inc(g.dma_start(out=sb["wbuf"][:, :].rearrange("p (kt c) -> p kt c", c=D),
                           in_=W[wname].rearrange("(kt p) c -> p kt c", p=128)), 16)
        w(g, ve, ac, pe, qs, qa)
        b.selfw(g, gp)
        gp.inc(g.memset(sb["pre"][:, :], 0.0))
        w(v, gp)
        w(a, gp)
        w(t, gp)
        bankseq = {}
        need = {}
        for tt in range(NTT):
            w(s, pe)
            qs.inc(s.dma_start(out=sb["xs"][:, :],
                               in_=b.hview(b.h_xn)[:, :, tt * 512:(tt + 1) * 512]), 16)
            for mt in range(NDT):
                bA = (mt % 2) * 2
                w(t, qs, qg)
                if bA in need:
                    t.wait_ge(ve.sem, need[bA][0])
                    t.wait_ge(ac.sem, need[bA][1])
                for kt in range(NDT):
                    bk = bA + kt % 2
                    if bk in bankseq:
                        t.wait_ge(pe.sem, bankseq[bk])
                    tok = t.matmul(
                        pp[bk][:, 0:512],
                        sb["wbuf"][:, kt * D + mt * 128: kt * D + mt * 128 + 128],
                        sb["xs"][:, :].rearrange("p (dt c) -> p dt c", dt=NDT)[:, kt, :],
                        start=kt < 2, stop=kt >= 6)
                    bankseq[bk] = b.pe.inc(tok)
                base = mt * 520
                bA, bB = (mt % 2) * 2, (mt % 2) * 2 + 1
                w(v, ac)
                b.selfw(v, ve)
                ve.inc(v.tensor_copy(sb["pre"][:, base + 0: base + 3],
                                     sb["pre"][:, base + 512: base + 515]))
                w(a, pe, ve)
                b.selfw(a, ac)
                ac.inc(a.copy(sb["ev1"][:, :], pp[bA][:, 0:512]))
                w(v, ac, pe)
                b.selfw(v, ve)
                ve.inc(v.tensor_add(sb["pre"][:, base + 3: base + 515],
                                    pp[bB][:, 0:512], sb["ev1"][:, :]))
                need[bA] = (ve.n, ac.n)
                need[bB] = (ve.n, ac.n)
                cw = sb["convw"]
                w(v, qa)
                b.selfw(v, ve)
                ve.inc(v.tensor_scalar(
                    sb["accA"][:, :], sb["pre"][:, base + 0: base + 512],
                    cw[:, mt * 16 + convj * 4 + 0: mt * 16 + convj * 4 + 1],
                    None, ALU.mult))
                for tap in range(1, 4):
                    src, dst = ("accA", "accB") if tap % 2 == 1 else ("accB", "accA")
                    b.selfw(v, ve)
                    ve.inc(v.scalar_tensor_tensor(
                        sb[dst][:, :], sb["pre"][:, base + tap: base + tap + 512],
                        cw[:, mt * 16 + convj * 4 + tap: mt * 16 + convj * 4 + tap + 1],
                        sb[src][:, :], ALU.mult, ALU.add))
                w(a, ve)
                b.selfw(a, ac)
                if dest == "v":
                    w(a, qa)
                    ac.inc(a.activation(sb["accA"][:, :], sb["accB"][:, :], AF.Silu))
                    w(a, ac)
                    qa.inc(a.dma_start(
                        out=b.h_v[mt * 128:(mt + 1) * 128, tt * 512:(tt + 1) * 512],
                        in_=sb["accA"][:, :]), 16)
                else:
                    ac.inc(a.activation(
                        sb[dest][:, mt * T + tt * 512: mt * T + (tt + 1) * 512],
                        sb["accB"][:, :], AF.Silu))

    sweep("wv", 2, "v")
    sweep("wk", 1, "k")
    sweep("wq", 0, "q")

    b.barrier()
    for tt in range(NTT):
        w(s, pe, ac)
        qs.inc(s.dma_start(out=sb["xs"][:, :],
                           in_=b.hview(b.h_xn)[:, :, tt * 512:(tt + 1) * 512]), 16)
        last = None
        for kt in range(NDT):
            w(t, qs, ve, ac)
            if last is not None:
                t.wait_ge(pe.sem, last)
            tok = t.matmul(pp[4][0:4, 0:512], sb["wb_w"][:, kt * 4:(kt + 1) * 4],
                           sb["xs"][:, :].rearrange("p (dt c) -> p dt c", dt=NDT)[:, kt, :],
                           start=kt == 0, stop=kt == 7)
            last = pe.inc(tok)
        w(a, pe)
        b.selfw(a, ac)
        ac.inc(a.activation(sb["rows1"][0:4, tt * 512:(tt + 1) * 512],
                            pp[4][0:4, 0:512], AF.Sigmoid))


    def l2pass(src, rowbase):
        b.barrier()
        for tt in range(NTT):
            last = None
            for dd in range(NDT):
                h = dd // 2
                w(a, pe, ve)
                b.selfw(a, ac)
                ac.inc(a.activation(sb["sqrow"][:, 0:512],
                                    sb[src][:, dd * T + tt * 512: dd * T + (tt + 1) * 512],
                                    AF.Square))
                w(t, ac)
                if last is not None:
                    t.wait_ge(pe.sem, last)
                last = pe.inc(t.matmul(pp[4][0:4, 0:512], sb["hsel"][:, h * 4: h * 4 + 4],
                                       sb["sqrow"][:, 0:512],
                                       start=dd == 0, stop=dd == 7))
            w(v, pe)
            b.selfw(v, ve)
            ve.inc(v.tensor_scalar(sb["ev1"][0:4, 0:512], pp[4][0:4, 0:512],
                                   1e-6, None, ALU.add))
            b.selfw(v, ve)
            ve.inc(v.reciprocal(sb["ev1"][0:4, 0:512], sb["ev1"][0:4, 0:512]))
            w(a, ve)
            b.selfw(a, ac)
            ac.inc(a.activation(sb["ev1"][0:4, 0:512], sb["ev1"][0:4, 0:512], AF.Sqrt))
            w(t, ac)
            tp = (0, rowbase) if rowbase == 96 else None
            pe.inc(t.matmul(pp[5][rowbase:rowbase + 4, 0:512],
                            sb["identf"][0:4, 0:4],
                            sb["ev1"][0:4, 0:512], start=True, stop=True,
                            tile_position=tp))
            w(v, pe)
            b.selfw(v, ve)
            ve.inc(v.tensor_copy(sb["rows1"][rowbase:rowbase + 4, tt * 512:(tt + 1) * 512],
                                 pp[5][rowbase:rowbase + 4, 0:512]))
            for h in range(4):
                bk = pp[2 + h % 2]
                w(t, ve)
                pe.inc(t.matmul(bk[:, 0:512], sb["selhf"][0:4, h * 128:(h + 1) * 128],
                                sb["ev1"][0:4, 0:512], start=True, stop=True))
                for dd in (2 * h, 2 * h + 1):
                    w(v, pe)
                    b.selfw(v, ve)
                    ve.inc(v.tensor_mul(
                        sb[src][:, dd * T + tt * 512: dd * T + (tt + 1) * 512],
                        sb[src][:, dd * T + tt * 512: dd * T + (tt + 1) * 512],
                        bk[:, 0:512]))

    l2pass("q", 64)
    l2pass("k", 96)


def p4(b, nblk=NBLK):
    t, v, a, g, s = b.t, b.v, b.a, b.g, b.s
    sb, pp, w = b.sb, b.pp, b.w
    pe, ve, ac, gp, qs, qg = b.pe, b.ve, b.ac, b.gp, b.qs, b.qg
    b.barrier()
    g.memset(sb["S_f"][:, :], 0.0)
    gp.inc(g.memset(sb["S_b"][:, :], 0.0))
    w(t, gp)

    for b4 in range(nblk):
        c0b = b4 * 128
        w(s, pe, ve, ac)
        qs.inc(s.dma_start(out=sb["vblk"][:, :],
                           in_=b.hview(b.h_v)[:, :, c0b:c0b + 128]), 16)
        w(t, qs, ve, ac)
        pe.inc(t.transpose(b.pb[0][0:128, 0:4], sb["rows1"][0:4, c0b:c0b + 128],
                           sb["ident"][0:4, 0:4]))
        w(v, pe)
        b.selfw(v, ve)
        ve.inc(v.tensor_copy(sb["btm"][:, 0:4], b.pb[0][0:128, 0:4]))
        w(t, ve, ac)
        lastk = None
        for dd in range(NDT):
            if lastk is not None:
                t.wait_ge(pe.sem, lastk)
            lastk = pe.inc(t.transpose(b.pb[0][:, dd * 128:(dd + 1) * 128],
                                       sb["k"][:, dd * T + c0b: dd * T + c0b + 128],
                                       sb["ident"][:, :]))
        w(a, pe)
        b.selfw(a, ac)
        ac.inc(a.copy(sb["ktm"][:, :], b.pb[0][:, :]))
        w(t, qs)
        lastv = None
        for dd in range(NDT):
            if lastv is not None:
                t.wait_ge(pe.sem, lastv)
            lastv = pe.inc(t.transpose(b.pb[1][:, dd * 128:(dd + 1) * 128],
                                       sb["vblk"][:, dd * 128:(dd + 1) * 128],
                                       sb["ident"][:, :]))
        w(v, pe)
        for dd in range(NDT):
            h = dd // 2
            b.selfw(v, ve)
            ve.inc(v.tensor_scalar(sb["vtm"][:, dd * 128:(dd + 1) * 128],
                                   b.pb[1][:, dd * 128:(dd + 1) * 128],
                                   sb["btm"][:, h:h + 1],
                                   None, ALU.mult))
        for h in range(4):
            bank = pp[4 + h % 2]
            w(t, ve)
            pe.inc(t.matmul(bank[:, 0:128], sb["negselh"][0:4, h * 128:(h + 1) * 128],
                            sb["rows1"][0:4, c0b:c0b + 128], start=True, stop=True))
            for dd in (2 * h, 2 * h + 1):
                w(v, pe)
                b.selfw(v, ve)
                ve.inc(v.tensor_mul(sb["kbn"][:, dd * 128:(dd + 1) * 128],
                                    sb["k"][:, dd * T + c0b: dd * T + c0b + 128],
                                    bank[:, 0:128]))
        for ci in range(4):
            c0 = ci * 32
            tpA = (0, c0) if c0 == 96 else None
            tpB = (96, 96) if c0 == 96 else None
            tpC = (96, 0) if c0 == 96 else None
            tok0 = c0b + ci * 32
            lastg = None
            for h in range(4):
                for dkt in range(2):
                    dd = 2 * h + dkt
                    w(t, ve, ac)
                    if lastg is not None and dkt == 1:
                        t.wait_ge(pe.sem, lastg)
                    tok = t.matmul(
                        pp[2][32 * h:32 * h + 32, 32 * h:32 * h + 32],
                        sb["kbn"][:, dd * 128 + ci * 32: dd * 128 + ci * 32 + 32],
                        sb["k"][:, dd * T + tok0: dd * T + tok0 + 32],
                        start=dkt == 0, stop=dkt == 1,
                        tile_position=(0, 32 * h) if h == 3 else None)
                    lastg = pe.inc(tok)
            w(v, pe)
            b.selfw(v, ve)
            ve.inc(v.scalar_tensor_tensor(sb["tmA"][:, :], pp[2][:, 0:128], 1.0,
                                          sb["mask_bdsl"][:, :], ALU.mult, ALU.mult))
            b.selfw(v, ve)
            ve.inc(v.transpose(sb["tmB"][:, :], sb["tmA"][:, :]))
            b.selfw(v, ve)
            ve.inc(v.tensor_add(sb["ttA"][:, :], sb["tmB"][:, :], sb["ident"][:, :]))
            ttc, ttn = "ttA", "ttB"
            for j in range(4):
                w(t, ve, ac)
                pe.inc(t.matmul(pp[3][:, 0:128], sb["tmB"][:, :], sb["tmA"][:, :],
                                start=True, stop=True))
                w(a, pe)
                b.selfw(a, ac)
                ac.inc(a.copy(sb["tmA"][:, :], pp[3][:, 0:128]))
                w(v, ac)
                b.selfw(v, ve)
                ve.inc(v.transpose(sb["tmB"][:, :], sb["tmA"][:, :]))
                w(t, ac, ve)
                t.wait_ge(pe.sem, pe.n)
                pe.inc(t.matmul(pp[3][:, 128:256], sb["tmA"][:, :], sb[ttc][:, :],
                                start=True, stop=True))
                w(v, pe)
                b.selfw(v, ve)
                ve.inc(v.scalar_tensor_tensor(sb[ttn][:, :], pp[3][:, 128:256], 1.0,
                                              sb[ttc][:, :], ALU.mult, ALU.add))
                ttc, ttn = ttn, ttc
            w(t, ve)
            pe.inc(t.matmul(pp[5][c0:c0 + 32, 0:128], sb["sel4"][:, :], sb[ttc][:, :],
                            start=True, stop=True, tile_position=tpA))
            w(a, pe)
            b.selfw(a, ac)
            ac.inc(a.copy(sb["ttf"][c0:c0 + 32, :], pp[5][c0:c0 + 32, 0:128]))
            lasta = None
            for h in range(4):
                for dkt in range(2):
                    dd = 2 * h + dkt
                    w(t, ac, ve)
                    if lasta is not None:
                        t.wait_ge(pe.sem, lasta)
                    tok = t.matmul(
                        pp[4][c0:c0 + 32, 32 * h:32 * h + 32],
                        sb["k"][:, dd * T + tok0: dd * T + tok0 + 32],
                        sb["q"][:, dd * T + tok0: dd * T + tok0 + 32],
                        start=dkt == 0, stop=dkt == 1, tile_position=tpA)
                    lasta = pe.inc(tok)
            w(v, pe)
            b.selfw(v, ve)
            ve.inc(v.scalar_tensor_tensor(sb["attf"][c0:c0 + 32, :],
                                          pp[4][c0:c0 + 32, 0:128], 1.0,
                                          sb["mask_triu"][c0:c0 + 32, :],
                                          ALU.mult, ALU.mult))
            zl = {}
            for h in range(4):
                for dkt in range(2):
                    dd = 2 * h + dkt
                    bank = pp[0] if h < 2 else pp[1]
                    fo = (h % 2) * 256
                    w(t, ve, ac)
                    key = (h < 2)
                    if key in zl:
                        t.wait_ge(pe.sem, zl[key])
                    tok = t.matmul(bank[c0:c0 + 32, fo:fo + 256],
                                   sb["kbn"][:, dd * 128 + ci * 32: dd * 128 + ci * 32 + 32],
                                   sb["S_b"][:, dkt * 1024 + h * 256: dkt * 1024 + (h + 1) * 256],
                                   start=dkt == 0, stop=dkt == 1, tile_position=tpA)
                    zl[key] = pe.inc(tok)
            w(v, pe)
            b.selfw(v, ve)
            ve.inc(v.scalar_tensor_tensor(sb["z_sb"][c0:c0 + 32, 0:512],
                                          pp[0][c0:c0 + 32, 0:512], 1.0,
                                          sb["vtm"][c0:c0 + 32, 0:512],
                                          ALU.mult, ALU.add))
            b.selfw(v, ve)
            ve.inc(v.scalar_tensor_tensor(sb["z_sb"][c0:c0 + 32, 512:1024],
                                          pp[1][c0:c0 + 32, 0:512], 1.0,
                                          sb["vtm"][c0:c0 + 32, 512:1024],
                                          ALU.mult, ALU.add))
            for h in range(4):
                bank = pp[4] if h < 2 else pp[5]
                fo = (h % 2) * 256
                w(t, ve)
                t.wait_ge(pe.sem, pe.n)
                pe.inc(t.matmul(bank[c0:c0 + 32, fo:fo + 256],
                                sb["ttf"][c0:c0 + 32, 32 * h:32 * h + 32],
                                sb["z_sb"][c0:c0 + 32, h * 256:(h + 1) * 256],
                                start=True, stop=True, tile_position=tpB))
            w(v, pe)
            b.selfw(v, ve)
            ve.inc(v.tensor_copy(sb["u_sb"][c0:c0 + 32, 0:512], pp[4][c0:c0 + 32, 0:512]))
            b.selfw(v, ve)
            ve.inc(v.tensor_copy(sb["u_sb"][c0:c0 + 32, 512:1024], pp[5][c0:c0 + 32, 0:512]))
            ol = {}
            for h in range(4):
                bank = pp[0] if h < 2 else pp[1]
                fo = (h % 2) * 256
                for dkt in range(2):
                    dd = 2 * h + dkt
                    w(t, ve)
                    key = (h < 2)
                    if key in ol:
                        t.wait_ge(pe.sem, ol[key])
                    ol[key] = pe.inc(t.matmul(
                        bank[c0:c0 + 32, fo:fo + 256],
                        sb["q"][:, dd * T + tok0: dd * T + tok0 + 32],
                        sb["S_b"][:, dkt * 1024 + h * 256: dkt * 1024 + (h + 1) * 256],
                        start=dkt == 0, stop=dkt == 1, tile_position=tpA))
                bank2 = pp[4] if h < 2 else pp[5]
                w(t, ve)
                t.wait_ge(pe.sem, pe.n)
                ol[("a", h < 2)] = pe.inc(t.matmul(
                    bank2[c0:c0 + 32, fo:fo + 256],
                    sb["attf"][c0:c0 + 32, 32 * h:32 * h + 32],
                    sb["u_sb"][c0:c0 + 32, h * 256:(h + 1) * 256],
                    start=True, stop=True, tile_position=tpB))
            w(v, pe)
            b.selfw(v, ve)
            ve.inc(v.tensor_copy(sb["o_tm"][c0:c0 + 32, 0:512], pp[4][c0:c0 + 32, 0:512]))
            b.selfw(v, ve)
            ve.inc(v.tensor_copy(sb["o_tm"][c0:c0 + 32, 512:1024], pp[5][c0:c0 + 32, 0:512]))
            b.selfw(v, ve)
            ve.inc(v.scalar_tensor_tensor(sb["o_tm"][c0:c0 + 32, 0:512],
                                          pp[0][c0:c0 + 32, 0:512], 1.0,
                                          sb["o_tm"][c0:c0 + 32, 0:512],
                                          ALU.mult, ALU.add))
            b.selfw(v, ve)
            ve.inc(v.scalar_tensor_tensor(sb["o_tm"][c0:c0 + 32, 512:1024],
                                          pp[1][c0:c0 + 32, 0:512], 1.0,
                                          sb["o_tm"][c0:c0 + 32, 512:1024],
                                          ALU.mult, ALU.add))
            for wave in range(2):
                for i2 in range(4):
                    h = wave * 2 + i2 // 2
                    dkt = i2 % 2
                    bank = pp[2 + i2]
                    w(t, ve, ac)
                    t.wait_ge(pe.sem, pe.n)
                    pe.inc(t.matmul(
                        bank[0:128, 0:256],
                        sb["ktm"][c0:c0 + 32, (2 * h + dkt) * 128:(2 * h + dkt + 1) * 128],
                        sb["u_sb"][c0:c0 + 32, h * 256:(h + 1) * 256],
                        start=True, stop=True, tile_position=tpC))
                for i2 in range(4):
                    h = wave * 2 + i2 // 2
                    dkt = i2 % 2
                    bank = pp[2 + i2]
                    sl = slice(dkt * 1024 + h * 256, dkt * 1024 + (h + 1) * 256)
                    w(v, pe)
                    b.selfw(v, ve)
                    ve.inc(v.scalar_tensor_tensor(sb["S_f"][:, sl], sb["S_f"][:, sl],
                                                  sb["lam"][:, h:h + 1],
                                                  bank[0:128, 0:256],
                                                  ALU.mult, ALU.add))
                    w(a, ve)
                    b.selfw(a, ac)
                    ac.inc(a.copy(sb["S_b"][:, sl], sb["S_f"][:, sl]))
        w(t, ve, ac)
        lasto = None
        for dd in range(NDT):
            if lasto is not None:
                t.wait_ge(pe.sem, lasto)
            lasto = pe.inc(t.transpose(b.pb[0][:, dd * 128:(dd + 1) * 128],
                                       sb["o_tm"][:, dd * 128:(dd + 1) * 128],
                                       sb["ident"][:, :]))
        w(v, pe, qs)
        b.selfw(v, ve)
        ve.inc(v.tensor_copy(
            sb["dfm"][:, :].rearrange("p (dt blk c) -> p dt blk c",
                                      blk=4, c=128)[:, :, b4 % 4, :],
            b.pb[0][:, :].rearrange("p (dt c) -> p dt c", c=128)))
        if b4 % 4 == 3:
            tt = b4 // 4
            for dd in range(NDT):
                w(s, ve, ac)
                qs.inc(s.dma_start(
                    out=b.h_df[dd * 128:(dd + 1) * 128, tt * 512:(tt + 1) * 512],
                    in_=sb["dfm"][:, dd * 512:(dd + 1) * 512]), 16)


def p5(b, upto="all"):
    t, v, a, g, s = b.t, b.v, b.a, b.g, b.s
    sb, pp, W, w = b.sb, b.pp, b.W, b.w
    pe, ve, ac, gp, qs, qa, qg = b.pe, b.ve, b.ac, b.gp, b.qs, b.qa, b.qg

    b.barrier()
    for dd in range(NDT):
        w(s, ve, ac, qa)
        qs.inc(s.dma_start(out=sb["vrow"][:, :], in_=b.h_v[dd * 128:(dd + 1) * 128, :]), 16)
        w(v, qs, qa)
        b.selfw(v, ve)
        ve.inc(v.tensor_scalar(sb["srow"][:, 0:T], sb["vrow"][:, 0:T],
                               sb["firs"][:, dd * 4 + 2: dd * 4 + 3], None, ALU.mult))
        for tap in range(2):
            sh = 2 - tap
            b.selfw(v, ve)
            ve.inc(v.scalar_tensor_tensor(sb["srow"][:, sh:T], sb["vrow"][:, 0:T - sh],
                                          sb["firs"][:, dd * 4 + tap: dd * 4 + tap + 1],
                                          sb["srow"][:, sh:T], ALU.mult, ALU.add))
        w(a, ve)
        qa.inc(a.dma_start(out=b.h_sh[dd * 128:(dd + 1) * 128, :], in_=sb["srow"][:, :]), 16)
        w(v, qa)
        b.selfw(v, ve)
        ve.inc(v.tensor_scalar(sb["lrow"][:, 0:T], sb["vrow"][:, 0:T],
                               sb["firl"][:, dd * 64 + 62: dd * 64 + 63], None, ALU.mult))
        for tap in range(62):
            sh = 62 - tap
            b.selfw(v, ve)
            ve.inc(v.scalar_tensor_tensor(sb["lrow"][:, sh:T], sb["vrow"][:, 0:T - sh],
                                          sb["firl"][:, dd * 64 + tap: dd * 64 + tap + 1],
                                          sb["lrow"][:, sh:T], ALU.mult, ALU.add))
        w(a, ve)
        b.selfw(a, ac)
        qa.inc(a.dma_start(out=b.h_lo[dd * 128:(dd + 1) * 128, :], in_=sb["lrow"][:, :]), 16)

    if upto == "fir":
        return

    def do_stats(pair, srcs, phase="all"):
        b.barrier()
        for sub, src_h in enumerate(srcs):
            for dd in range(NDT):
                slot = "vrow" if dd % 2 == 0 else "srow"
                w(s, ve, ac, pe)
                qs.inc(s.dma_start(out=sb[slot][:, :],
                                   in_=src_h[dd * 128:(dd + 1) * 128, :]), 16)
                for ti2 in range(32):
                    bank = b.pb[ti2 % 2]
                    c = sub * 256 + ti2 * 8 + dd
                    w(t, qs, ve, ac)
                    pe.inc(t.transpose(bank[:, 0:128],
                                       sb[slot][:, ti2 * 128:(ti2 + 1) * 128],
                                       sb["ident"][:, :]))
                    if phase != "accV":
                        w(a, pe)
                        b.selfw(a, ac)
                        ac.inc(a.activation(sb["scr"][:, :], bank[:, 0:128], AF.Copy,
                                            accum_out=sb["sumP"][:, c:c + 1]))
                        b.selfw(a, ac)
                        ac.inc(a.activation(sb["scr"][:, :], bank[:, 0:128], AF.Square,
                                            accum_out=sb["sqP"][:, c:c + 1]))
                    if phase != "accA":
                        w(v, ac, pe)
                        b.selfw(v, ve)
                        ve.inc(v.tensor_reduce(sb["maxP"][:, c:c + 1], bank[:, 0:128],
                                               AX.X, ALU.max))
        if phase == "acc":
            return
        sP = lambda nm, sub, t2: sb[nm][:, sub * 256 + t2 * 8:
                                        sub * 256 + (t2 + 1) * 8].rearrange(
            "p (h two) -> p h two", two=2)
        pbase = pair * 32
        for ti2 in range(32):
            for sub in range(2):
                c0 = ti2 * 32 + sub * 16
                stm = sb["statm"][:, c0:c0 + 16].rearrange(
                    "p (h st) -> p h st", st=4)
                se, so = sP("sumP", sub, ti2)[:, :, 0], sP("sumP", sub, ti2)[:, :, 1]
                qe, qo = sP("sqP", sub, ti2)[:, :, 0], sP("sqP", sub, ti2)[:, :, 1]
                me, mo = sP("maxP", sub, ti2)[:, :, 0], sP("maxP", sub, ti2)[:, :, 1]
                w(v, ac, pe)
                b.selfw(v, ve)
                ve.inc(v.tensor_add(stm[:, :, 0], se, so))
                b.selfw(v, ve)
                ve.inc(v.tensor_scalar(stm[:, :, 0], stm[:, :, 0], 1.0 / 256.0,
                                       None, ALU.mult))
                b.selfw(v, ve)
                ve.inc(v.tensor_add(stm[:, :, 3], qe, qo))
                b.selfw(v, ve)
                ve.inc(v.tensor_mul(stm[:, :, 1], stm[:, :, 0], stm[:, :, 0]))
                b.selfw(v, ve)
                ve.inc(v.scalar_tensor_tensor(stm[:, :, 1], stm[:, :, 3], 1.0 / 256.0,
                                              stm[:, :, 1], ALU.mult, ALU.subtract))
                b.selfw(v, ve)
                ve.inc(v.tensor_max(stm[:, :, 2], me, mo))
                w(a, ve)
                b.selfw(a, ac)
                ac.inc(a.activation(stm[:, :, 3], stm[:, :, 3], AF.Sqrt))
            bank = pp[ti2 % 2]
            w(t, ve, ac)
            t.wait_ge(pe.sem, pe.n)
            pe.inc(t.matmul(bank[pbase:pbase + 32, 0:128],
                            sb["statm"][:, ti2 * 32:(ti2 + 1) * 32],
                            sb["identf"][:, :], start=True, stop=True))
            w(v, pe)
            b.selfw(v, ve)
            ve.inc(v.tensor_copy(sb["gstats"][pbase:pbase + 32,
                                              ti2 * 128:(ti2 + 1) * 128],
                                 bank[pbase:pbase + 32, 0:128]))

    if upto in ("acc0", "accA", "accV"):
        do_stats(0, (b.h_sh, b.h_lo),
                 phase="acc" if upto == "acc0" else upto)
        return
    do_stats(0, (b.h_sh, b.h_lo))
    if upto == "stats0":
        return
    do_stats(1, (b.h_df, b.h_v))
    if upto == "stats":
        return

    b.barrier()
    qg.inc(g.dma_start(out=sb["w1b"][:, 0:8 * 2048].rearrange("p (kt c) -> p kt c", c=2048),
                       in_=W["w1"][0:1024, :].rearrange("(kt p) c -> p kt c", p=128)), 16)
    qg.inc(g.dma_start(out=sb["w1b"][0:64, 8 * 2048:9 * 2048],
                       in_=W["w1"][1024:1088, :]), 16)
    for tt in range(NTT):
        w(s, pe, ve, ac)
        qs.inc(s.dma_start(out=sb["gxs"][:, :],
                           in_=b.hview(b.h_xn)[:, :, tt * 512:(tt + 1) * 512]), 16)
        gneed = {}
        lastb = {}
        for mt in range(16):
            bA = (mt % 2) * 2
            w(t, qs, qg)
            if bA in gneed:
                t.wait_ge(ve.sem, gneed[bA][0])
                t.wait_ge(ac.sem, gneed[bA][1])
            for kt in range(9):
                bk = bA + kt % 2
                if bk in lastb:
                    t.wait_ge(pe.sem, lastb[bk])
                if kt < 8:
                    rhs = sb["gxs"][:, :].rearrange("p (dt c) -> p dt c", dt=NDT)[:, kt, :]
                    lhs = sb["w1b"][:, kt * 2048 + mt * 128: kt * 2048 + mt * 128 + 128]
                else:
                    rhs = sb["gstats"][0:64, tt * 512:(tt + 1) * 512]
                    lhs = sb["w1b"][0:64, 8 * 2048 + mt * 128: 8 * 2048 + mt * 128 + 128]
                lastb[bk] = pe.inc(t.matmul(pp[bk][:, 0:512], lhs, rhs,
                                            start=kt < 2, stop=kt >= 7))
            w(a, pe, ve)
            b.selfw(a, ac)
            ac.inc(a.copy(sb["ev5"][:, :], pp[bA][:, 0:512]))
            w(v, ac, pe)
            b.selfw(v, ve)
            ve.inc(v.tensor_add(sb["ev6"][:, :], pp[bA + 1][:, 0:512], sb["ev5"][:, :]))
            w(a, ve)
            b.selfw(a, ac)
            ac.inc(a.activation(sb["hmid"][:, mt * 512:(mt + 1) * 512], sb["ev6"][:, :],
                                AF.Gelu, bias=sb["b1t"][:, mt:mt + 1]))
            gneed[bA] = (ve.n, ac.n)
            gneed[bA + 1] = gneed[bA]
        lastw = None
        w(t, ac, ve)
        for kt in range(16):
            if lastw is not None:
                t.wait_ge(pe.sem, lastw)
            lastw = pe.inc(t.matmul(pp[2][0:16, 0:512], sb["w2w"][:, kt * 16:(kt + 1) * 16],
                                    sb["hmid"][:, kt * 512:(kt + 1) * 512],
                                    start=kt == 0, stop=kt == 15))
        w(v, pe)
        b.selfw(v, ve)
        ve.inc(v.tensor_scalar(sb["zb16"][0:16, 0:512], pp[2][0:16, 0:512],
                               sb["p16"][0:16, 0:1], sb["p16"][0:16, 1:2],
                               ALU.add, ALU.mult))
        w(a, ve)
        b.selfw(a, ac)
        ac.inc(a.activation(sb["e16"][0:16, 0:512], sb["zb16"][0:16, 0:512], AF.Exp))
        w(t, ac)
        t.wait_ge(pe.sem, pe.n)
        pe.inc(t.matmul(pp[3][0:16, 0:512], sb["sumself"][0:16, 0:16],
                        sb["e16"][0:16, 0:512], start=True, stop=True))
        w(v, pe)
        b.selfw(v, ve)
        ve.inc(v.reciprocal(sb["zb16"][0:16, 0:512], pp[3][0:16, 0:512]))
        b.selfw(v, ve)
        ve.inc(v.tensor_mul(sb["zb16"][0:16, 0:512], sb["zb16"][0:16, 0:512],
                            sb["e16"][0:16, 0:512]))
        b.selfw(v, ve)
        ve.inc(v.tensor_scalar(sb["e16"][0:16, 0:512], sb["zb16"][0:16, 0:512],
                               sb["p16"][0:16, 2:3], None, ALU.max))
        w(t, ve)
        t.wait_ge(pe.sem, pe.n)
        pe.inc(t.matmul(pp[3][0:16, 0:512], sb["sumself"][0:16, 0:16],
                        sb["e16"][0:16, 0:512], start=True, stop=True))
        w(v, pe)
        b.selfw(v, ve)
        ve.inc(v.reciprocal(sb["zb16"][0:16, 0:512], pp[3][0:16, 0:512]))
        b.selfw(v, ve)
        ve.inc(v.tensor_mul(sb["zb16"][0:16, 0:512], sb["zb16"][0:16, 0:512],
                            sb["e16"][0:16, 0:512]))
        b.selfw(v, ve)
        ve.inc(v.tensor_copy(sb["p_fm"][0:16, tt * 512:(tt + 1) * 512],
                             sb["zb16"][0:16, 0:512]))
    if upto == "gate":
        return

    b.barrier()
    qg.inc(g.dma_start(out=sb["wobuf"][:, :].rearrange("p (kt c) -> p kt c", c=D),
                       in_=W["wo"].rearrange("(kt p) c -> p kt c", p=128)), 16)
    for tt in range(NTT):
        sl = slice(tt * 512, (tt + 1) * 512)
        w(s, pe, ve, ac)
        qs.inc(s.dma_start(out=sb["vtile"][:, :], in_=b.hview(b.h_v)[:, :, sl]), 16)
        qs.inc(s.dma_start(out=sb["dtile"][:, :], in_=b.hview(b.h_df)[:, :, sl]), 16)
        qs.inc(s.dma_start(out=sb["stile"][:, :], in_=b.hview(b.h_sh)[:, :, sl]), 16)
        qs.inc(s.dma_start(out=sb["ltile"][:, :], in_=b.hview(b.h_lo)[:, :, sl]), 16)
        lastpb = {}
        for dd in range(NDT):
            h = dd // 2
            dsl = slice(dd * 512, (dd + 1) * 512)
            w(t, qs, ve)
            for j in range(4):
                if j in lastpb:
                    t.wait_ge(pe.sem, lastpb[j])
                lastpb[j] = pe.inc(t.matmul(
                    pp[j][:, 0:512],
                    sb["selh16"][0:16, (4 * h + j) * 128:(4 * h + j + 1) * 128],
                    sb["p_fm"][0:16, sl], start=True, stop=True))
            w(v, pe)
            b.selfw(v, ve)
            ve.inc(v.tensor_mul(sb["ocomb"][:, 0:512], sb["stile"][:, dsl], pp[0][:, 0:512]))
            b.selfw(v, ve)
            ve.inc(v.tensor_mul(sb["ocm2"][:, 0:512], sb["ltile"][:, dsl], pp[1][:, 0:512]))
            b.selfw(v, ve)
            ve.inc(v.tensor_add(sb["ocomb"][:, 0:512], sb["ocomb"][:, 0:512], sb["ocm2"][:, 0:512]))
            b.selfw(v, ve)
            ve.inc(v.tensor_mul(sb["ocm2"][:, 0:512], sb["dtile"][:, dsl], pp[2][:, 0:512]))
            b.selfw(v, ve)
            ve.inc(v.tensor_add(sb["ocomb"][:, 0:512], sb["ocomb"][:, 0:512], sb["ocm2"][:, 0:512]))
            b.selfw(v, ve)
            ve.inc(v.tensor_mul(sb["ocm2"][:, 0:512], sb["vtile"][:, dsl], pp[3][:, 0:512]))
            b.selfw(v, ve)
            ve.inc(v.tensor_add(sb["ocomb"][:, 0:512], sb["ocomb"][:, 0:512], sb["ocm2"][:, 0:512]))
            if dd % 2 == 1:
                w(a, ve)
                b.selfw(a, ac)
                ac.inc(a.activation(sb["sqcb"][:, 0:512], sb["ocm3"][:, 0:512],
                                    AF.Square))
                b.selfw(a, ac)
                ac.inc(a.activation(sb["sqcb"][:, 512:1024], sb["ocomb"][:, 0:512],
                                    AF.Square))
                w(t, ac)
                t.wait_ge(pe.sem, pe.n)
                rms1 = pe.inc(t.matmul(pp[5][0:1, 0:512], sb["ones"][:, 0:1],
                                       sb["sqcb"][:, 0:512], start=True, stop=False))
                t.wait_ge(pe.sem, rms1)
                pe.inc(t.matmul(pp[5][0:1, 0:512], sb["ones"][:, 0:1],
                                sb["sqcb"][:, 512:1024], start=False, stop=True))
                w(v, pe)
                b.selfw(v, ve)
                ve.inc(v.tensor_copy(sb["ev6"][0:1, 0:512], pp[5][0:1, 0:512]))
                b.selfw(v, ve)
                ve.inc(v.tensor_scalar(sb["ev6"][0:1, 0:512], sb["ev6"][0:1, 0:512],
                                       1.0 / 256.0, 1e-5, ALU.mult, ALU.add))
                b.selfw(v, ve)
                ve.inc(v.reciprocal(sb["ev6"][0:1, 0:512], sb["ev6"][0:1, 0:512]))
                w(a, ve)
                b.selfw(a, ac)
                ac.inc(a.activation(sb["ev6"][0:1, 0:512], sb["ev6"][0:1, 0:512], AF.Sqrt))
                w(t, ac, ve)
                t.wait_ge(pe.sem, pe.n)
                pe.inc(t.matmul(pp[4][:, 0:512], sb["onesf"][0:1, 0:128],
                                sb["ev6"][0:1, 0:512], start=True, stop=True))
                for d2 in (dd - 1, dd):
                    nwcol = 0 if d2 % 2 == 0 else 1
                    w(v, pe)
                    b.selfw(v, ve)
                    ve.inc(v.scalar_tensor_tensor(
                        sb["onrm"][:, d2 * 512:(d2 + 1) * 512],
                        sb["ocomb"][:, 0:512] if d2 == dd else sb["ocm3"][:, 0:512],
                        sb["normw"][:, nwcol:nwcol + 1], pp[4][:, 0:512],
                        ALU.mult, ALU.mult))
            if dd % 2 == 0:
                b.selfw(v, ve)
                ve.inc(v.tensor_copy(sb["ocm3"][:, 0:512], sb["ocomb"][:, 0:512]))
        for mt in range(NDT):
            lastb = {}
            w(t, ve, ac)
            for kt in range(NDT):
                bk = kt % 2
                if bk in lastb:
                    t.wait_ge(pe.sem, lastb[bk])
                lastb[bk] = pe.inc(t.matmul(
                    pp[bk][:, 0:512],
                    sb["wobuf"][:, kt * D + mt * 128: kt * D + mt * 128 + 128],
                    sb["onrm"][:, kt * 512:(kt + 1) * 512],
                    start=kt < 2, stop=kt >= 6))
            w(a, pe, ve)
            b.selfw(a, ac)
            ac.inc(a.copy(sb["ev5"][:, :], pp[0][:, 0:512]))
            w(v, ac, pe, qs)
            b.selfw(v, ve)
            ve.inc(v.tensor_add(sb["ev6"][:, :], pp[1][:, 0:512], sb["ev5"][:, :]))
            w(s, ve)
            qs.inc(s.dma_start(out=b.h_s[mt * 128:(mt + 1) * 128, sl],
                               in_=sb["ev6"][:, :]), 16)


def p6(b):
    t, v, a, g, s = b.t, b.v, b.a, b.g, b.s
    sb, pp, W, w = b.sb, b.pp, b.W, b.w
    pe, ve, ac, gp, qs, qa, qg, ccs = (b.pe, b.ve, b.ac, b.gp, b.qs, b.qa,
                                       b.qg, b.ccs)

    def dequant_t(wsrc, dstname):
        b.barrier()
        for it in range(NI):
            w(s, ve, ac, pe)
            qs.inc(s.dma_start(out=sb["dqs"][:, :], in_=wsrc[it * 128:(it + 1) * 128, :]), 16)
            w(v, qs)
            b.selfw(v, ve)
            ve.inc(v.tensor_reduce(sb["dqm"][:, 0:32],
                                   sb["dqs"][:, :].rearrange("p (bk e) -> p bk e", e=32),
                                   AX.X, ALU.max, apply_absolute_value=True))
            b.selfw(v, ve)
            ve.inc(v.tensor_scalar(sb["dqm"][:, 0:32], sb["dqm"][:, 0:32],
                                   1.0 / 7.0, 1e-12, ALU.mult, ALU.max))
            b.selfw(v, ve)
            ve.inc(v.reciprocal(sb["dqm"][:, 32:64], sb["dqm"][:, 0:32]))
            for bk in range(32):
                b.selfw(v, ve)
                ve.inc(v.tensor_scalar(sb["dqs"][:, bk * 32:(bk + 1) * 32],
                                       sb["dqs"][:, bk * 32:(bk + 1) * 32],
                                       sb["dqm"][:, 32 + bk:33 + bk], MAGIC,
                                       ALU.mult, ALU.add))
                b.selfw(v, ve)
                ve.inc(v.tensor_scalar(sb["dqs"][:, bk * 32:(bk + 1) * 32],
                                       sb["dqs"][:, bk * 32:(bk + 1) * 32],
                                       MAGIC, sb["dqm"][:, bk:bk + 1],
                                       ALU.subtract, ALU.mult))
            w(a, ve)
            b.selfw(a, ac)
            ac.inc(a.copy(sb["dqb"][:, :], sb["dqs"][:, :]))
            for kt in range(NDT):
                bank = b.pb[kt % 2]
                w(t, ac, ve)
                t.wait_ge(pe.sem, pe.n)
                pe.inc(t.transpose(bank[:, 0:128], sb["dqb"][:, kt * 128:(kt + 1) * 128],
                                   sb["ident"][:, :]))
                if kt % 2 == 0:
                    w(a, pe)
                    b.selfw(a, ac)
                    ac.inc(a.copy(
                        sb[dstname][:, kt * 3072 + it * 128: kt * 3072 + it * 128 + 128],
                        bank[:, 0:128]))
                else:
                    w(v, pe)
                    b.selfw(v, ve)
                    ve.inc(v.tensor_copy(
                        sb[dstname][:, kt * 3072 + it * 128: kt * 3072 + it * 128 + 128],
                        bank[:, 0:128]))

    dequant_t(W["wu_f"], "wut")
    dequant_t(W["wg_f"], "wgt")

    b.barrier()
    amax_col = [33, 34]
    w(v, qs)
    b.selfw(v, ve)
    ve.inc(v.memset(sb["amax"][:, 33:35], 0.0))
    for tt in range(NTT):
        w(s, pe, ve, ac)
        qs.inc(s.dma_start(out=sb["gxs"][:, :],
                           in_=b.hview(b.h_xn)[:, :, tt * 512:(tt + 1) * 512]), 16)
        for dd in range(NDT):
            w(v, qs)
            b.selfw(v, ve)
            ve.inc(v.tensor_scalar(sb["ev7"][:, :],
                                   sb["gxs"][:, :].rearrange("p (dt c) -> p dt c", dt=NDT)[:, dd, :],
                                   sb["scb"][:, 1:2], MAGIC, ALU.mult, ALU.add))
            b.selfw(v, ve)
            ve.inc(v.tensor_scalar(sb["xq_t"][:, dd * 512:(dd + 1) * 512],
                                   sb["ev7"][:, :], MAGIC, sb["scb"][:, 0:1],
                                   ALU.subtract, ALU.mult))
        w(t, ve)
        fneed = {}
        lastb = {}
        for which, wname, hdst, acol in (("up", "wut", b.h_up, 33), ("ga", "wgt", b.h_ga, 34)):
            for it in range(NI):
                bA = (it % 2) * 2
                w(t, qs)
                if bA in fneed:
                    t.wait_ge(ve.sem, fneed[bA][0])
                    t.wait_ge(ac.sem, fneed[bA][1])
                for kt in range(NDT):
                    bk = bA + kt % 2
                    if bk in lastb:
                        t.wait_ge(pe.sem, lastb[bk])
                    lastb[bk] = pe.inc(t.matmul(
                        pp[bk][:, 0:512],
                        sb[wname][:, kt * 3072 + it * 128: kt * 3072 + it * 128 + 128],
                        sb["xq_t"][:, kt * 512:(kt + 1) * 512],
                        start=kt < 2, stop=kt >= 6))
                w(a, pe, ve)
                b.selfw(a, ac)
                ac.inc(a.copy(sb["ev7"][:, :], pp[bA][:, 0:512]))
                w(v, ac, pe, qs, qa)
                b.selfw(v, ve)
                ve.inc(v.tensor_add(sb["ev8"][:, :], pp[bA + 1][:, 0:512], sb["ev7"][:, :]))
                w(a, ve)
                b.selfw(a, ac)
                qa.inc(a.dma_start(out=hdst[it * 128:(it + 1) * 128, tt * 512:(tt + 1) * 512],
                                   in_=sb["ev8"][:, :]), 16)
                b.selfw(v, ve)
                ve.inc(v.tensor_reduce(sb["amax"][:, 40:41], sb["ev8"][:, :], AX.X,
                                       ALU.max, apply_absolute_value=True))
                b.selfw(v, ve)
                ve.inc(v.tensor_max(sb["amax"][:, acol:acol + 1],
                                    sb["amax"][:, acol:acol + 1], sb["amax"][:, 40:41]))
                fneed[bA] = (ve.n, ac.n)
                fneed[bA + 1] = fneed[bA]

    def fold_and_ar(cols, aridx, scslots):
        for i, cn in enumerate(cols):
            w(t, ve, ac)
            t.wait_ge(pe.sem, pe.n)
            pe.inc(t.transpose(pp[2][0:1, 0:128], sb["amax"][:, cn:cn + 1],
                               sb["identf"][:, :]))
            w(v, pe)
            b.selfw(v, ve)
            ve.inc(v.tensor_reduce(sb["sc"][0:1, 10 + i:11 + i], pp[2][0:1, 0:128],
                                   AX.X, ALU.max))
        w(g, ve)
        b.selfw(g, gp)
        qg.inc(g.dma_start(out=b.ar_in[aridx].ap()[0:1, 0:2],
                           in_=sb["sc"][0:1, 10:12]), 16)
        w(g, qg)
        ccs.inc(g.collective_compute("AllReduce", ALU.max,
                                     replica_groups=[list(range(8))],
                                     ins=[b.ar_in[aridx].ap()],
                                     outs=[b.ar_out[aridx].ap()]))
        w(g, ccs)
        qg.inc(g.dma_start(out=sb["sc"][0:1, 12:14], in_=b.ar_out[aridx].ap()[0:1, 0:2]), 16)
        w(v, qg)
        for i, (scol, icol) in enumerate(scslots):
            b.selfw(v, ve)
            ve.inc(v.tensor_scalar(sb["sc"][0:1, 14 + i:15 + i], sb["sc"][0:1, 12 + i:13 + i],
                                   1.0 / 32767.0, 1e-12, ALU.mult, ALU.max))
            b.selfw(v, ve)
            ve.inc(v.reciprocal(sb["sc"][0:1, 16 + i:17 + i], sb["sc"][0:1, 14 + i:15 + i]))
        w(t, ve)
        t.wait_ge(pe.sem, pe.n)
        pe.inc(t.matmul(pp[3][:, 0:4], sb["onesf"][0:1, 0:128], sb["sc"][0:1, 14:18],
                        start=True, stop=True))
        w(v, pe)
        for i, (scol, icol) in enumerate(scslots):
            b.selfw(v, ve)
            ve.inc(v.tensor_copy(sb["scb"][:, scol:scol + 1], pp[3][:, i:i + 1]))
            b.selfw(v, ve)
            ve.inc(v.tensor_copy(sb["scb"][:, icol:icol + 1], pp[3][:, 2 + i:3 + i]))

    fold_and_ar([33, 34], 1, [(2, 3), (4, 5)])
    b.selfw(v, ve)
    ve.inc(v.tensor_scalar(sb["scb"][:, 10:11], sb["scb"][:, 4:5], -MAGIC, None, ALU.mult))
    b.selfw(v, ve)
    ve.inc(v.tensor_scalar(sb["scb"][:, 12:13], sb["scb"][:, 2:3], -MAGIC, None, ALU.mult))

    b.barrier()
    b.selfw(v, ve)
    ve.inc(v.memset(sb["amax"][:, 35:36], 0.0))
    for it in range(NI):
        w(s, ve, ac, qa)
        qs.inc(s.dma_start(out=sb["ffa"][:, :], in_=b.h_ga[it * 128:(it + 1) * 128, :]), 16)
        w(a, qs)
        b.selfw(a, ac)
        ac.inc(a.activation(sb["ffb"][:, :], sb["ffa"][:, :], AF.Copy,
                            scale=sb["scb"][:, 5:6], bias=float(MAGIC)))
        w(v, ac)
        b.selfw(v, ve)
        ve.inc(v.tensor_scalar(sb["ffb"][:, :], sb["ffb"][:, :], MAGIC,
                               sb["scb"][:, 4:5], ALU.subtract, ALU.mult))
        w(a, ve)
        b.selfw(a, ac)
        ac.inc(a.activation(sb["ffc"][:, :], sb["ffb"][:, :], AF.Sigmoid))
        w(v, ac)
        b.selfw(v, ve)
        ve.inc(v.tensor_scalar(sb["ffc"][:, :], sb["ffc"][:, :], 65536.0, MAGIC,
                               ALU.mult, ALU.add))
        b.selfw(v, ve)
        ve.inc(v.tensor_scalar(sb["ffc"][:, :], sb["ffc"][:, :], MAGIC, 65535.0,
                               ALU.subtract, ALU.min))
        b.selfw(v, ve)
        ve.inc(v.scalar_tensor_tensor(sb["ffa"][:, :], sb["ffc"][:, :], float(2.0 ** -16),
                                      sb["ffb"][:, :], ALU.mult, ALU.mult))
        w(a, ve)
        b.selfw(a, ac)
        qa.inc(a.dma_start(out=b.h_t[it * 128:(it + 1) * 128, :], in_=sb["ffa"][:, :]), 16)
        b.selfw(v, ve)
        ve.inc(v.tensor_reduce(sb["amax"][:, 40:41], sb["ffa"][:, :], AX.X,
                               ALU.max, apply_absolute_value=True))
        b.selfw(v, ve)
        ve.inc(v.tensor_max(sb["amax"][:, 35:36], sb["amax"][:, 35:36],
                            sb["amax"][:, 40:41]))
    fold_and_ar([35], 2, [(6, 7)])
    b.selfw(v, ve)
    ve.inc(v.tensor_scalar(sb["scb"][:, 11:12], sb["scb"][:, 6:7], -MAGIC, None, ALU.mult))

    b.barrier()
    b.selfw(v, ve)
    ve.inc(v.memset(sb["amax"][:, 36:37], 0.0))
    for it in range(NI):
        w(s, ve, ac)
        qs.inc(s.dma_start(out=sb["ffa"][:, :], in_=b.h_t[it * 128:(it + 1) * 128, :]), 16)
        qs.inc(s.dma_start(out=sb["ffb"][:, :], in_=b.h_up[it * 128:(it + 1) * 128, :]), 16)
        w(a, qs, ve)
        b.selfw(a, ac)
        ac.inc(a.activation(sb["ffa"][:, :], sb["ffa"][:, :], AF.Copy,
                            scale=sb["scb"][:, 7:8], bias=float(MAGIC)))
        b.selfw(a, ac)
        ac.inc(a.activation(sb["ffb"][:, :], sb["ffb"][:, :], AF.Copy,
                            scale=sb["scb"][:, 3:4], bias=float(MAGIC)))
        w(v, qs, ac, qa)
        b.selfw(v, ve)
        ve.inc(v.tensor_scalar(sb["ffa"][:, :], sb["ffa"][:, :], MAGIC, sb["scb"][:, 6:7],
                               ALU.subtract, ALU.mult))
        b.selfw(v, ve)
        ve.inc(v.tensor_scalar(sb["ffb"][:, :], sb["ffb"][:, :], MAGIC, sb["scb"][:, 2:3],
                               ALU.subtract, ALU.mult))
        b.selfw(v, ve)
        ve.inc(v.tensor_mul(sb["ffc"][:, :], sb["ffa"][:, :], sb["ffb"][:, :]))
        w(a, ve)
        b.selfw(a, ac)
        qa.inc(a.dma_start(out=b.h_ot[it * 128:(it + 1) * 128, :], in_=sb["ffc"][:, :]), 16)
        b.selfw(v, ve)
        ve.inc(v.tensor_reduce(sb["amax"][:, 40:41], sb["ffc"][:, :], AX.X,
                               ALU.max, apply_absolute_value=True))
        b.selfw(v, ve)
        ve.inc(v.tensor_max(sb["amax"][:, 36:37], sb["amax"][:, 36:37],
                            sb["amax"][:, 40:41]))
    fold_and_ar([36], 3, [(8, 9)])
    b.selfw(v, ve)
    ve.inc(v.tensor_scalar(sb["scb"][:, 13:14], sb["scb"][:, 8:9], -MAGIC, None, ALU.mult))

    b.barrier()
    for pt in range(NDT):
        w(s, ve, ac, pe)
        qs.inc(s.dma_start(out=sb["ffa"][:, 0:3072], in_=W["wd_f"][pt * 128:(pt + 1) * 128, :]), 16)
        w(v, qs)
        b.selfw(v, ve)
        ve.inc(v.tensor_reduce(sb["dqm"][:, 0:96] if False else sb["ffb"][:, 0:96],
                               sb["ffa"][:, 0:3072].rearrange("p (bk e) -> p bk e", e=32),
                               AX.X, ALU.max, apply_absolute_value=True))
        b.selfw(v, ve)
        ve.inc(v.tensor_scalar(sb["ffb"][:, 0:96], sb["ffb"][:, 0:96],
                               1.0 / 7.0, 1e-12, ALU.mult, ALU.max))
        b.selfw(v, ve)
        ve.inc(v.reciprocal(sb["ffb"][:, 96:192], sb["ffb"][:, 0:96]))
        for bk in range(96):
            b.selfw(v, ve)
            ve.inc(v.tensor_scalar(sb["ffa"][:, bk * 32:(bk + 1) * 32],
                                   sb["ffa"][:, bk * 32:(bk + 1) * 32],
                                   sb["ffb"][:, 96 + bk:97 + bk], MAGIC,
                                   ALU.mult, ALU.add))
            b.selfw(v, ve)
            ve.inc(v.tensor_scalar(sb["ffa"][:, bk * 32:(bk + 1) * 32],
                                   sb["ffa"][:, bk * 32:(bk + 1) * 32],
                                   MAGIC, sb["ffb"][:, bk:bk + 1],
                                   ALU.subtract, ALU.mult))
        w(a, ve)
        b.selfw(a, ac)
        ac.inc(a.copy(sb["ffc"][:, 0:3072], sb["ffa"][:, 0:3072]))
        for c3 in range(3):
            w(v, ac)
            b.selfw(v, ve)
            ve.inc(v.tensor_copy(sb["dqb"][:, :], sb["ffc"][:, c3 * 1024:(c3 + 1) * 1024]))
            for kt in range(8):
                it = c3 * 8 + kt
                bank = b.pb[kt % 2]
                w(t, ve, ac)
                t.wait_ge(pe.sem, pe.n)
                pe.inc(t.transpose(bank[:, 0:128], sb["dqb"][:, kt * 128:(kt + 1) * 128],
                                   sb["ident"][:, :]))
                w(v, pe)
                b.selfw(v, ve)
                ve.inc(v.tensor_copy(
                    sb["wdt"][:, it * 1024 + pt * 128: it * 1024 + pt * 128 + 128],
                    bank[:, 0:128]))
    for tt in range(NTT):
        sl = slice(tt * 512, (tt + 1) * 512)
        for blk in range(4):
            w(s, ve, ac)
            qs.inc(s.dma_start(out=sb["x_t"][:, :],
                               in_=b.dx[tt * 512 + blk * 128: tt * 512 + (blk + 1) * 128, :]), 16)
            for dd in range(NDT):
                bank = pp[2 + dd % 2]
                w(t, qs, ve, ac)
                t.wait_ge(pe.sem, pe.n)
                pe.inc(t.transpose(bank[:, 0:128], sb["x_t"][:, dd * 128:(dd + 1) * 128],
                                   sb["identf"][:, :]))
                w(v, pe)
                b.selfw(v, ve)
                ve.inc(v.tensor_copy(
                    sb["xpt"][:, dd * 512 + blk * 128: dd * 512 + blk * 128 + 128],
                    bank[:, 0:128]))
        lastd = {}
        for half in range(2):
            for it in range(NI):
                w(s, ve, ac)
                qs.inc(s.dma_start(out=sb["sgt"][:, :], in_=b.h_ot[it * 128:(it + 1) * 128, sl]), 16)
                w(a, qs, pe)
                b.selfw(a, ac)
                ac.inc(a.activation(sb["sgt"][:, :], sb["sgt"][:, :], AF.Copy,
                                    scale=sb["scb"][:, 9:10], bias=float(MAGIC)))
                w(v, ac, pe)
                b.selfw(v, ve)
                ve.inc(v.tensor_scalar(sb["o2t"][:, :], sb["sgt"][:, :], MAGIC,
                                       sb["scb"][:, 8:9], ALU.subtract, ALU.mult))
                w(t, ve)
                if it == 0:
                    w(t, ac)
                for mi in range(4):
                    mt = half * 4 + mi
                    bank = pp[mi]
                    if mi in lastd:
                        t.wait_ge(pe.sem, lastd[mi])
                    lastd[mi] = pe.inc(t.matmul(
                        bank[:, 0:512],
                        sb["wdt"][:, it * 1024 + mt * 128: it * 1024 + mt * 128 + 128],
                        sb["o2t"][:, :],
                        start=it == 0, stop=it == NI - 1))
            for mi in range(4):
                mt = half * 4 + mi
                w(s, ve, ac)
                qs.inc(s.dma_start(out=sb["sgt"][:, :], in_=b.h_s[mt * 128:(mt + 1) * 128, sl]), 16)
                w(a, pe, qs)
                b.selfw(a, ac)
                ac.inc(a.activation(sb["gt"][:, :], pp[mi][:, 0:512], AF.Silu))
                w(v, ac, qs)
                b.selfw(v, ve)
                ve.inc(v.tensor_mul(sb["gt"][:, :], sb["gt"][:, :], sb["sgt"][:, :]))
                b.selfw(v, ve)
                ve.inc(v.tensor_add(sb["outf"][:, mt * 512:(mt + 1) * 512],
                                    sb["gt"][:, :], sb["xpt"][:, mt * 512:(mt + 1) * 512]))
        for blk in range(4):
            for dd in range(NDT):
                bank = pp[2 + dd % 2]
                w(t, ve, ac)
                t.wait_ge(pe.sem, pe.n)
                pe.inc(t.transpose(bank[:, 0:128],
                                   sb["outf"][:, dd * 512 + blk * 128: dd * 512 + blk * 128 + 128],
                                   sb["identf"][:, :]))
                if dd % 2 == 0:
                    w(a, pe, qs)
                    b.selfw(a, ac)
                    ac.inc(a.copy(sb["obuf"][:, dd * 128:(dd + 1) * 128], bank[:, 0:128]))
                else:
                    w(v, pe, qs)
                    b.selfw(v, ve)
                    ve.inc(v.tensor_copy(sb["obuf"][:, dd * 128:(dd + 1) * 128], bank[:, 0:128]))
            w(s, ve, ac)
            qs.inc(s.dma_start(
                out=b.dy[tt * 512 + blk * 128: tt * 512 + (blk + 1) * 128, :],
                in_=sb["obuf"][:, 0:1024]), 16)



def fixup_bias_note():
    pass


def finish(b):
    w = b.w
    b.barrier()
    b.g.wait_ge(b.qs.sem, b.qs.n)
    b.g.wait_ge(b.qa.sem, b.qa.n)
    b.g.wait_ge(b.qg.sem, b.qg.n)


def build(dump=None):
    b = mkbuilder(dump)
    p0(b)
    p1(b)
    p3(b)
    if dump == "xn":
        dbg_dump(b, b.h_xn, 0)
    elif dump == "q":
        dbg_dump_sb(b, "q", 0)
    elif dump == "v":
        dbg_dump(b, b.h_v, 0)
    elif dump == "S1":
        p4(b, nblk=1)
        sb = b.sb
        b.barrier()
        b.selfw(b.v, b.ve)
        b.ve.inc(b.v.memset(sb["ffb"][:, :], 0.0))
        b.selfw(b.v, b.ve)
        b.ve.inc(b.v.tensor_copy(sb["ffb"][:, 0:2048], sb["S_f"][:, :]))
        b.w(b.s, b.ve)
        b.qs.inc(b.s.dma_start(out=b.d_dbg, in_=sb["ffb"][:, :]), 16)
    else:
        p4(b)
        if dump == "delta":
            dbg_dump(b, b.h_df, 0)
        elif dump in ("fir", "acc0", "accA", "accV", "stats0", "stats", "gate"):
            p5(b, upto=dump)
            sb = b.sb
            if dump == "fir":
                dbg_dump(b, b.h_sh, 0)
            else:
                buf = "ffa" if dump == "gate" else "ffb"
                b.barrier()
                b.selfw(b.v, b.ve)
                b.ve.inc(b.v.memset(sb[buf][:, :], 0.0))
                b.selfw(b.v, b.ve)
                if dump in ("acc0", "accA", "accV"):
                    src = "maxP" if dump == "accV" else "sumP"
                    b.ve.inc(b.v.tensor_copy(sb[buf][0:64, 0:512], sb[src][0:64, :]))
                elif dump in ("stats", "stats0"):
                    b.ve.inc(b.v.tensor_copy(sb[buf][0:64, :], sb["gstats"][:, :]))
                else:
                    b.ve.inc(b.v.tensor_copy(sb[buf][0:16, :], sb["p_fm"][0:16, :]))
                b.w(b.s, b.ve)
                b.qs.inc(b.s.dma_start(out=b.d_dbg, in_=sb[buf][:, :]), 16)
        else:
            p5(b)
            if dump == "s":
                dbg_dump_f32(b, b.h_s, 0)
            else:
                p6(b)
    finish(b)
    nc, t, v, a, g, s = b.nc, b.t, b.v, b.a, b.g, b.s
    with nc.Block() as block:
        @block.sync
        def _(eng):
            s.replay(eng)

        @block.tensor
        def _(eng):
            t.replay(eng)

        @block.vector
        def _(eng):
            v.replay(eng)

        @block.scalar
        def _(eng):
            a.replay(eng)

        @block.gpsimd
        def _(eng):
            g.replay(eng)
    return b


def dbg_dump(b, hsrc, row):
    sb, w = b.sb, b.w
    b.barrier()
    b.qs.inc(b.s.dma_start(out=sb["vrow"][:, :], in_=hsrc[row * 128:(row + 1) * 128, :]), 16)
    w(b.v, b.qs)
    b.selfw(b.v, b.ve)
    b.ve.inc(b.v.tensor_copy(sb["ffb"][:, :], sb["vrow"][:, :]))
    w(b.s, b.ve)
    b.qs.inc(b.s.dma_start(out=b.d_dbg, in_=sb["ffb"][:, :]), 16)


def dbg_dump_sb(b, name, dd):
    sb, w = b.sb, b.w
    b.barrier()
    b.selfw(b.v, b.ve)
    b.ve.inc(b.v.tensor_copy(sb["ffb"][:, :], sb[name][:, dd * T:(dd + 1) * T]))
    w(b.s, b.ve)
    b.qs.inc(b.s.dma_start(out=b.d_dbg, in_=sb["ffb"][:, :]), 16)


def dbg_dump_f32(b, hsrc, row):
    sb, w = b.sb, b.w
    b.barrier()
    b.qs.inc(b.s.dma_start(out=sb["ffa"][:, :], in_=hsrc[row * 128:(row + 1) * 128, :]), 16)
    w(b.s, b.qs)
    b.qs.inc(b.s.dma_start(out=b.d_dbg, in_=sb["ffa"][:, :]), 16)


def inputs_to_maps(inputs):
    ins = {k: np.ascontiguousarray(np.asarray(vv, np.float32))
           for k, vv in inputs.items()}
    nmap = dict(
        wq="L0_attn_Wq", wk="L0_attn_Wk", wv="L0_attn_Wv", wb="L0_attn_Wb",
        conv_q="L0_attn_conv_q", conv_k="L0_attn_conv_k", conv_v="L0_attn_conv_v",
        w1="L0_attn_W1", w2="L0_attn_W2", wo="L0_attn_Wo",
        wg_f="L0_ffn_w_gate", wu_f="L0_ffn_w_up", wd_f="L0_ffn_w_down")
    base = {k: ins[v2] for k, v2 in nmap.items()}
    base["forget"] = ins["L0_attn_forget_param"].reshape(1, H)
    base["log_temp"] = ins["L0_attn_log_temp"].reshape(1, H)
    base["floor"] = np.ascontiguousarray(ins["L0_attn_floor_param"].reshape(H, 4))
    base["b1"] = ins["L0_attn_b1"].reshape(1, 2048)
    base["b2"] = ins["L0_attn_b2"].reshape(1, 16)
    base["norm_w"] = ins["L0_attn_norm_w"].reshape(1, DV)
    base["fir_s"] = np.ascontiguousarray(ins["L0_attn_fir_short_w"].reshape(D, 3))
    base["fir_l"] = np.ascontiguousarray(ins["L0_attn_fir_long_w"].reshape(D, 63))
    x = ins["L0_attn_hidden_states"]
    return [dict(base, x=np.ascontiguousarray(x[i])) for i in range(8)]


def kernel(**inputs):
    b = build()
    maps = inputs_to_maps(inputs)
    res = run_bass_kernel_spmd(b.nc, maps, core_ids=list(range(8)))
    return np.stack([r["y"] for r in res.results], axis=0)
